# version 01
# baseline (speedup 1.0000x reference)
COMPUTE_ENGINES = ("tensor", "vector", "scalar", "gpsimd")


class Op:
    __slots__ = ("engine", "axis", "fn", "reads", "writes", "inc", "tag",
                 "dep_ops", "idx", "waits", "signaled", "cum", "G", "C")

    def __init__(self, engine, axis, fn, reads, writes, inc, tag):
        self.engine = engine
        self.axis = axis
        self.fn = fn
        self.reads = tuple(reads)
        self.writes = tuple(writes)
        self.inc = inc
        self.tag = tag
        self.waits = []
        self.signaled = False


class Prog:
    def __init__(self):
        self.ops = []
        self.last_writer = {}
        self.readers = {}
        self.sinks = set()

    def op(self, engine, fn, reads=(), writes=(), accum=None, tag="", axis=None, inc=1):
        o = Op(engine, axis or engine, fn, reads, writes, inc, tag)
        o.idx = len(self.ops)
        self.ops.append(o)
        deps = []
        for k in o.reads:
            w = self.last_writer.get(k)
            if w is not None:
                deps.append(w)
        for k in o.writes:
            w = self.last_writer.get(k)
            if w is not None:
                deps.append(w)
            for r in self.readers.get(k, ()):
                deps.append(r)
        if accum is not None:
            key, is_start, is_stop = accum
            if is_start:
                w = self.last_writer.get(key)
                if w is not None:
                    deps.append(w)
                for r in self.readers.get(key, ()):
                    deps.append(r)
            if is_stop:
                self.last_writer[key] = o
                self.readers[key] = []
        seen = set()
        o.dep_ops = []
        for d in deps:
            if d is o or id(d) in seen:
                continue
            seen.add(id(d))
            o.dep_ops.append(d)
        for k in o.reads:
            self.readers.setdefault(k, []).append(o)
        for k in o.writes:
            self.last_writer[k] = o
            self.readers[k] = []
        return o

    def dma(self, family, out, in_, reads=(), writes=(), queue="sync", tag="", slow_ok=False):
        return self.op(queue, lambda eng: eng.dma_start(out=out, in_=in_,
                                                        allow_slow_non_contiguous=slow_ok),
                       reads=reads, writes=writes, tag=tag, axis=family, inc=16)

    def barrier(self, engine, keys, tag="barrier"):
        return self.op(engine, None, reads=keys, tag=tag)

    def fence(self, tag="fence"):
        keys = list(self.last_writer.keys())
        return self.op("vector", lambda v: v.engine_nop(),
                       reads=keys, writes=keys, tag=tag)

    def finalize_and_emit(self, nc):
        for o in self.ops:
            if o.inc == 16 or o.axis == "coll":
                o.signaled = True
            for d in o.dep_ops:
                d.signaled = True
        axis_cum = {}
        for o in self.ops:
            if o.signaled:
                axis_cum[o.axis] = axis_cum.get(o.axis, 0) + o.inc
                o.cum = axis_cum[o.axis]
        def meet(a, b):
            for k, v in b.items():
                if a.get(k, 0) < v:
                    a[k] = v
        eng_G = {}
        for o in self.ops:
            need = {}
            for d in o.dep_ops:
                assert d.idx < o.idx
                meet(need, d.C)
            G_prev = eng_G.get(o.engine, {})
            waits = {}
            for ax, v in need.items():
                if ax == o.engine and o.engine != "sync" and o.axis == o.engine:
                    pass
                if G_prev.get(ax, 0) >= v:
                    continue
                waits[ax] = v
            o.waits = sorted(waits.items())
            G = dict(G_prev)
            meet(G, waits)
            eng_G[o.engine] = G
            C = dict(G)
            if o.signaled:
                v = axis_cum[o.axis] if o.axis in self.sinks else o.cum
                C[o.axis] = max(C.get(o.axis, 0), v)
            o.C = C
        engines = {}
        for o in self.ops:
            engines.setdefault(o.engine, []).append(o)
        self.stats = {e: dict(n_ops=len(v), n_sig=sum(1 for x in v if x.signaled),
                              n_waits=sum(len(x.waits) for x in v))
                      for e, v in engines.items()}
        self.stats["n_axes"] = len(axis_cum)

        import contextlib
        stack = contextlib.ExitStack()
        axes = sorted(axis_cum)
        sems = {a: stack.enter_context(nc.semaphore(f"sem_{a}")) for a in axes}
        blk = stack.enter_context(nc.Block())

        def make_body(eops):
            def body(eng):
                for o in eops:
                    for (ax, v) in o.waits:
                        eng.wait_ge(sems[ax], v)
                    if o.fn is not None:
                        ins = o.fn(eng)
                        if o.signaled:
                            ins.then_inc(sems[o.axis], o.inc)
                    else:
                        assert not o.signaled
            return body

        for ename, eops in engines.items():
            getattr(blk, ename)(make_body(eops))
        return stack


"""Megakernel builder (shared by dev runs and kernel.py).

One uniform program across 8 cores; per-core differences live in input data.
Feature-major streams: [d_model 128-chunks on partitions x tokens on free].

Phases:
  A: LN stats + K/V (full batch elem) -> HBM scratch; Q (own tokens); stats rows -> dram.
  B: causal flash attention (shifted-exp softmax, mask-mult) -> O_norm;
     stream+fake-quant W_fc/W_proj with LoRA/scale folded -> Wtot bf16 (resident).
  C: L0 FFN (gelu+proj) + Wo(O_norm) + x residual -> x1 -> HBM scratch.
  D: L1 LN + QKV (+phi) + KV/Z partials -> pairwise AllReduce; L1 FFN (e4m3)
     folded into x1 -> x1' scratch.
  E: (Qp@KV_aug / norm) @ Wo + bo + x1' -> y.
"""
import numpy as np
import ml_dtypes
import concourse.bass as bass
import concourse.mybir as mybir

F32 = mybir.dt.float32
FP16 = mybir.dt.float16
BF16 = mybir.dt.bfloat16
I32 = mybir.dt.int32
AF = mybir.ActivationFunctionType
ALU = mybir.AluOpType
AX = mybir.AxisListType.X
BF = ml_dtypes.bfloat16

D = 1024
T = 8192
TOWN = 4096
NT_G = 16
NT_O = 8
TT = 512
TTC = 256
NT_C = TOWN // TTC
DK = 128
LN2 = float(np.log(2.0))
SM_SCALE = float(128 ** -0.5)
KB = 1024

CONST_COLS = {
    "neg16": (0, -16.0),
    "eps_ln": (1, 1e-5),
    "qexp": (2, -9.0 * LN2),
    "rexp": (3, 9.0 * LN2),
    "six": (4, 6.0),
    "zero": (5, 0.0),
    "eps2": (6, 2.0 ** -24),
}
NCONST = 8

STAGES = ["A", "B", "C", "D", "E"]


def build(stage="E", dbg_stage=None):
    st = STAGES.index(stage)
    nc = bass.Bass("TRN2", target_bir_lowering=False)
    P = Prog()
    P.sinks = {"init", "dbg", "x1pst", "kvcin", "kvcout", "cinit", "dinit", "dinitg", "einit", "einitg",
               "winit", "einit", "mxstfc", "mxldfc", "mxstpj", "mxldpj"}

    dram = {}

    def din(name, shape, dt=F32):
        dram[name] = nc.dram_tensor(name, shape, dt, kind="ExternalInput").ap()
        return dram[name]

    def dint(name, shape, dt=F32):
        dram[name] = nc.dram_tensor(name, shape, dt).ap()
        return dram[name]

    def dout(name, shape, dt=F32):
        dram[name] = nc.dram_tensor(name, shape, dt, kind="ExternalOutput").ap()
        return dram[name]

    xfull = din("xfull", [D, T])
    xown = din("xown", [D, TOWN])
    l0qkv_d = din("l0qkv", [D, 384])
    l0wo_d = din("l0wo", [DK, D])
    consts_d = din("consts", [128, NCONST])
    onesbf_d = din("onesbf", [128, 576], BF16)
    onesf_d = din("onesf", [1, 128])
    masks_d = din("masks", [128, 8, TT], BF16)
    wfcT_d = din("wfcT", [D, 4096])
    wpjT_d = din("wpjT", [4096, D])
    afc_d = din("afc", [16, D])
    bfcT_d = din("bfcT", [16, 4096])
    apj_d = din("apj", [16, 4096])
    bpjT_d = din("bpjT", [16, D])
    bfc_t_d = din("bfc_t", [128, 32])
    bpj_t_d = din("bpj_t", [128, 8])
    wqT_d = din("wqT", [D, D])
    wkT_d = din("wkT", [D, D])
    wvT_d = din("wvT", [D, D])
    woT_d = din("woT", [D, D])
    bq_t_d = din("bq_t", [128, 8])
    bkv_d = din("bkv", [2, D])
    bo_t_d = din("bo_t", [128, 8])
    wl1T_d = din("wl1T", [D, D])
    bl1_t_d = din("bl1_t", [128, 8])
    blk2_d = din("blk2", [2, 128], BF16)

    kd = dint("kd", [64, 128, 128], BF16)
    vd = dint("vd", [64, 128, 128], BF16)
    mr_d = dint("mr_d", [2, NT_O, TT])
    mx_d = dint("mx_d", [2, 128])
    x1_d = dint("x1_d", [8, 128, TOWN], BF16)
    x1p_d = dint("x1p_d", [8, 128, TOWN], BF16)
    kvin_d = dint("kvin", [64, 1040])
    kvout_d = dint("kvout", [64, 1040])
    y_d = dout("y", [8, 128, TOWN])

    base0 = (nc.sbuf_base + 31) & ~31
    pad = base0 - nc.sbuf_base
    TOTAL = 206 * KB
    nc.alloc_sbuf_tensor("arena", [128, TOTAL + pad], mybir.dt.uint8)

    def place(name, shape, dt, off):
        assert off % 32 == 0, (name, off)
        sz = int(np.prod(shape[1:])) * mybir.dt.size(dt)
        assert off + sz <= TOTAL, (name, off, sz, TOTAL)
        return nc.alloc_sbuf_tensor_at(name, shape, dt, offset=base0 + off)

    sc = place("sc", [128, NCONST], F32, 0)
    ones = place("ones", [128, 576], BF16, 64)
    wo_sb = place("wo_sb", [128, 8, 128], BF16, 1216)
    onesf_sb = place("onesf_sb", [1, 128], F32, 3264)
    BIGO = 3808

    xt = place("xt", [128, 2, 8, TT], F32, BIGO + 0)
    xb = place("xb", [128, 2, 8, TT], BF16, BIGO + 32 * KB)
    sq = place("sq", [128, 2, 8, TT], BF16, BIGO + 48 * KB)
    xn = place("xn", [128, 2, 8, TT], BF16, BIGO + 64 * KB)
    mb = place("mb", [128, 2, TT], BF16, BIGO + 80 * KB)
    rb = place("rb", [128, 2, TT], BF16, BIGO + 82 * KB)
    kevac = place("kevac", [128, 2, TT], BF16, BIGO + 84 * KB)
    vevac = place("vevac", [128, 2, 4, 128], BF16, BIGO + 86 * KB)
    l0w = place("l0w", [128, 8, 384], BF16, BIGO + 88 * KB)
    l0wf = place("l0wf", [128, 8, 384], F32, BIGO + 94 * KB)

    wtfc = place("wtfc", [128, 8, 4096], FP16, BIGO + 0)
    wtpj = place("wtpj", [128, 32, 1024], FP16, BIGO + 64 * KB)

    qp = place("qp", [128, 8, TOWN], BF16, BIGO + 0)
    wqT = place("wqT", [128, 8, D], BF16, BIGO + 64 * KB)
    wkT = place("wkT", [128, 8, D], BF16, BIGO + 80 * KB)
    wvT = place("wvT", [128, 8, D], BF16, BIGO + 96 * KB)
    wl1T = place("wl1T", [128, 8, D], BF16, BIGO + 112 * KB)
    woT = place("woT", [128, 8, D], BF16, BIGO + 64 * KB)
    wstgD = place("wstgD", [128, 4, D], F32, BIGO + 96 * KB)

    SCR = BIGO + 128 * KB
    rowt = place("rowt", [1, 4, TT], F32, SCR + 0)
    rowb = place("rowb", [1, 4, TT], BF16, SCR + 8 * KB)
    masks = place("masks", [128, 8, TT], BF16, SCR + 12 * KB)
    q_sb = place("q_sb", [128, NT_O, TT], BF16, SCR + 20 * KB)
    onorm = place("onorm", [128, NT_O, TT], BF16, SCR + 28 * KB)
    WK = SCR + 36 * KB
    pt = place("pt", [128, 2, TT], BF16, WK + 0)
    ksb = place("ksb", [128, 2, 128], BF16, WK + 2 * KB)
    vsb = place("vsb", [128, 2, 128], BF16, WK + 2 * KB + 512)
    rbb = place("rbb", [128, TT], F32, WK + 3 * KB)
    wstg = place("wstg", [128, 2, D], F32, WK + 5 * KB + 256)
    lt_z = place("lt_z", [128, 2, TT], I32, WK + 13 * KB + 256)
    lt_f = place("lt_f", [128, 2, TT], F32, WK + 17 * KB + 256)
    lt_m = place("lt_m", [128, 2, TT], F32, WK + 21 * KB + 256)
    abm = place("abm", [16, 4096], BF16, WK + 25 * KB + 256)
    abm2 = place("abm2", [16, D], BF16, WK + 33 * KB + 256)
    mxc = place("mxc", [128, 2], F32, WK + 35 * KB + 256)
    sclb = place("sclb", [128, 8], F32, WK + 35 * KB + 288)
    xra = place("xra", [128, TTC], F32, WK + 0)
    xrb = place("xrb", [128, TTC], F32, WK + 1 * KB)
    xbt = place("xbt", [128, TTC], FP16, WK + 2 * KB)
    xnc = place("xnc", [128, 2, 8, TTC], FP16, WK + 3 * KB)
    hsb = place("hsb", [128, 32, TTC], FP16, WK + 11 * KB)
    x1t = place("x1t", [128, 2, 8, TTC], BF16, WK + 27 * KB)
    mbc = place("mbc", [128, 2, TTC], FP16, WK + 35 * KB)
    rbc = place("rbc", [128, 2, TTC], FP16, WK + 36 * KB)
    bfc_t = place("bfc_t", [128, 32], F32, WK + 37 * KB)
    bpj_t = place("bpj_t", [128, 8], F32, WK + 37 * KB + 160)
    bstg = place("bstg", [128, 64], F32, WK + 37 * KB + 224)
    DW = SCR + 12 * KB
    x1rd = place("x1rd", [128, 8, TT], BF16, DW + 0)
    xn1 = place("xn1", [128, 8, TT], BF16, DW + 8 * KB)
    xqt = place("xqt", [128, 8, TT], BF16, DW + 16 * KB)
    mbd = place("mbd", [128, TT], BF16, DW + 24 * KB)
    rbd = place("rbd", [128, TT], BF16, DW + 25 * KB)
    kpb = place("kpb", [128, 2, D], BF16, DW + 26 * KB)
    vpb = place("vpb", [128, 2, 16, 65], BF16, DW + 30 * KB)
    kvsb = place("kvsb", [64, 16, 65], F32, DW + 34 * KB + 96)
    ffnt = place("ffnt", [128, 2, TT], BF16, DW + 38 * KB + 192)
    bq_t = place("bq_t", [128, 8], F32, DW + 40 * KB + 192)
    bo_t = place("bo_t", [128, 8], F32, DW + 40 * KB + 224)
    bl1_t = place("bl1_t", [128, 8], F32, DW + 40 * KB + 256)
    bkvb = place("bkvb", [1, 2, D], BF16, DW + 40 * KB + 288)
    ph1 = place("ph1", [128, TT], F32, DW + 44 * KB + 320)
    ph2 = place("ph2", [128, TT], F32, DW + 46 * KB + 320)
    ph9 = place("ph9", [128, TT], F32, DW + 48 * KB + 320)
    ph5 = place("ph5", [128, TT], I32, DW + 50 * KB + 320)
    ph3 = place("ph3", [128, TT], BF16, DW + 52 * KB + 320)
    ph6 = place("ph6", [128, TT], BF16, DW + 53 * KB + 320)
    ph7 = place("ph7", [128, TT], BF16, DW + 54 * KB + 320)
    sqd = place("sqd", [128, TT], BF16, DW + 55 * KB + 320)
    nrm = place("nrm", [128, 8, TT], BF16, DW + 0)
    x1prd = place("x1prd", [128, 8, TT], BF16, DW + 8 * KB)
    outb = place("outb", [128, 2, TT], F32, DW + 16 * KB)
    kvf = place("kvf", [64, 16, 65], F32, DW + 20 * KB)
    kvbb = place("kvbb", [64, 16, 65], BF16, DW + 24 * KB + 96)
    rbe = place("rbe", [64, TT], F32, DW + 26 * KB + 256)
    aev = place("aev", [128, 2, TT], BF16, DW + 28 * KB + 256)
    bo_tE = place("bo_tE", [128, 8], F32, DW + 30 * KB + 256)
    kvbd = place("kvbd", [128, 8, 128], BF16, DW + 31 * KB + 256)
    kvzd = place("kvzd", [128, 8, 2], BF16, DW + 33 * KB + 256)
    blk2 = place("blk2", [2, 128], BF16, DW + 33 * KB + 320)
    nrb = place("nrb", [2, TT], BF16, DW + 34 * KB)
    rbef = place("rbef", [128, TT], F32, DW + 35 * KB + 32)

    import contextlib
    stack = contextlib.ExitStack()
    PSt = lambda *a: stack.enter_context(nc.psum_tensor(*a))
    pb = [PSt(f"pb{i}", [128, TT], F32) for i in range(8)]

    CC = lambda n: sc[:, CONST_COLS[n][0]:CONST_COLS[n][0] + 1]
    CCr = lambda n: sc[0:1, CONST_COLS[n][0]:CONST_COLS[n][0] + 1]
    ONE_COL = ones[:, 0:1]
    ONE_ROW = ones[0:1, 0:128]

    P.dma("init", sc[:], consts_d, writes=["consts"])
    P.dma("init", ones[:], onesbf_d, writes=["ones"])
    P.dma("init", onesf_sb[:], onesf_d, writes=["onesf"])
    P.dma("init", masks[:], masks_d, writes=["masks"])
    for dc in range(8):
        P.dma("init", l0wf[:, dc, :], l0qkv_d[dc * 128:(dc + 1) * 128, :], writes=[("l0wf", dc)])
    for dc in range(8):
        P.op("vector", lambda v, dc=dc: v.tensor_copy(l0w[:, dc, :], l0wf[:, dc, :]),
             reads=[("l0wf", dc)], writes=[("l0w", dc)])

    def ln_tile(src_ap, rot, save=None):
        P.dma(f"x{rot}", xt[:, rot, :, :], src_ap, writes=[("xt", rot)], tag="ldx")
        for dc in range(8):
            P.op("vector", lambda v, dc=dc, rot=rot: v.tensor_copy(xb[:, rot, dc, :], xt[:, rot, dc, :]),
                 reads=[("xt", rot)], writes=[("xb", rot, dc)], tag="cast")
            P.op("gpsimd", lambda g, dc=dc, rot=rot: g.tensor_tensor(sq[:, rot, dc, :], xb[:, rot, dc, :], xb[:, rot, dc, :], ALU.mult),
                 reads=[("xb", rot, dc)], writes=[("sq", rot, dc)], tag="sq")
        for dc in range(8):
            P.op("tensor", lambda te, dc=dc, rot=rot: te.matmul(pb[0][0:1, :], ONE_COL, xb[:, rot, dc, :],
                                                                start=(dc == 0), stop=(dc == 7)),
                 reads=[("xb", rot, dc), "ones"], accum=("pb0", dc == 0, dc == 7), tag="summ")
        for dc in range(8):
            P.op("tensor", lambda te, dc=dc, rot=rot: te.matmul(pb[1][0:1, :], ONE_COL, sq[:, rot, dc, :],
                                                                start=(dc == 0), stop=(dc == 7)),
                 reads=[("sq", rot, dc), "ones"], accum=("pb1", dc == 0, dc == 7), tag="sumsq")
        m_ap, var_ap, s_ap, r_ap = (rowt[0:1, i, :] for i in range(4))
        m_bf, r_bf = rowb[0:1, 0, :], rowb[0:1, 1, :]
        P.op("vector", lambda v: v.tensor_scalar(m_ap, pb[0][0:1, :], 1.0 / D, None, ALU.mult),
             reads=["pb0"], writes=[("rowt", 0)])
        P.op("vector", lambda v: v.tensor_tensor(var_ap, m_ap, m_ap, ALU.mult),
             reads=[("rowt", 0)], writes=[("rowt", 1)])
        P.op("vector", lambda v: v.scalar_tensor_tensor(var_ap, pb[1][0:1, :], 1.0 / D, var_ap, ALU.mult, ALU.subtract),
             reads=["pb1", ("rowt", 1)], writes=[("rowt", 1)])
        P.op("scalar", lambda s: s.activation(s_ap, var_ap, AF.Sqrt, bias=CCr("eps_ln"), scale=1.0),
             reads=[("rowt", 1), "consts"], writes=[("rowt", 2)])
        P.op("vector", lambda v: v.reciprocal(r_ap, s_ap), reads=[("rowt", 2)], writes=[("rowt", 3)])
        P.op("vector", lambda v: v.tensor_copy(m_bf, m_ap), reads=[("rowt", 0)], writes=[("rowb", 0)])
        P.op("vector", lambda v: v.tensor_copy(r_bf, r_ap), reads=[("rowt", 3)], writes=[("rowb", 1)])
        if save is not None:
            P.dma("mrstm", mr_d[0, save, :], m_ap, reads=[("rowt", 0)], writes=[("mrd", 0, save)])
            P.dma("mrstr", mr_d[1, save, :], r_ap, reads=[("rowt", 3)], writes=[("mrd", 1, save)])
        P.op("tensor", lambda te: te.matmul(pb[2][:, :], ONE_ROW, m_bf, start=True, stop=True),
             reads=[("rowb", 0), "ones"], writes=["pb2"])
        P.op("tensor", lambda te: te.matmul(pb[3][:, :], ONE_ROW, r_bf, start=True, stop=True),
             reads=[("rowb", 1), "ones"], writes=["pb3"])
        P.op("scalar", lambda s, rot=rot: s.activation(mb[:, rot, :], pb[2][:, :], AF.Copy),
             reads=["pb2"], writes=[("mb", rot)])
        P.op("scalar", lambda s, rot=rot: s.activation(rb[:, rot, :], pb[3][:, :], AF.Copy),
             reads=["pb3"], writes=[("rb", rot)])
        for dc in range(8):
            eng = "vector" if dc % 2 == 0 else "gpsimd"
            P.op(eng, lambda e, dc=dc, rot=rot: e.tensor_tensor(xn[:, rot, dc, :], xb[:, rot, dc, :], mb[:, rot, :], ALU.subtract),
                 reads=[("xb", rot, dc), ("mb", rot)], writes=[("xn", rot, dc)])
            P.op(eng, lambda e, dc=dc, rot=rot: e.tensor_tensor(xn[:, rot, dc, :], xn[:, rot, dc, :], rb[:, rot, :], ALU.mult),
                 reads=[("xn", rot, dc), ("rb", rot)], writes=[("xn", rot, dc)])

    for t in range(NT_G):
        rot = t % 2
        ln_tile(xfull.rearrange("(c p) t -> p c t", p=128)[:, :, t * TT:(t + 1) * TT], rot)
        for dc in range(8):
            P.op("tensor", lambda te, dc=dc, rot=rot: te.matmul(pb[4][:, :], l0w[:, dc, 128:256], xn[:, rot, dc, :],
                                                                start=(dc == 0), stop=(dc == 7)),
                 reads=[("xn", rot, dc), ("l0w", dc)], accum=("pb4", dc == 0, dc == 7), tag="kmm")
        P.op("scalar", lambda s, rot=rot: s.activation(kevac[:, rot, :], pb[4][:, :], AF.Copy),
             reads=["pb4"], writes=[("kevac", rot)])
        P.dma(f"kst{rot}", kd[4 * t:4 * t + 4, :, :].rearrange("j p t -> p j t"), kevac[:, rot, :],
              reads=[("kevac", rot)], writes=[("kdt", t)])
        for s4 in range(4):
            bank = 5 + (s4 % 2)
            for dc in range(8):
                P.op("tensor", lambda te, dc=dc, s4=s4, bank=bank, rot=rot: te.matmul(
                        pb[bank][:, 0:128], xn[:, rot, dc, s4 * 128:(s4 + 1) * 128], l0w[:, dc, 256:384],
                        start=(dc == 0), stop=(dc == 7)),
                     reads=[("xn", rot, dc), ("l0w", dc)], accum=(f"pb{bank}", dc == 0, dc == 7), tag="vmm")
            P.op("scalar", lambda s, bank=bank, s4=s4, rot=rot: s.activation(vevac[:, rot, s4, :], pb[bank][:, 0:128], AF.Copy),
                 reads=[f"pb{bank}"], writes=[("vevac", rot, s4)])
        P.dma(f"vst{rot}", vd[4 * t:4 * t + 4, :, :].rearrange("j p t -> p j t"), vevac[:, rot, :, :],
              reads=[("vevac", rot, s4) for s4 in range(4)], writes=[("vdt", t)])
    for t in range(NT_O):
        rot = t % 2
        ln_tile(xown.rearrange("(c p) t -> p c t", p=128)[:, :, t * TT:(t + 1) * TT], rot, save=t)
        for dc in range(8):
            P.op("tensor", lambda te, dc=dc, rot=rot: te.matmul(pb[7][:, :], l0w[:, dc, 0:128], xn[:, rot, dc, :],
                                                                start=(dc == 0), stop=(dc == 7)),
                 reads=[("xn", rot, dc), ("l0w", dc)], accum=("pb7", dc == 0, dc == 7), tag="qmm")
        P.op("scalar", lambda s, t=t: s.activation(q_sb[:, t, :], pb[7][:, :], AF.Copy),
             reads=["pb7"], writes=[("q", t)])

    outs = []
    if dbg_stage == "A":
        q_out = dout("q_out", [128, TOWN], BF16)
        mr_out = dout("mr_out", [2, NT_O, TT])
        k_out = dout("k_out", [64, 128, 128], BF16)
        v_out = dout("v_out", [64, 128, 128], BF16)
        for t in range(NT_O):
            P.dma("dbg", q_out[:, t * TT:(t + 1) * TT], q_sb[:, t, :],
                  reads=[("q", t)], writes=[("out", "q", t)])
        P.dma("dbg", mr_out[:, :, :], mr_d[:, :, :],
              reads=[("mrd", i, t) for i in range(2) for t in range(NT_O)], writes=[("out", "mr")])
        P.dma("dbg", k_out[:, :, :], kd[:, :, :], reads=[("kdt", t) for t in range(NT_G)], writes=[("out", "k")])
        P.dma("dbg", v_out[:, :, :], vd[:, :, :], reads=[("vdt", t) for t in range(NT_G)], writes=[("out", "v")])
        outs = ["q_out", "mr_out", "k_out", "v_out"]

    if st >= 1:
        P.fence("A->B")
        P.dma("winit", l0wf[:, 0, 0:256].rearrange("p (c t) -> p c t", c=2), None, writes=[]) if False else None
        wof = l0wf
        P.dma("winit", wof[:, 0, 0:128], l0wo_d[:, 0:1024].rearrange("p (c t) -> p c t", c=8)[:, 0, :][:, 0:128], writes=[]) if False else None
        P.dma("winit", wof[:, 0:3, :].rearrange("p c t -> p (c t)")[:, 0:1024], l0wo_d[:, :],
              writes=[("wof",)])
        P.op("vector", lambda v: v.tensor_copy(wo_sb[:, :, :].rearrange("p c t -> p (c t)"),
                                               wof[:, 0:3, :].rearrange("p c t -> p (c t)")[:, 0:1024]),
             reads=[("wof",)], writes=["wo"])

        def ld16(dstslice, key, src_ap, stg_key, fam):
            P.dma("abld", wstg[0:16, 0, :], src_ap, writes=[("wstg", 0)])
            P.op("vector", lambda v: v.tensor_copy(dstslice, wstg[0:16, 0, :]),
                 reads=[("wstg", 0)], writes=[key])

        def attn_block(i):
            nk = 8 * i + 8
            for j in range(nk):
                jr = j % 2
                sbank = 4 + jr
                P.dma(f"kld{jr}", ksb[:, jr, :], kd[j, :, :], reads=[("kdt", j // 4)],
                      writes=[("ksb", jr)], tag="kld")
                P.dma(f"vld{jr}", vsb[:, jr, :], vd[j, :, :], reads=[("vdt", j // 4)],
                      writes=[("vsb", jr)], tag="vld")
                P.op("tensor", lambda te, i=i, jr=jr, sbank=sbank: te.matmul(
                        pb[sbank][:, :], ksb[:, jr, :], q_sb[:, i, :], start=True, stop=True),
                     reads=[("ksb", jr), ("q", i)], writes=[f"pb{sbank}"], tag="smm")
                P.op("scalar", lambda s, jr=jr, sbank=sbank: s.activation(
                        pt[:, jr, :], pb[sbank][:, :], AF.Exp, bias=CC("neg16"), scale=SM_SCALE),
                     reads=[f"pb{sbank}", "consts"], writes=[("pt", jr)], tag="exp")
                if j >= 8 * i:
                    mi = j - 8 * i
                    P.op("vector", lambda v, jr=jr, mi=mi: v.tensor_tensor(
                            pt[:, jr, :], pt[:, jr, :], masks[:, mi, :], ALU.mult),
                         reads=[("pt", jr), "masks"], writes=[("pt", jr)], tag="mask")
                P.op("tensor", lambda te, jr=jr, j=j, nk=nk: te.matmul(pb[6][:, :], vsb[:, jr, :], pt[:, jr, :],
                                                                       start=(j == 0), stop=(j == nk - 1)),
                     reads=[("vsb", jr), ("pt", jr)], accum=("pb6", j == 0, j == nk - 1), tag="omm")
                P.op("tensor", lambda te, jr=jr, j=j, nk=nk: te.matmul(pb[7][0:1, :], ONE_COL, pt[:, jr, :],
                                                                       start=(j == 0), stop=(j == nk - 1)),
                     reads=[("pt", jr), "ones"], accum=("pb7", j == 0, j == nk - 1), tag="dmm")
            P.op("vector", lambda v: v.reciprocal(rowt[0:1, 0, :], pb[7][0:1, :]),
                 reads=["pb7"], writes=[("rowt", 0)])
            P.op("vector", lambda v: v.tensor_copy(rowb[0:1, 2, :], rowt[0:1, 0, :]),
                 reads=[("rowt", 0)], writes=[("rowb", 2)])
            P.op("tensor", lambda te: te.matmul(pb[3][:, :], ONE_ROW, rowb[0:1, 2, :], start=True, stop=True),
                 reads=[("rowb", 2), "ones"], writes=["pb3"])
            P.op("scalar", lambda s: s.activation(rbb[:, :], pb[3][:, :], AF.Copy),
                 reads=["pb3"], writes=["rbb"])
            P.op("vector", lambda v, i=i: v.tensor_tensor(onorm[:, i, :], pb[6][:, :], rbb[:, :], ALU.mult),
                 reads=["pb6", "rbb"], writes=[("onorm", i)], tag="onorm")

        wq = []

        def wq_pass1(fam, src_d, nch, wslice):
            for ch in range(nch):
                for cc in range(4096 // D if fam == "wfc" else 1):
                    def th(ch=ch, cc=cc, fam=fam, src_d=src_d, wslice=wslice):
                        r = (ch + cc) % 2
                        P.dma(f"w{fam}{r}", wstg[:, r, :], src_d[ch * 128:(ch + 1) * 128, cc * D:(cc + 1) * D],
                              writes=[("wstg", r)], tag="wld")
                        P.op("vector", lambda v: v.tensor_reduce(mxc[:, r:r + 1], wstg[:, r, :], AX, ALU.max,
                                                                 apply_absolute_value=True),
                             reads=[("wstg", r)], writes=[("mxcol", fam, r)], tag="wmx")
                        P.op("vector", lambda v: v.tensor_tensor(sclb[:, 3:4], sclb[:, 3:4], mxc[:, r:r + 1], ALU.max),
                             reads=[("mxcol", fam, r), ("mxacc", fam)], writes=[("mxacc", fam)], tag="wmxa")
                        dst, dkey = wslice(ch, cc)
                        P.op("vector", lambda v: v.tensor_copy(dst, wstg[:, r, :]),
                             reads=[("wstg", r)], writes=[dkey], tag="wcast")
                    wq.append(th)

        def wq_scale(fam):
            def th(fam=fam):
                mslot = 0 if fam == "wfc" else 1
                P.dma("mxstfc" if fam == "wfc" else "mxstpj",
                      mx_d[mslot, :], sclb[:, 3:4], reads=[("mxacc", fam)], writes=[("mxd", fam)])
                P.dma("mxldfc" if fam == "wfc" else "mxldpj",
                      rowt[0:1, 0, 0:128], mx_d[mslot, :], reads=[("mxd", fam)], writes=[("rowt", 0)])
                P.op("vector", lambda v: v.tensor_reduce(rowt[0:1, 1, 0:1], rowt[0:1, 0, 0:128], AX, ALU.max,
                                                         apply_absolute_value=True),
                     reads=[("rowt", 0)], writes=[("rowt", 1)])
                P.op("vector", lambda v: v.tensor_scalar(rowt[0:1, 1, 1:2], rowt[0:1, 1, 0:1], 1.0 / 127.0, None, ALU.mult),
                     reads=[("rowt", 1)], writes=[("rowt", 1)])
                P.op("vector", lambda v: v.reciprocal(rowt[0:1, 1, 2:3], rowt[0:1, 1, 1:2]),
                     reads=[("rowt", 1)], writes=[("rowt", 1)])
                P.op("tensor", lambda te: te.matmul(pb[3][:, 0:2], onesf_sb[0:1, :], rowt[0:1, 1, 1:3],
                                                    start=True, stop=True),
                     reads=[("rowt", 1), ("rowt", 1), "onesf"], writes=["pb3"])
                P.op("scalar", lambda s: s.activation(sclb[:, 0:2], pb[3][:, 0:2], AF.Copy),
                     reads=["pb3"], writes=[("sclb", fam)])
                if fam == "wfc":
                    P.op("gpsimd", lambda g: g.memset(sclb[:, 3:4], 0.0),
                         writes=[("mxacc", "wfc"), ("mxacc", "wpj")])
            wq.append(th)

        def wq_pass2(fam, nch, ncc512, wslice512, a_lhs, b_rhs):
            for ch in range(nch):
                for fc in range(ncc512):
                    def th(ch=ch, fc=fc, fam=fam, wslice512=wslice512, a_lhs=a_lhs, b_rhs=b_rhs):
                        r = (ch * ncc512 + fc) % 2
                        wsl, wkey = wslice512(ch, fc)
                        P.op("tensor", lambda te: te.matmul(pb[2][:, :], a_lhs(ch), b_rhs(fc), start=True, stop=True),
                             reads=[("abm",)], writes=["pb2"], tag="lmm")
                        P.op("scalar", lambda s: s.activation(lt_m[:, r, :], pb[2][:, :], AF.Copy, scale=2.0),
                             reads=["pb2"], writes=[("ltm", r)], tag="lme")
                        P.op("vector", lambda v: v.tensor_scalar(lt_z[:, r, :], wsl, sclb[:, 1:2], None, ALU.mult),
                             reads=[wkey, ("sclb", fam)], writes=[("ltz", r)], tag="lz")
                        P.op("vector", lambda v: v.tensor_scalar(lt_f[:, r, :], lt_z[:, r, :], sclb[:, 0:1], None, ALU.mult),
                             reads=[("ltz", r), ("sclb", fam)], writes=[("ltf", r)], tag="lzs")
                        P.op("gpsimd", lambda g: g.tensor_tensor(wsl, lt_f[:, r, :], lt_m[:, r, :], ALU.add),
                             reads=[("ltf", r), ("ltm", r)], writes=[wkey], tag="lw")
                    wq.append(th)

        def fc_slice1024(ch, cc):
            return wtfc[:, ch, cc * D:(cc + 1) * D], ("wtfc", ch, cc)

        def fc_slice512(ch, fc):
            return wtfc[:, ch, fc * TT:(fc + 1) * TT], ("wtfc", ch, fc // 2)

        def pj_slice1024(ch, cc):
            return wtpj[:, ch * 4:(ch + 1) * 4, :].rearrange("p c t -> p (c t)")[:, 0:D] if False else (None, None)

        def pj_slice1024b(ch, cc):
            return wtpj[:, ch, :], ("wtpj", ch)

        def pj_slice512(ch, fc):
            return wtpj[:, ch, fc * TT:(fc + 1) * TT], ("wtpj", ch)

        def afc_lhs(ch):
            return abm2[:, ch * 128:(ch + 1) * 128]

        def bfc_rhs(fc):
            return abm[:, fc * TT:(fc + 1) * TT]

        def apj_lhs(ch):
            return abm[:, ch * 128:(ch + 1) * 128]

        def bpj_rhs(fc):
            return abm2[:, fc * TT:(fc + 1) * TT]

        def fc_mats():
            P.op("gpsimd", lambda g: g.memset(sclb[:, 3:4], 0.0), writes=[("mxacc", "wfc")])
            ld16(abm2[:, :], ("abm",), afc_d[:, :], "afc", "wfc")
            for q in range(4):
                ld16(abm[:, q * D:(q + 1) * D], ("abm",), bfcT_d[:, q * D:(q + 1) * D], f"bfcT{q}", "wfc")
        def pj_mats():
            for q in range(4):
                ld16(abm[:, q * D:(q + 1) * D], ("abm",), apj_d[:, q * D:(q + 1) * D], f"apj{q}", "wpj")
            ld16(abm2[:, :], ("abm",), bpjT_d[:, :], "bpjT", "wpj")

        wq.append(fc_mats)
        wq_pass1("wfc", wfcT_d, 8, fc_slice1024)
        wq_scale("wfc")
        wq_pass2("wfc", 8, 8, fc_slice512, afc_lhs, bfc_rhs)
        wq.append(pj_mats)
        wq_pass1("wpj", wpjT_d, 32, pj_slice1024b)
        wq_scale("wpj")
        wq_pass2("wpj", 32, 2, pj_slice512, apj_lhs, bpj_rhs)

        per = (len(wq) + NT_O - 1) // NT_O
        for i in range(NT_O):
            attn_block(i)
            for th in wq[i * per:(i + 1) * per]:
                th()

        if dbg_stage == "B":
            on_out = dout("on_out", [128, TOWN], BF16)
            wt_out = dout("wt_out", [128, 8, 4096], FP16)
            wt2_out = dout("wt2_out", [128, 32, 1024], FP16)
            for i in range(NT_O):
                P.dma("dbg", on_out[:, i * TT:(i + 1) * TT], onorm[:, i, :],
                      reads=[("onorm", i)], writes=[("out", "on", i)])
            P.dma("dbg", wt_out[:, :, :], wtfc[:, :, :],
                  reads=[("wtfc", ch, cc) for ch in range(8) for cc in range(4)], writes=[("out", "wt")])
            P.dma("dbg", wt2_out[:, :, :], wtpj[:, :, :],
                  reads=[("wtpj", ch) for ch in range(32)], writes=[("out", "wt2")])
            outs = ["on_out", "wt_out", "wt2_out"]

    if st >= 2:
        P.fence("B->C")
        P.dma("cinit", bstg[:, 0:32], bfc_t_d, writes=[("bstg", 0)])
        P.op("vector", lambda v: v.tensor_copy(bfc_t[:, :], bstg[:, 0:32]), reads=[("bstg", 0)], writes=["bfct"])
        P.dma("cinit", bstg[:, 32:40], bpj_t_d, writes=[("bstg", 1)])
        P.op("vector", lambda v: v.tensor_copy(bpj_t[:, :], bstg[:, 32:40]), reads=[("bstg", 1)], writes=["bpjt"])

        for tc in range(NT_C):
            rot = tc % 2
            t512 = tc // 2
            half = tc % 2
            cofs = half * TTC
            P.dma(f"mrldm{rot}", rowt[0:1, 2 * rot, 0:TTC], mr_d[0, t512, cofs:cofs + TTC],
                  reads=[("mrd", 0, t512)], writes=[("rowt", 2 * rot)], tag="mrld")
            P.dma(f"mrldr{rot}", rowt[0:1, 2 * rot + 1, 0:TTC], mr_d[1, t512, cofs:cofs + TTC],
                  reads=[("mrd", 1, t512)], writes=[("rowt", 2 * rot + 1)], tag="mrld")
            P.op("vector", lambda v, rot=rot: v.tensor_copy(rowb[0:1, 2 * rot, 0:TTC], rowt[0:1, 2 * rot, 0:TTC]),
                 reads=[("rowt", 2 * rot)], writes=[("rowb", 2 * rot)])
            P.op("vector", lambda v, rot=rot: v.tensor_copy(rowb[0:1, 2 * rot + 1, 0:TTC], rowt[0:1, 2 * rot + 1, 0:TTC]),
                 reads=[("rowt", 2 * rot + 1)], writes=[("rowb", 2 * rot + 1)])
            P.op("tensor", lambda te, rot=rot: te.matmul(pb[2][:, 0:TTC], ONE_ROW, rowb[0:1, 2 * rot, 0:TTC],
                                                         start=True, stop=True),
                 reads=[("rowb", 2 * rot), "ones"], writes=["pb2"])
            P.op("scalar", lambda s, rot=rot: s.activation(mbc[:, rot, :], pb[2][:, 0:TTC], AF.Copy),
                 reads=["pb2"], writes=[("mbc", rot)])
            P.op("tensor", lambda te, rot=rot: te.matmul(pb[3][:, 0:TTC], ONE_ROW, rowb[0:1, 2 * rot + 1, 0:TTC],
                                                         start=True, stop=True),
                 reads=[("rowb", 2 * rot + 1), "ones"], writes=["pb3"])
            P.op("scalar", lambda s, rot=rot: s.activation(rbc[:, rot, :], pb[3][:, 0:TTC], AF.Copy),
                 reads=["pb3"], writes=[("rbc", rot)])
            src3 = xown.rearrange("(c p) t -> p c t", p=128)
            for dc in range(8):
                P.dma("xca", xra[:, :], src3[:, dc, tc * TTC:(tc + 1) * TTC],
                      writes=[("xra",)], tag="ldxc")
                P.op("vector", lambda v: v.tensor_copy(xbt[:, :], xra[:, :]),
                     reads=[("xra",)], writes=[("xbt",)])
                eng = "vector" if dc % 2 == 0 else "gpsimd"
                P.op(eng, lambda e, dc=dc, rot=rot: e.tensor_tensor(
                        xnc[:, rot, dc, :], xbt[:, :], mbc[:, rot, :], ALU.subtract),
                     reads=[("xbt",), ("mbc", rot)], writes=[("xnc", rot, dc)])
                P.op(eng, lambda e, dc=dc, rot=rot: e.tensor_tensor(
                        xnc[:, rot, dc, :], xnc[:, rot, dc, :], rbc[:, rot, :], ALU.mult),
                     reads=[("xnc", rot, dc), ("rbc", rot)], writes=[("xnc", rot, dc)])
            for f in range(32):
                bank = f % 8
                for dc in range(8):
                    P.op("tensor", lambda te, f=f, dc=dc, bank=bank, rot=rot: te.matmul(
                            pb[bank][:, 0:TTC], wtfc[:, dc, f * 128:(f + 1) * 128], xnc[:, rot, dc, :],
                            start=(dc == 0), stop=(dc == 7)),
                         reads=[("xnc", rot, dc), ("wtfc", dc, f // 8)], accum=(f"pb{bank}", dc == 0, dc == 7),
                         tag="fcmm")
                P.op("scalar", lambda s, f=f, bank=bank: s.activation(
                        hsb[:, f, :], pb[bank][:, 0:TTC], AF.Gelu, bias=bfc_t[:, f:f + 1], scale=1.0),
                     reads=[f"pb{bank}", "bfct"], writes=[("h", f)], tag="gelu")
            for do in range(8):
                bank = do % 8
                for fc in range(32):
                    P.op("tensor", lambda te, do=do, fc=fc, bank=bank: te.matmul(
                            pb[bank][:, 0:TTC], wtpj[:, fc, do * 128:(do + 1) * 128], hsb[:, fc, :],
                            start=(fc == 0), stop=False),
                         reads=[("h", fc), ("wtpj", fc)], accum=(f"pb{bank}", fc == 0, False), tag="pjmm")
                P.op("tensor", lambda te, do=do, bank=bank, t512=t512, cofs=cofs: te.matmul(
                        pb[bank][:, 0:TTC], wo_sb[:, do, :], onorm[:, t512, cofs:cofs + TTC],
                        start=False, stop=True),
                     reads=["wo", ("onorm", t512)], accum=(f"pb{bank}", False, True), tag="womm")
                P.dma("xcb", xrb[:, :], src3[:, do, tc * TTC:(tc + 1) * TTC],
                      writes=[("xrb",)], tag="ldxr")
                P.op("scalar", lambda s, do=do, bank=bank, rot=rot: s.activation(
                        x1t[:, rot, do, :], pb[bank][:, 0:TTC], AF.Identity, bias=bpj_t[:, do:do + 1], scale=1.0),
                     reads=[f"pb{bank}", "bpjt"], writes=[("x1t", rot, do)], tag="pjev")
                eng = "vector" if do % 2 == 0 else "gpsimd"
                P.op(eng, lambda e, do=do, rot=rot: e.tensor_tensor(
                        x1t[:, rot, do, :], x1t[:, rot, do, :], xrb[:, :], ALU.add),
                     reads=[("x1t", rot, do), ("xrb",)], writes=[("x1t", rot, do)], tag="resadd")
            P.dma(f"x1st{rot}", x1_d[:, :, tc * TTC:(tc + 1) * TTC].rearrange("c p t -> p c t"), x1t[:, rot, :, :],
                  reads=[("x1t", rot, do) for do in range(8)], writes=[("x1d", tc)], tag="x1st")

        if dbg_stage == "C":
            x1_out = dout("x1_out", [8, 128, TOWN], BF16)
            P.dma("dbg", x1_out[:, :, :], x1_d[:, :, :],
                  reads=[("x1d", tc) for tc in range(NT_C)], writes=[("out", "x1")])
            outs = ["x1_out"]

    if st >= 3:
        P.fence("C->D")
        for wname, wdst, wsrc in (("wq", wqT, wqT_d), ("wk", wkT, wkT_d),
                                  ("wv", wvT, wvT_d), ("wl", wl1T, wl1T_d)):
            P.dma("dinitg", wdst[:, :, :], wsrc.rearrange("(c p) t -> p c t", p=128),
                  queue="gpsimd", writes=[("w1", wname)])
        P.dma("dinitg", bkvb[0:1, :, :], bkv_d.rearrange("a b -> (a b)").rearrange("(a b) -> a b", a=1).rearrange("a (c d) -> a c d", c=2), queue="gpsimd", writes=["bkvb"])
        P.dma("dinit", bq_t[:, :], bq_t_d, writes=["bqt"])
        P.dma("dinit", bl1_t[:, :], bl1_t_d, writes=["bl1t"])
        P.op("gpsimd", lambda g: g.memset(vpb[:, :, :, 64:65], 1.0), writes=["vpones"])
        P.op("vector", lambda v: v.memset(kvsb[:, :, :], 0.0), writes=["kvsb"])

        def phi_evac(psum_ap, out_ap, out_key, bias_col, pbkey, width):
            P.op("scalar", lambda sx: sx.activation(ph1[:, 0:width], psum_ap, AF.Identity,
                                                    bias=bias_col, scale=1.0),
                 reads=[pbkey, "bqt"], writes=["ph1"], tag="praw")
            P.op("vector", lambda v: v.tensor_scalar(ph2[:, 0:width], ph1[:, 0:width], 0.0, None, ALU.min),
                 reads=["ph1"], writes=["ph2"], tag="pmin")
            P.op("scalar", lambda sx: sx.activation(ph9[:, 0:width], ph2[:, 0:width], AF.Exp),
                 reads=["ph2"], writes=["ph9"], tag="pexp")
            P.op("vector", lambda v: v.scalar_tensor_tensor(out_ap, ph1[:, 0:width], 0.0, ph9[:, 0:width],
                                                            ALU.max, ALU.add),
                 reads=["ph1", "ph9"], writes=[out_key], tag="phi")

        for it in range(NT_O):
            c0 = it * TT
            P.dma("x1r", x1rd[:, :, :], x1_d[:, :, c0:c0 + TT].rearrange("c p t -> p c t"),
                  reads=[("x1d", 2 * it), ("x1d", 2 * it + 1)], writes=["x1rd"], tag="x1ld")
            for dc in range(8):
                P.op("tensor", lambda te, dc=dc: te.matmul(pb[0][0:1, :], ONE_COL, x1rd[:, dc, :],
                                                           start=(dc == 0), stop=(dc == 7)),
                     reads=["x1rd", "ones"], accum=("pb0", dc == 0, dc == 7), tag="dsum")
            for dc in range(8):
                P.op("gpsimd", lambda g, dc=dc: g.tensor_tensor(sqd[:, :], x1rd[:, dc, :], x1rd[:, dc, :], ALU.mult),
                     reads=["x1rd"], writes=["sqd"], tag="dsq")
                P.op("tensor", lambda te, dc=dc: te.matmul(pb[1][0:1, :], ONE_COL, sqd[:, :],
                                                           start=(dc == 0), stop=(dc == 7)),
                     reads=["sqd", "ones"], accum=("pb1", dc == 0, dc == 7), tag="dsumsq")
            m_ap, var_ap, s_ap, r_ap = (rowt[0:1, i, :] for i in range(4))
            m_bf, r_bf = rowb[0:1, 0, :], rowb[0:1, 1, :]
            P.op("vector", lambda v: v.tensor_scalar(m_ap, pb[0][0:1, :], 1.0 / D, None, ALU.mult),
                 reads=["pb0"], writes=[("rowt", 0)])
            P.op("vector", lambda v: v.tensor_tensor(var_ap, m_ap, m_ap, ALU.mult),
                 reads=[("rowt", 0)], writes=[("rowt", 1)])
            P.op("vector", lambda v: v.scalar_tensor_tensor(var_ap, pb[1][0:1, :], 1.0 / D, var_ap, ALU.mult, ALU.subtract),
                 reads=["pb1", ("rowt", 1)], writes=[("rowt", 1)])
            P.op("scalar", lambda sx: sx.activation(s_ap, var_ap, AF.Sqrt, bias=CCr("eps_ln"), scale=1.0),
                 reads=[("rowt", 1), "consts"], writes=[("rowt", 2)])
            P.op("vector", lambda v: v.reciprocal(r_ap, s_ap), reads=[("rowt", 2)], writes=[("rowt", 3)])
            P.op("vector", lambda v: v.tensor_copy(m_bf, m_ap), reads=[("rowt", 0)], writes=[("rowb", 0)])
            P.op("vector", lambda v: v.tensor_copy(r_bf, r_ap), reads=[("rowt", 3)], writes=[("rowb", 1)])
            P.op("tensor", lambda te: te.matmul(pb[2][:, :], ONE_ROW, m_bf, start=True, stop=True),
                 reads=[("rowb", 0), "ones"], writes=["pb2"])
            P.op("tensor", lambda te: te.matmul(pb[3][:, :], ONE_ROW, r_bf, start=True, stop=True),
                 reads=[("rowb", 1), "ones"], writes=["pb3"])
            P.op("scalar", lambda sx: sx.activation(mbd[:, :], pb[2][:, :], AF.Copy),
                 reads=["pb2"], writes=["mbd"])
            P.op("scalar", lambda sx: sx.activation(rbd[:, :], pb[3][:, :], AF.Copy),
                 reads=["pb3"], writes=["rbd"])
            for dc in range(8):
                eng = "vector" if dc % 2 == 0 else "gpsimd"
                P.op(eng, lambda e, dc=dc: e.tensor_tensor(xn1[:, dc, :], x1rd[:, dc, :], mbd[:, :], ALU.subtract),
                     reads=["x1rd", "mbd"], writes=[("xn1", dc)])
                P.op(eng, lambda e, dc=dc: e.tensor_tensor(xn1[:, dc, :], xn1[:, dc, :], rbd[:, :], ALU.mult),
                     reads=[("xn1", dc), "rbd"], writes=[("xn1", dc)])
            for oc in range(8):
                bank = oc % 2
                for dc in range(8):
                    P.op("tensor", lambda te, oc=oc, dc=dc, bank=bank: te.matmul(
                            pb[bank][:, :], wqT[:, dc, oc * 128:(oc + 1) * 128], xn1[:, dc, :],
                            start=(dc == 0), stop=(dc == 7)),
                         reads=[("xn1", dc), ("w1", "wq")], accum=(f"pb{bank}", dc == 0, dc == 7), tag="qpmm")
                phi_evac(pb[bank][:, :], qp[:, oc, c0:c0 + TT], ("qp", oc, it), bq_t[:, oc:oc + 1],
                         f"pb{bank}", TT)
            for s4 in range(4):
                sr = s4 % 2
                for oh in range(2):
                    bank = 2 + oh
                    for dc in range(8):
                        P.op("tensor", lambda te, s4=s4, oh=oh, dc=dc, bank=bank: te.matmul(
                                pb[bank][:, :], xn1[:, dc, s4 * 128:(s4 + 1) * 128], wkT[:, dc, oh * 512:(oh + 1) * 512],
                                start=(dc == 0), stop=False),
                             reads=[("xn1", dc), ("w1", "wk")], accum=(f"pb{bank}", dc == 0, False), tag="kpmm")
                    P.op("tensor", lambda te, oh=oh, bank=bank: te.matmul(
                            pb[bank][:, :], ONE_ROW, bkvb[0:1, 0, oh * 512:(oh + 1) * 512],
                            start=False, stop=True),
                         reads=["bkvb", "ones"], accum=(f"pb{bank}", False, True), tag="kbias")
                    P.op("scalar", lambda sx, bank=bank: sx.activation(ph1[:, :], pb[bank][:, :], AF.Copy),
                         reads=[f"pb{bank}"], writes=["ph1"], tag="kraw")
                    P.op("vector", lambda v: v.tensor_scalar(ph2[:, :], ph1[:, :], 0.0, None, ALU.min),
                         reads=["ph1"], writes=["ph2"])
                    P.op("scalar", lambda sx: sx.activation(ph9[:, :], ph2[:, :], AF.Exp),
                         reads=["ph2"], writes=["ph9"])
                    P.op("vector", lambda v, sr=sr, oh=oh: v.scalar_tensor_tensor(
                            kpb[:, sr, oh * 512:(oh + 1) * 512], ph1[:, :], 0.0, ph9[:, :], ALU.max, ALU.add),
                         reads=["ph1", "ph9"], writes=[("kpb", sr)], tag="kphi")
                    bank2 = 2 + oh
                    for dc in range(8):
                        P.op("tensor", lambda te, s4=s4, oh=oh, dc=dc, bank2=bank2: te.matmul(
                                pb[bank2][:, :], xn1[:, dc, s4 * 128:(s4 + 1) * 128], wvT[:, dc, oh * 512:(oh + 1) * 512],
                                start=(dc == 0), stop=False),
                             reads=[("xn1", dc), ("w1", "wv")], accum=(f"pb{bank2}", dc == 0, False), tag="vpmm")
                    P.op("tensor", lambda te, oh=oh, bank2=bank2: te.matmul(
                            pb[bank2][:, :], ONE_ROW, bkvb[0:1, 1, oh * 512:(oh + 1) * 512],
                            start=False, stop=True),
                         reads=["bkvb", "ones"], accum=(f"pb{bank2}", False, True), tag="vbias")
                    P.op("scalar", lambda sx, sr=sr, oh=oh, bank2=bank2: sx.activation(
                            vpb[:, sr, oh * 8:(oh + 1) * 8, 0:64], pb[bank2][:, :], AF.Copy),
                         reads=[f"pb{bank2}"], writes=[("vpb", sr, oh)], tag="vev")
                for h in range(16):
                    bank = 4 + (h % 4)
                    colof = (h // 4) * 128
                    P.op("tensor", lambda te, h=h, sr=sr, bank=bank, colof=colof, s4=s4: te.matmul(
                            pb[bank][0:64, colof:colof + 65], kpb[:, sr, h * 64:(h + 1) * 64], vpb[:, sr, h, :],
                            start=(s4 == 0 and h // 4 == 0), stop=(s4 == 3 and h // 4 == 3),
                            skip_group_check=True),
                         reads=[("kpb", sr), ("vpb", sr, h // 8), "vpones"],
                         accum=(f"pb{bank}", s4 == 0 and h // 4 == 0, s4 == 3 and h // 4 == 3), tag="kvmm")
            for h in range(16):
                bank = 4 + (h % 4)
                colof = (h // 4) * 128
                P.op("vector", lambda v, h=h, bank=bank, colof=colof: v.scalar_tensor_tensor(
                        kvsb[:, h, :], pb[bank][0:64, colof:colof + 65], 0.0, kvsb[:, h, :], ALU.max, ALU.add),
                     reads=[f"pb{bank}", "kvsb"], writes=["kvsb"], tag="kvacc")
            for dc in range(8):
                P.op("scalar", lambda sx, dc=dc: sx.activation(ph1[:, :], xn1[:, dc, :], AF.Square),
                     reads=[("xn1", dc)], writes=["ph1"], tag="lsq")
                P.op("scalar", lambda sx: sx.activation(ph2[:, :], ph1[:, :], AF.Ln, bias=CC("eps2"), scale=1.0),
                     reads=["ph1", "consts"], writes=["ph2"], tag="lln")
                P.op("vector", lambda v: v.tensor_scalar(ph5[:, :], ph2[:, :], 0.5 / LN2, -0.5, ALU.mult, ALU.add),
                     reads=["ph2"], writes=["ph5"], tag="lei")
                P.op("scalar", lambda sx: sx.activation(ph6[:, :], ph5[:, :], AF.Relu, bias=CC("six"), scale=1.0),
                     reads=["ph5", "consts"], writes=["ph6"], tag="lrelu")
                P.op("scalar", lambda sx: sx.activation(ph3[:, :], ph6[:, :], AF.Exp, bias=CC("qexp"), scale=LN2),
                     reads=["ph6", "consts"], writes=["ph3"], tag="lq")
                P.op("scalar", lambda sx: sx.activation(ph7[:, :], ph6[:, :], AF.Exp, bias=CC("rexp"), scale=-LN2),
                     reads=["ph6", "consts"], writes=["ph7"], tag="lr")
                P.op("gpsimd", lambda g, dc=dc: g.tensor_tensor(ph2[:, :], xn1[:, dc, :], ph7[:, :], ALU.mult),
                     reads=[("xn1", dc), "ph7"], writes=["ph2"], tag="lu")
                P.op("vector", lambda v: v.tensor_scalar(ph5[:, :], ph2[:, :], 0.0, None, ALU.add),
                     reads=["ph2"], writes=["ph5"], tag="lk")
                P.op("vector", lambda v: v.tensor_scalar(ph6[:, :], ph5[:, :], 0.0, None, ALU.add),
                     reads=["ph5"], writes=["ph6"], tag="lkf")
                P.op("gpsimd", lambda g, dc=dc: g.tensor_tensor(xqt[:, dc, :], ph6[:, :], ph3[:, :], ALU.mult),
                     reads=["ph6", "ph3"], writes=[("xqt", dc)], tag="lxq")
            for oc in range(8):
                bank = oc % 2
                for dc in range(8):
                    P.op("tensor", lambda te, oc=oc, dc=dc, bank=bank: te.matmul(
                            pb[bank][:, :], wl1T[:, dc, oc * 128:(oc + 1) * 128], xqt[:, dc, :],
                            start=(dc == 0), stop=(dc == 7)),
                         reads=[("xqt", dc), ("w1", "wl")], accum=(f"pb{bank}", dc == 0, dc == 7), tag="f1mm")
                P.op("scalar", lambda sx, oc=oc, bank=bank: sx.activation(
                        ffnt[:, oc % 2, :], pb[bank][:, :], AF.Identity, bias=bl1_t[:, oc:oc + 1], scale=1.0),
                     reads=[f"pb{bank}", "bl1t"], writes=[("ffnt", oc % 2)], tag="f1ev")
                eng = "vector" if oc % 2 == 0 else "gpsimd"
                P.op(eng, lambda e, oc=oc: e.tensor_tensor(x1rd[:, oc, :], x1rd[:, oc, :], ffnt[:, oc % 2, :], ALU.add),
                     reads=["x1rd", ("ffnt", oc % 2)], writes=["x1rd"], tag="f1add")
            P.dma("x1ps", x1p_d[:, :, c0:c0 + TT].rearrange("c p t -> p c t"), x1rd[:, :, :],
                  reads=["x1rd"], writes=[("x1pd", it)], tag="x1pst")
        P.dma("kvcin", kvin_d[:, :], kvsb[:, :, :].rearrange("p h e -> p (h e)"),
              reads=["kvsb"], writes=["kvin"], tag="kvin")
        P.op("gpsimd", lambda g: g.collective_compute(
                "AllReduce", ALU.add, replica_groups=[[0, 1], [2, 3], [4, 5], [6, 7]],
                ins=[dram["kvin"]], outs=[dram["kvout"]]),
             reads=["kvin"], writes=["kvout"], axis="coll", tag="allreduce")

    if st >= 4:
        P.fence("D->E")
        kvo3 = dram["kvout"].rearrange("p (h e) -> p h e", e=65)
        P.op("vector", lambda v: v.memset(kvbd[:, :, :], 0.0),
             writes=[("kvbd", h) for h in range(16)], tag="kvbd0")
        P.op("vector", lambda v: v.memset(kvzd[:, :, :], 0.0),
             writes=[("kvzd", h) for h in range(16)], tag="kvzd0")
        for h in range(16):
            P.dma("kvcout", kvbd[(h % 2) * 64:(h % 2) * 64 + 64, h // 2, :][:, 0:64],
                  kvo3[:, h, 0:64], queue="gpsimd", reads=["kvout"], writes=[("kvbd", h)], tag="kvbd")
            P.dma("kvcout", kvzd[(h % 2) * 64:(h % 2) * 64 + 64, h // 2, (h % 2):(h % 2) + 1],
                  kvo3[:, h, 64:65], queue="gpsimd", reads=["kvout"], writes=[("kvzd", h)],
                  tag="kvzd", slow_ok=True)
        P.dma("einit", bo_tE[:, :], bo_t_d, writes=["boE"])
        P.dma("einit", blk2[:, :], blk2_d, writes=["blk2"])
        P.dma("einitg", woT[:, :, :], woT_d.rearrange("(c p) t -> p c t", p=128),
              queue="gpsimd", writes=[("w1", "wo")])
        for it in range(NT_O):
            c0 = it * TT
            P.dma("x1pr", x1prd[:, :, :], x1p_d[:, :, c0:c0 + TT].rearrange("c p t -> p c t"),
                  reads=[("x1pd", it)], writes=["x1prd"], tag="x1pld")
            for m in range(8):
                bank = m % 2
                P.op("tensor", lambda te, m=m, bank=bank, c0=c0: te.matmul(
                        pb[bank][:, :], kvbd[:, m, :], qp[:, m, c0:c0 + TT], start=True, stop=True),
                     reads=[("kvbd", 2 * m), ("kvbd", 2 * m + 1), ("qp", m, it)], writes=[f"pb{bank}"], tag="apre")
                P.op("tensor", lambda te, m=m, c0=c0: te.matmul(
                        pb[2][0:2, :], kvzd[:, m, :], qp[:, m, c0:c0 + TT], start=True, stop=True),
                     reads=[("kvzd", 2 * m), ("kvzd", 2 * m + 1), ("qp", m, it)], writes=["pb2"], tag="anrm")
                P.op("vector", lambda v: v.tensor_scalar(rbe[0:2, :], pb[2][0:2, :], 1e-6, None, ALU.add),
                     reads=["pb2"], writes=[("rbe", 0)], tag="eps")
                P.op("vector", lambda v: v.reciprocal(rbef[0:2, :], rbe[0:2, :]),
                     reads=[("rbe", 0)], writes=[("rbe", 1)])
                P.op("vector", lambda v: v.tensor_copy(nrb[:, :], rbef[0:2, :]),
                     reads=[("rbe", 1)], writes=["nrb"])
                P.op("tensor", lambda te: te.matmul(pb[3][:, :], blk2[:, :], nrb[:, :], start=True, stop=True),
                     reads=["nrb", "blk2"], writes=["pb3"], tag="ebc")
                P.op("scalar", lambda sx: sx.activation(rbef[:, :], pb[3][:, :], AF.Copy),
                     reads=["pb3"], writes=["rbef"], tag="ebce")
                P.op("vector", lambda v, m=m, bank=bank: v.tensor_tensor(
                        nrm[:, m, :], pb[bank][:, :], rbef[:, :], ALU.mult),
                     reads=[f"pb{bank}", "rbef"], writes=[("nrm", m)], tag="enorm")
            for do in range(8):
                bank = 4 + do % 2
                for oc in range(8):
                    P.op("tensor", lambda te, do=do, oc=oc, bank=bank: te.matmul(
                            pb[bank][:, :], woT[:, oc, do * 128:(do + 1) * 128], nrm[:, oc, :],
                            start=(oc == 0), stop=(oc == 7)),
                         reads=[("nrm", oc), ("w1", "wo")], accum=(f"pb{bank}", oc == 0, oc == 7), tag="womm2")
                P.op("scalar", lambda sx, do=do, bank=bank: sx.activation(
                        aev[:, do % 2, :], pb[bank][:, :], AF.Identity, bias=bo_tE[:, do:do + 1], scale=1.0),
                     reads=[f"pb{bank}", "boE"], writes=[("aev", do % 2)], tag="woev")
                P.op("vector", lambda v, do=do: v.tensor_tensor(
                        outb[:, do % 2, :], x1prd[:, do, :], aev[:, do % 2, :], ALU.add),
                     reads=["x1prd", ("aev", do % 2)], writes=[("outbf", do % 2)], tag="finadd")
                P.dma(f"yst{do % 2}", y_d[do, :, c0:c0 + TT], outb[:, do % 2, :],
                      reads=[("outbf", do % 2)], writes=[("out", "y", it, do)], tag="yst")

    P.barrier("sync", [k for k in P.last_writer if isinstance(k, tuple) and k[0] == "out"])
    em = P.finalize_and_emit(nc)
    em.close()
    stack.close()
    return nc, P, outs


def own_token_idx(p):
    return np.concatenate([np.arange((2 * i + p) * TT, (2 * i + p + 1) * TT) for i in range(8)])


def host_prep_core(I, core):
    b, p = core // 2, core % 2
    x = I["L0_attn_x"]
    xfull = np.ascontiguousarray(x[b].T)
    xown = np.ascontiguousarray(x[b].T[:, own_token_idx(p)])
    l0qkv = np.ascontiguousarray(np.concatenate(
        [I["L0_attn_Wq"], I["L0_attn_Wk"], I["L0_attn_Wv"]], axis=1))
    consts = np.zeros((128, NCONST), dtype=np.float32)
    for name, (col, val) in CONST_COLS.items():
        consts[:, col] = val
    onesbf = np.ones((128, 576), dtype=BF)
    onesf = np.ones((1, 128), dtype=np.float32)
    kk = np.arange(128)[:, None]
    tq = np.arange(TT)[None, :]
    diag = [(128 * m + kk <= tq).astype(np.float32) for m in range(4)]
    if p == 0:
        mlist = diag + [np.zeros((128, TT), np.float32)] * 4
    else:
        mlist = [np.ones((128, TT), np.float32)] * 4 + diag
    masks = np.stack(mlist, axis=1).astype(BF)
    d = dict(xfull=xfull, xown=xown, l0qkv=l0qkv, consts=consts,
             onesbf=onesbf, onesf=onesf, masks=masks,
             l0wo=np.ascontiguousarray(I["L0_attn_Wo"]),
             wfcT=np.ascontiguousarray(I["L0_ffn_W_fc"].T),
             wpjT=np.ascontiguousarray(I["L0_ffn_W_proj"].T),
             afc=np.ascontiguousarray(I["L0_ffn_A_fc"]),
             bfcT=np.ascontiguousarray(I["L0_ffn_B_fc"].T),
             apj=np.ascontiguousarray(I["L0_ffn_A_proj"]),
             bpjT=np.ascontiguousarray(I["L0_ffn_B_proj"].T),
             bfc_t=np.ascontiguousarray(I["L0_ffn_b_fc"].reshape(32, 128).T),
             bpj_t=np.ascontiguousarray(I["L0_ffn_b_proj"].reshape(8, 128).T),
             wqT=np.ascontiguousarray(I["L1_attn_Wq"].T),
             wkT=np.ascontiguousarray(I["L1_attn_Wk"].T),
             wvT=np.ascontiguousarray(I["L1_attn_Wv"].T),
             woT=np.ascontiguousarray(I["L1_attn_Wo"].T),
             bq_t=np.ascontiguousarray(I["L1_attn_bq"].reshape(8, 128).T),
             bkv=np.ascontiguousarray(np.stack([I["L1_attn_bk"], I["L1_attn_bv"]])),
             bo_t=np.ascontiguousarray(I["L1_attn_bo"].reshape(8, 128).T),
             wl1T=np.ascontiguousarray(I["L1_ffn_weight"].T),
             bl1_t=np.ascontiguousarray(I["L1_ffn_bias"].reshape(8, 128).T),
             blk2=np.concatenate([np.repeat([[1, 0]], 64, 1).reshape(1, 128),
                                  np.repeat([[0, 1]], 64, 1).reshape(1, 128)]).astype(BF))
    return d


_BUILD_CACHE = {}


def kernel(**inputs):
    from concourse.bass_utils import run_bass_kernel_spmd
    if "nc" not in _BUILD_CACHE:
        nc, P, _ = build(stage="E")
        _BUILD_CACHE["nc"] = nc
    nc = _BUILD_CACHE["nc"]
    in_maps = [host_prep_core(inputs, c) for c in range(8)]
    res = run_bass_kernel_spmd(nc, in_maps, core_ids=list(range(8)))
    out = np.zeros((4, 8192, 1024), dtype=np.float32)
    for c in range(8):
        b, p = c // 2, c % 2
        yc = np.asarray(res.results[c]["y"]).reshape(1024, 4096)
        out[b, own_token_idx(p), :] = yc.T
    return out


# version 02
# speedup vs baseline: 1.0007x; 1.0007x over previous
COMPUTE_ENGINES = ("tensor", "vector", "scalar", "gpsimd")


class Op:
    __slots__ = ("engine", "axis", "fn", "reads", "writes", "inc", "tag",
                 "dep_ops", "idx", "waits", "signaled", "cum", "G", "C")

    def __init__(self, engine, axis, fn, reads, writes, inc, tag):
        self.engine = engine
        self.axis = axis
        self.fn = fn
        self.reads = tuple(reads)
        self.writes = tuple(writes)
        self.inc = inc
        self.tag = tag
        self.waits = []
        self.signaled = False


class Prog:
    def __init__(self):
        self.ops = []
        self.last_writer = {}
        self.readers = {}
        self.sinks = set()
        self.last_fence = None

    def op(self, engine, fn, reads=(), writes=(), accum=None, tag="", axis=None, inc=1):
        o = Op(engine, axis or engine, fn, reads, writes, inc, tag)
        o.idx = len(self.ops)
        self.ops.append(o)
        deps = []
        for k in o.reads:
            w = self.last_writer.get(k)
            if w is not None:
                deps.append(w)
        for k in o.writes:
            w = self.last_writer.get(k)
            if w is not None:
                deps.append(w)
            for r in self.readers.get(k, ()):
                deps.append(r)
        if self.last_fence is not None and self.last_fence is not o:
            deps.append(self.last_fence)
        if accum is not None:
            key, is_start, is_stop = accum
            if is_start:
                w = self.last_writer.get(key)
                if w is not None:
                    deps.append(w)
                for r in self.readers.get(key, ()):
                    deps.append(r)
            if is_stop:
                self.last_writer[key] = o
                self.readers[key] = []
        seen = set()
        o.dep_ops = []
        for d in deps:
            if d is o or id(d) in seen:
                continue
            seen.add(id(d))
            o.dep_ops.append(d)
        for k in o.reads:
            self.readers.setdefault(k, []).append(o)
        for k in o.writes:
            self.last_writer[k] = o
            self.readers[k] = []
        return o

    def dma(self, family, out, in_, reads=(), writes=(), queue="sync", tag="", slow_ok=False):
        return self.op(queue, lambda eng: eng.dma_start(out=out, in_=in_,
                                                        allow_slow_non_contiguous=slow_ok),
                       reads=reads, writes=writes, tag=tag, axis=family, inc=16)

    def barrier(self, engine, keys, tag="barrier"):
        return self.op(engine, None, reads=keys, tag=tag)

    def fence(self, tag="fence"):
        keys = list(self.last_writer.keys())
        f = self.op("vector", lambda v: v.engine_nop(),
                    reads=keys, writes=keys, tag=tag)
        self.last_fence = f
        return f

    def finalize_and_emit(self, nc):
        for o in self.ops:
            if o.inc == 16 or o.axis == "coll":
                o.signaled = True
            for d in o.dep_ops:
                d.signaled = True
        axis_cum = {}
        for o in self.ops:
            if o.signaled:
                axis_cum[o.axis] = axis_cum.get(o.axis, 0) + o.inc
                o.cum = axis_cum[o.axis]
        def meet(a, b):
            for k, v in b.items():
                if a.get(k, 0) < v:
                    a[k] = v
        eng_G = {}
        for o in self.ops:
            need = {}
            for d in o.dep_ops:
                assert d.idx < o.idx
                meet(need, d.C)
            G_prev = eng_G.get(o.engine, {})
            waits = {}
            for ax, v in need.items():
                if ax == o.engine and o.engine != "sync" and o.axis == o.engine:
                    pass
                if G_prev.get(ax, 0) >= v:
                    continue
                waits[ax] = v
            o.waits = sorted(waits.items())
            G = dict(G_prev)
            meet(G, waits)
            eng_G[o.engine] = G
            C = dict(G)
            if o.signaled:
                v = axis_cum[o.axis] if o.axis in self.sinks else o.cum
                C[o.axis] = max(C.get(o.axis, 0), v)
            o.C = C
        engines = {}
        for o in self.ops:
            engines.setdefault(o.engine, []).append(o)
        self.stats = {e: dict(n_ops=len(v), n_sig=sum(1 for x in v if x.signaled),
                              n_waits=sum(len(x.waits) for x in v))
                      for e, v in engines.items()}
        self.stats["n_axes"] = len(axis_cum)

        import contextlib
        stack = contextlib.ExitStack()
        axes = sorted(axis_cum)
        sems = {a: stack.enter_context(nc.semaphore(f"sem_{a}")) for a in axes}
        blk = stack.enter_context(nc.Block())

        def make_body(eops):
            def body(eng):
                for o in eops:
                    for (ax, v) in o.waits:
                        eng.wait_ge(sems[ax], v)
                    if o.fn is not None:
                        ins = o.fn(eng)
                        if o.signaled:
                            ins.then_inc(sems[o.axis], o.inc)
                    else:
                        assert not o.signaled
            return body

        for ename, eops in engines.items():
            getattr(blk, ename)(make_body(eops))
        return stack


"""Megakernel builder (shared by dev runs and kernel.py).

One uniform program across 8 cores; per-core differences live in input data.
Feature-major streams: [d_model 128-chunks on partitions x tokens on free].

Phases:
  A: LN stats + K/V (full batch elem) -> HBM scratch; Q (own tokens); stats rows -> dram.
  B: causal flash attention (shifted-exp softmax, mask-mult) -> O_norm;
     stream+fake-quant W_fc/W_proj with LoRA/scale folded -> Wtot bf16 (resident).
  C: L0 FFN (gelu+proj) + Wo(O_norm) + x residual -> x1 -> HBM scratch.
  D: L1 LN + QKV (+phi) + KV/Z partials -> pairwise AllReduce; L1 FFN (e4m3)
     folded into x1 -> x1' scratch.
  E: (Qp@KV_aug / norm) @ Wo + bo + x1' -> y.
"""
import numpy as np
import ml_dtypes
import concourse.bass as bass
import concourse.mybir as mybir

F32 = mybir.dt.float32
FP16 = mybir.dt.float16
BF16 = mybir.dt.bfloat16
I32 = mybir.dt.int32
AF = mybir.ActivationFunctionType
ALU = mybir.AluOpType
AX = mybir.AxisListType.X
BF = ml_dtypes.bfloat16

D = 1024
T = 8192
TOWN = 4096
NT_G = 16
NT_O = 8
TT = 512
TTC = 256
NT_C = TOWN // TTC
DK = 128
LN2 = float(np.log(2.0))
SM_SCALE = float(128 ** -0.5)
KB = 1024

CONST_COLS = {
    "neg16": (0, -16.0),
    "eps_ln": (1, 1e-5),
    "qexp": (2, -9.0 * LN2),
    "rexp": (3, 9.0 * LN2),
    "six": (4, 6.0),
    "zero": (5, 0.0),
    "eps2": (6, 2.0 ** -24),
}
NCONST = 8

STAGES = ["A", "B", "C", "D", "E"]


def build(stage="E", dbg_stage=None):
    st = STAGES.index(stage)
    nc = bass.Bass("TRN2", target_bir_lowering=False)
    P = Prog()
    P.sinks = {"init", "dbg", "x1pst", "kvcin", "kvcout", "cinit", "dinit", "dinitg", "einit", "einitg",
               "winit", "einit", "mxstfc", "mxldfc", "mxstpj", "mxldpj"}

    dram = {}

    def din(name, shape, dt=F32):
        dram[name] = nc.dram_tensor(name, shape, dt, kind="ExternalInput").ap()
        return dram[name]

    def dint(name, shape, dt=F32):
        dram[name] = nc.dram_tensor(name, shape, dt).ap()
        return dram[name]

    def dout(name, shape, dt=F32):
        dram[name] = nc.dram_tensor(name, shape, dt, kind="ExternalOutput").ap()
        return dram[name]

    xfull = din("xfull", [D, T])
    xown = din("xown", [D, TOWN])
    l0qkv_d = din("l0qkv", [D, 384])
    l0wo_d = din("l0wo", [DK, D])
    consts_d = din("consts", [128, NCONST])
    onesbf_d = din("onesbf", [128, 576], BF16)
    onesf_d = din("onesf", [1, 128])
    masks_d = din("masks", [128, 8, TT], BF16)
    wfcT_d = din("wfcT", [D, 4096])
    wpjT_d = din("wpjT", [4096, D])
    afc_d = din("afc", [16, D])
    bfcT_d = din("bfcT", [16, 4096])
    apj_d = din("apj", [16, 4096])
    bpjT_d = din("bpjT", [16, D])
    bfc_t_d = din("bfc_t", [128, 32])
    bpj_t_d = din("bpj_t", [128, 8])
    wqT_d = din("wqT", [D, D])
    wkT_d = din("wkT", [D, D])
    wvT_d = din("wvT", [D, D])
    woT_d = din("woT", [D, D])
    bq_t_d = din("bq_t", [128, 8])
    bkv_d = din("bkv", [2, D])
    bo_t_d = din("bo_t", [128, 8])
    wl1T_d = din("wl1T", [D, D])
    bl1_t_d = din("bl1_t", [128, 8])
    blk2_d = din("blk2", [2, 128], BF16)

    kd = dint("kd", [64, 128, 128], BF16)
    vd = dint("vd", [64, 128, 128], BF16)
    mr_d = dint("mr_d", [2, NT_O, TT])
    mx_d = dint("mx_d", [2, 128])
    x1_d = dint("x1_d", [8, 128, TOWN], BF16)
    x1p_d = dint("x1p_d", [8, 128, TOWN], BF16)
    kvin_d = dint("kvin", [64, 1040])
    kvout_d = dint("kvout", [64, 1040])
    y_d = dout("y", [8, 128, TOWN])

    base0 = (nc.sbuf_base + 31) & ~31
    pad = base0 - nc.sbuf_base
    TOTAL = 207 * KB + 256
    nc.alloc_sbuf_tensor("arena", [128, TOTAL + pad], mybir.dt.uint8)

    def place(name, shape, dt, off):
        assert off % 32 == 0, (name, off)
        sz = int(np.prod(shape[1:])) * mybir.dt.size(dt)
        assert off + sz <= TOTAL, (name, off, sz, TOTAL)
        return nc.alloc_sbuf_tensor_at(name, shape, dt, offset=base0 + off)

    sc = place("sc", [128, NCONST], F32, 0)
    ones = place("ones", [128, 576], BF16, 64)
    wo_sb = place("wo_sb", [128, 8, 128], BF16, 1216)
    onesf_sb = place("onesf_sb", [1, 128], F32, 3264)
    BIGO = 3808

    xt = place("xt", [128, 2, 8, TT], F32, BIGO + 0)
    xb = place("xb", [128, 2, 8, TT], BF16, BIGO + 32 * KB)
    sq = place("sq", [128, 2, 8, TT], BF16, BIGO + 48 * KB)
    xn = place("xn", [128, 2, 8, TT], BF16, BIGO + 64 * KB)
    mb = place("mb", [128, 2, TT], BF16, BIGO + 80 * KB)
    rb = place("rb", [128, 2, TT], BF16, BIGO + 82 * KB)
    kevac = place("kevac", [128, 2, TT], BF16, BIGO + 84 * KB)
    vevac = place("vevac", [128, 2, 4, 128], BF16, BIGO + 86 * KB)
    l0w = place("l0w", [128, 8, 384], BF16, BIGO + 88 * KB)
    l0wf = place("l0wf", [128, 8, 384], F32, BIGO + 94 * KB)

    wtfc = place("wtfc", [128, 8, 4096], FP16, BIGO + 0)
    wtpj = place("wtpj", [128, 32, 1024], FP16, BIGO + 64 * KB)

    qp = place("qp", [128, 8, TOWN], BF16, BIGO + 0)
    wqT = place("wqT", [128, 8, D], BF16, BIGO + 64 * KB)
    wkT = place("wkT", [128, 8, D], BF16, BIGO + 80 * KB)
    wvT = place("wvT", [128, 8, D], BF16, BIGO + 96 * KB)
    wl1T = place("wl1T", [128, 8, D], BF16, BIGO + 112 * KB)
    woT = place("woT", [128, 8, D], BF16, BIGO + 64 * KB)
    wstgD = place("wstgD", [128, 4, D], F32, BIGO + 96 * KB)

    SCR = BIGO + 128 * KB
    rowt = place("rowt", [1, 4, TT], F32, SCR + 0)
    rowb = place("rowb", [1, 4, TT], BF16, SCR + 8 * KB)
    masks = place("masks", [128, 8, TT], BF16, SCR + 12 * KB)
    q_sb = place("q_sb", [128, NT_O, TT], BF16, SCR + 20 * KB)
    onorm = place("onorm", [128, NT_O, TT], BF16, SCR + 28 * KB)
    WK = SCR + 36 * KB
    pt = place("pt", [128, 4, TT], BF16, WK + 0)
    ksb = place("ksb", [128, 8, 128], BF16, WK + 4 * KB)
    vsb = place("vsb", [128, 8, 128], BF16, WK + 6 * KB)
    rbb = place("rbb", [128, TT], F32, WK + 8 * KB)
    wstg = place("wstg", [128, 2, D], F32, WK + 10 * KB)
    lt_z = place("lt_z", [128, 2, TT], I32, WK + 18 * KB)
    lt_f = place("lt_f", [128, 2, TT], F32, WK + 22 * KB)
    lt_m = place("lt_m", [128, 1, TT], F32, WK + 26 * KB)
    abm = place("abm", [16, 4096], BF16, WK + 28 * KB)
    abm2 = place("abm2", [16, D], BF16, WK + 36 * KB)
    mxc = place("mxc", [128, 2], F32, WK + 38 * KB)
    sclb = place("sclb", [128, 8], F32, WK + 38 * KB + 32)
    hsb = place("hsb", [128, 32, TTC], FP16, WK + 0)
    x1t = place("x1t", [128, 2, 8, TTC], BF16, WK + 16 * KB)
    mbc = place("mbc", [128, 2, TTC], FP16, WK + 24 * KB)
    rbc = place("rbc", [128, 2, TTC], FP16, WK + 25 * KB)
    bfc_t = place("bfc_t", [128, 32], F32, WK + 26 * KB)
    bpj_t = place("bpj_t", [128, 8], F32, WK + 26 * KB + 160)
    bstg = place("bstg", [128, 64], F32, WK + 26 * KB + 224)
    xbt = place("xbt", [128, 2, 8, TTC], FP16, SCR + 12 * KB)
    xrb = place("xrb", [128, 8, TTC], F32, SCR + 20 * KB)
    DW = SCR + 12 * KB
    x1rd = place("x1rd", [128, 8, TT], BF16, DW + 0)
    xn1 = place("xn1", [128, 8, TT], BF16, DW + 8 * KB)
    xqt = place("xqt", [128, 8, TT], BF16, DW + 16 * KB)
    mbd = place("mbd", [128, TT], BF16, DW + 24 * KB)
    rbd = place("rbd", [128, TT], BF16, DW + 25 * KB)
    kpb = place("kpb", [128, 2, D], BF16, DW + 26 * KB)
    vpb = place("vpb", [128, 2, 16, 65], BF16, DW + 30 * KB)
    kvsb = place("kvsb", [64, 16, 65], F32, DW + 34 * KB + 96)
    ffnt = place("ffnt", [128, 2, TT], BF16, DW + 38 * KB + 192)
    bq_t = place("bq_t", [128, 8], F32, DW + 40 * KB + 192)
    bo_t = place("bo_t", [128, 8], F32, DW + 40 * KB + 224)
    bl1_t = place("bl1_t", [128, 8], F32, DW + 40 * KB + 256)
    bkvb = place("bkvb", [1, 2, D], BF16, DW + 40 * KB + 288)
    ph1 = place("ph1", [128, 2, TT], F32, DW + 44 * KB + 320)
    ph2 = place("ph2", [128, 2, TT], F32, DW + 48 * KB + 320)
    ph5 = place("ph5", [128, 2, TT], I32, DW + 52 * KB + 320)
    ph3 = place("ph3", [128, 2, TT], BF16, DW + 56 * KB + 320)
    ph6 = place("ph6", [128, 2, TT], BF16, DW + 58 * KB + 320)
    ph7 = place("ph7", [128, 2, TT], BF16, DW + 60 * KB + 320)
    nrm = place("nrm", [128, 8, TT], BF16, DW + 0)
    x1prd = place("x1prd", [128, 8, TT], BF16, DW + 8 * KB)
    outb = place("outb", [128, 2, TT], F32, DW + 16 * KB)
    kvf = place("kvf", [64, 16, 65], F32, DW + 20 * KB)
    kvbb = place("kvbb", [64, 16, 65], BF16, DW + 24 * KB + 96)
    rbe = place("rbe", [64, TT], F32, DW + 26 * KB + 256)
    aev = place("aev", [128, 2, TT], BF16, DW + 28 * KB + 256)
    bo_tE = place("bo_tE", [128, 8], F32, DW + 30 * KB + 256)
    kvbd = place("kvbd", [128, 8, 128], BF16, DW + 31 * KB + 256)
    kvzd = place("kvzd", [128, 8, 2], BF16, DW + 33 * KB + 256)
    blk2 = place("blk2", [2, 128], BF16, DW + 33 * KB + 320)
    nrb = place("nrb", [2, TT], BF16, DW + 34 * KB)
    rbef = place("rbef", [128, TT], F32, DW + 35 * KB + 32)

    import contextlib
    stack = contextlib.ExitStack()
    PSt = lambda *a: stack.enter_context(nc.psum_tensor(*a))
    pb = [PSt(f"pb{i}", [128, TT], F32) for i in range(8)]

    CC = lambda n: sc[:, CONST_COLS[n][0]:CONST_COLS[n][0] + 1]
    CCr = lambda n: sc[0:1, CONST_COLS[n][0]:CONST_COLS[n][0] + 1]
    ONE_COL = ones[:, 0:1]
    ONE_ROW = ones[0:1, 0:128]

    P.dma("init", sc[:], consts_d, writes=["consts"])
    P.dma("init", ones[:], onesbf_d, writes=["ones"])
    P.dma("init", onesf_sb[:], onesf_d, writes=["onesf"])
    P.dma("init", masks[:], masks_d, writes=["masks"])
    for dc in range(8):
        P.dma("init", l0wf[:, dc, :], l0qkv_d[dc * 128:(dc + 1) * 128, :], writes=[("l0wf", dc)])
    for dc in range(8):
        P.op("vector", lambda v, dc=dc: v.tensor_copy(l0w[:, dc, :], l0wf[:, dc, :]),
             reads=[("l0wf", dc)], writes=[("l0w", dc)])

    def ln_tile(src_ap, rot, save=None):
        P.dma(f"x{rot}", xt[:, rot, :, :], src_ap, writes=[("xt", rot)], tag="ldx")
        for dc in range(8):
            P.op("vector", lambda v, dc=dc, rot=rot: v.tensor_copy(xb[:, rot, dc, :], xt[:, rot, dc, :]),
                 reads=[("xt", rot)], writes=[("xb", rot, dc)], tag="cast")
            P.op("gpsimd", lambda g, dc=dc, rot=rot: g.tensor_tensor(sq[:, rot, dc, :], xb[:, rot, dc, :], xb[:, rot, dc, :], ALU.mult),
                 reads=[("xb", rot, dc)], writes=[("sq", rot, dc)], tag="sq")
        for dc in range(8):
            P.op("tensor", lambda te, dc=dc, rot=rot: te.matmul(pb[0][0:1, :], ONE_COL, xb[:, rot, dc, :],
                                                                start=(dc == 0), stop=(dc == 7)),
                 reads=[("xb", rot, dc), "ones"], accum=("pb0", dc == 0, dc == 7), tag="summ")
        for dc in range(8):
            P.op("tensor", lambda te, dc=dc, rot=rot: te.matmul(pb[1][0:1, :], ONE_COL, sq[:, rot, dc, :],
                                                                start=(dc == 0), stop=(dc == 7)),
                 reads=[("sq", rot, dc), "ones"], accum=("pb1", dc == 0, dc == 7), tag="sumsq")
        m_ap, var_ap, s_ap, r_ap = (rowt[0:1, i, :] for i in range(4))
        m_bf, r_bf = rowb[0:1, 0, :], rowb[0:1, 1, :]
        P.op("vector", lambda v: v.tensor_scalar(m_ap, pb[0][0:1, :], 1.0 / D, None, ALU.mult),
             reads=["pb0"], writes=[("rowt", 0)])
        P.op("vector", lambda v: v.tensor_tensor(var_ap, m_ap, m_ap, ALU.mult),
             reads=[("rowt", 0)], writes=[("rowt", 1)])
        P.op("vector", lambda v: v.scalar_tensor_tensor(var_ap, pb[1][0:1, :], 1.0 / D, var_ap, ALU.mult, ALU.subtract),
             reads=["pb1", ("rowt", 1)], writes=[("rowt", 1)])
        P.op("scalar", lambda s: s.activation(s_ap, var_ap, AF.Sqrt, bias=CCr("eps_ln"), scale=1.0),
             reads=[("rowt", 1), "consts"], writes=[("rowt", 2)])
        P.op("vector", lambda v: v.reciprocal(r_ap, s_ap), reads=[("rowt", 2)], writes=[("rowt", 3)])
        P.op("vector", lambda v: v.tensor_copy(m_bf, m_ap), reads=[("rowt", 0)], writes=[("rowb", 0)])
        P.op("vector", lambda v: v.tensor_copy(r_bf, r_ap), reads=[("rowt", 3)], writes=[("rowb", 1)])
        if save is not None:
            P.dma("mrstm", mr_d[0, save, :], m_ap, reads=[("rowt", 0)], writes=[("mrd", 0, save)])
            P.dma("mrstr", mr_d[1, save, :], r_ap, reads=[("rowt", 3)], writes=[("mrd", 1, save)])
        P.op("tensor", lambda te: te.matmul(pb[2][:, :], ONE_ROW, m_bf, start=True, stop=True),
             reads=[("rowb", 0), "ones"], writes=["pb2"])
        P.op("tensor", lambda te: te.matmul(pb[3][:, :], ONE_ROW, r_bf, start=True, stop=True),
             reads=[("rowb", 1), "ones"], writes=["pb3"])
        P.op("scalar", lambda s, rot=rot: s.activation(mb[:, rot, :], pb[2][:, :], AF.Copy),
             reads=["pb2"], writes=[("mb", rot)])
        P.op("scalar", lambda s, rot=rot: s.activation(rb[:, rot, :], pb[3][:, :], AF.Copy),
             reads=["pb3"], writes=[("rb", rot)])
        for dc in range(8):
            eng = "vector" if dc % 2 == 0 else "gpsimd"
            P.op(eng, lambda e, dc=dc, rot=rot: e.tensor_tensor(xn[:, rot, dc, :], xb[:, rot, dc, :], mb[:, rot, :], ALU.subtract),
                 reads=[("xb", rot, dc), ("mb", rot)], writes=[("xn", rot, dc)])
            P.op(eng, lambda e, dc=dc, rot=rot: e.tensor_tensor(xn[:, rot, dc, :], xn[:, rot, dc, :], rb[:, rot, :], ALU.mult),
                 reads=[("xn", rot, dc), ("rb", rot)], writes=[("xn", rot, dc)])

    for t in range(NT_G):
        rot = t % 2
        ln_tile(xfull.rearrange("(c p) t -> p c t", p=128)[:, :, t * TT:(t + 1) * TT], rot)
        for dc in range(8):
            P.op("tensor", lambda te, dc=dc, rot=rot: te.matmul(pb[4][:, :], l0w[:, dc, 128:256], xn[:, rot, dc, :],
                                                                start=(dc == 0), stop=(dc == 7)),
                 reads=[("xn", rot, dc), ("l0w", dc)], accum=("pb4", dc == 0, dc == 7), tag="kmm")
        P.op("scalar", lambda s, rot=rot: s.activation(kevac[:, rot, :], pb[4][:, :], AF.Copy),
             reads=["pb4"], writes=[("kevac", rot)])
        P.dma(f"kst{rot}", kd[4 * t:4 * t + 4, :, :].rearrange("j p t -> p j t"), kevac[:, rot, :],
              reads=[("kevac", rot)], writes=[("kdt", t)])
        for s4 in range(4):
            bank = 5 + (s4 % 2)
            for dc in range(8):
                P.op("tensor", lambda te, dc=dc, s4=s4, bank=bank, rot=rot: te.matmul(
                        pb[bank][:, 0:128], xn[:, rot, dc, s4 * 128:(s4 + 1) * 128], l0w[:, dc, 256:384],
                        start=(dc == 0), stop=(dc == 7)),
                     reads=[("xn", rot, dc), ("l0w", dc)], accum=(f"pb{bank}", dc == 0, dc == 7), tag="vmm")
            P.op("scalar", lambda s, bank=bank, s4=s4, rot=rot: s.activation(vevac[:, rot, s4, :], pb[bank][:, 0:128], AF.Copy),
                 reads=[f"pb{bank}"], writes=[("vevac", rot, s4)])
        P.dma(f"vst{rot}", vd[4 * t:4 * t + 4, :, :].rearrange("j p t -> p j t"), vevac[:, rot, :, :],
              reads=[("vevac", rot, s4) for s4 in range(4)], writes=[("vdt", t)])
    for t in range(NT_O):
        rot = t % 2
        ln_tile(xown.rearrange("(c p) t -> p c t", p=128)[:, :, t * TT:(t + 1) * TT], rot, save=t)
        for dc in range(8):
            P.op("tensor", lambda te, dc=dc, rot=rot: te.matmul(pb[7][:, :], l0w[:, dc, 0:128], xn[:, rot, dc, :],
                                                                start=(dc == 0), stop=(dc == 7)),
                 reads=[("xn", rot, dc), ("l0w", dc)], accum=("pb7", dc == 0, dc == 7), tag="qmm")
        P.op("scalar", lambda s, t=t: s.activation(q_sb[:, t, :], pb[7][:, :], AF.Copy),
             reads=["pb7"], writes=[("q", t)])

    outs = []
    if dbg_stage == "A":
        q_out = dout("q_out", [128, TOWN], BF16)
        mr_out = dout("mr_out", [2, NT_O, TT])
        k_out = dout("k_out", [64, 128, 128], BF16)
        v_out = dout("v_out", [64, 128, 128], BF16)
        for t in range(NT_O):
            P.dma("dbg", q_out[:, t * TT:(t + 1) * TT], q_sb[:, t, :],
                  reads=[("q", t)], writes=[("out", "q", t)])
        P.dma("dbg", mr_out[:, :, :], mr_d[:, :, :],
              reads=[("mrd", i, t) for i in range(2) for t in range(NT_O)], writes=[("out", "mr")])
        P.dma("dbg", k_out[:, :, :], kd[:, :, :], reads=[("kdt", t) for t in range(NT_G)], writes=[("out", "k")])
        P.dma("dbg", v_out[:, :, :], vd[:, :, :], reads=[("vdt", t) for t in range(NT_G)], writes=[("out", "v")])
        outs = ["q_out", "mr_out", "k_out", "v_out"]

    if st >= 1:
        P.fence("A->B")
        P.dma("winit", l0wf[:, 0, 0:256].rearrange("p (c t) -> p c t", c=2), None, writes=[]) if False else None
        wof = l0wf
        P.dma("winit", wof[:, 0, 0:128], l0wo_d[:, 0:1024].rearrange("p (c t) -> p c t", c=8)[:, 0, :][:, 0:128], writes=[]) if False else None
        P.dma("winit", wof[:, 0:3, :].rearrange("p c t -> p (c t)")[:, 0:1024], l0wo_d[:, :],
              writes=[("wof",)])
        P.op("vector", lambda v: v.tensor_copy(wo_sb[:, :, :].rearrange("p c t -> p (c t)"),
                                               wof[:, 0:3, :].rearrange("p c t -> p (c t)")[:, 0:1024]),
             reads=[("wof",)], writes=["wo"])

        def ld16(dstslice, key, src_ap, stg_key, fam):
            P.dma("abld", wstg[0:16, 0, :], src_ap, writes=[("wstg", 0)])
            P.op("vector", lambda v: v.tensor_copy(dstslice, wstg[0:16, 0, :]),
                 reads=[("wstg", 0)], writes=[key])

        SBANKS = [0, 1, 4, 5]

        def attn_block(i):
            nk = 8 * i + 8
            for j in range(nk):
                jr = j % 8
                P.dma(f"kld{jr}", ksb[:, jr, :], kd[j, :, :], reads=[("kdt", j // 4)],
                      writes=[("ksb", jr)], tag="kld")
                P.dma(f"vld{jr}", vsb[:, jr, :], vd[j, :, :], reads=[("vdt", j // 4)],
                      writes=[("vsb", jr)], tag="vld")
            for j in range(nk):
                jr = j % 8
                pr = j % 4
                sbank = SBANKS[j % 4]
                P.op("tensor", lambda te, i=i, jr=jr, sbank=sbank: te.matmul(
                        pb[sbank][:, :], ksb[:, jr, :], q_sb[:, i, :], start=True, stop=True),
                     reads=[("ksb", jr), ("q", i)], writes=[f"pb{sbank}"], tag="smm")
                P.op("scalar", lambda s, pr=pr, sbank=sbank: s.activation(
                        pt[:, pr, :], pb[sbank][:, :], AF.Exp, bias=CC("neg16"), scale=SM_SCALE),
                     reads=[f"pb{sbank}", "consts"], writes=[("pt", pr)], tag="exp")
                if j >= 8 * i:
                    mi = j - 8 * i
                    P.op("vector", lambda v, pr=pr, mi=mi: v.tensor_tensor(
                            pt[:, pr, :], pt[:, pr, :], masks[:, mi, :], ALU.mult),
                         reads=[("pt", pr), "masks"], writes=[("pt", pr)], tag="mask")
                P.op("tensor", lambda te, jr=jr, pr=pr, j=j, nk=nk: te.matmul(pb[6][:, :], vsb[:, jr, :], pt[:, pr, :],
                                                                       start=(j == 0), stop=(j == nk - 1)),
                     reads=[("vsb", jr), ("pt", pr)], accum=("pb6", j == 0, j == nk - 1), tag="omm")
                P.op("tensor", lambda te, pr=pr, j=j, nk=nk: te.matmul(pb[7][0:1, :], ONE_COL, pt[:, pr, :],
                                                                       start=(j == 0), stop=(j == nk - 1)),
                     reads=[("pt", pr), "ones"], accum=("pb7", j == 0, j == nk - 1), tag="dmm")
            P.op("vector", lambda v: v.reciprocal(rowt[0:1, 0, :], pb[7][0:1, :]),
                 reads=["pb7"], writes=[("rowt", 0)])
            P.op("vector", lambda v: v.tensor_copy(rowb[0:1, 2, :], rowt[0:1, 0, :]),
                 reads=[("rowt", 0)], writes=[("rowb", 2)])
            P.op("tensor", lambda te: te.matmul(pb[3][:, :], ONE_ROW, rowb[0:1, 2, :], start=True, stop=True),
                 reads=[("rowb", 2), "ones"], writes=["pb3"])
            P.op("scalar", lambda s: s.activation(rbb[:, :], pb[3][:, :], AF.Copy),
                 reads=["pb3"], writes=["rbb"])
            P.op("vector", lambda v, i=i: v.tensor_tensor(onorm[:, i, :], pb[6][:, :], rbb[:, :], ALU.mult),
                 reads=["pb6", "rbb"], writes=[("onorm", i)], tag="onorm")

        wq = []

        def wq_pass1(fam, src_d, nch, wslice):
            for ch in range(nch):
                for cc in range(4096 // D if fam == "wfc" else 1):
                    def th(ch=ch, cc=cc, fam=fam, src_d=src_d, wslice=wslice):
                        r = (ch + cc) % 2
                        P.dma(f"w{fam}{r}", wstg[:, r, :], src_d[ch * 128:(ch + 1) * 128, cc * D:(cc + 1) * D],
                              writes=[("wstg", r)], tag="wld")
                        P.op("vector", lambda v: v.tensor_reduce(mxc[:, r:r + 1], wstg[:, r, :], AX, ALU.max,
                                                                 apply_absolute_value=True),
                             reads=[("wstg", r)], writes=[("mxcol", fam, r)], tag="wmx")
                        P.op("vector", lambda v: v.tensor_tensor(sclb[:, 3:4], sclb[:, 3:4], mxc[:, r:r + 1], ALU.max),
                             reads=[("mxcol", fam, r), ("mxacc", fam)], writes=[("mxacc", fam)], tag="wmxa")
                        dst, dkey = wslice(ch, cc)
                        P.op("vector", lambda v: v.tensor_copy(dst, wstg[:, r, :]),
                             reads=[("wstg", r)], writes=[dkey], tag="wcast")
                    wq.append(th)

        def wq_scale(fam):
            def th(fam=fam):
                mslot = 0 if fam == "wfc" else 1
                P.dma("mxstfc" if fam == "wfc" else "mxstpj",
                      mx_d[mslot, :], sclb[:, 3:4], reads=[("mxacc", fam)], writes=[("mxd", fam)])
                P.dma("mxldfc" if fam == "wfc" else "mxldpj",
                      rowt[0:1, 0, 0:128], mx_d[mslot, :], reads=[("mxd", fam)], writes=[("rowt", 0)])
                P.op("vector", lambda v: v.tensor_reduce(rowt[0:1, 1, 0:1], rowt[0:1, 0, 0:128], AX, ALU.max,
                                                         apply_absolute_value=True),
                     reads=[("rowt", 0)], writes=[("rowt", 1)])
                P.op("vector", lambda v: v.tensor_scalar(rowt[0:1, 1, 1:2], rowt[0:1, 1, 0:1], 1.0 / 127.0, None, ALU.mult),
                     reads=[("rowt", 1)], writes=[("rowt", 1)])
                P.op("vector", lambda v: v.reciprocal(rowt[0:1, 1, 2:3], rowt[0:1, 1, 1:2]),
                     reads=[("rowt", 1)], writes=[("rowt", 1)])
                P.op("tensor", lambda te: te.matmul(pb[3][:, 0:2], onesf_sb[0:1, :], rowt[0:1, 1, 1:3],
                                                    start=True, stop=True),
                     reads=[("rowt", 1), ("rowt", 1), "onesf"], writes=["pb3"])
                P.op("scalar", lambda s: s.activation(sclb[:, 0:2], pb[3][:, 0:2], AF.Copy),
                     reads=["pb3"], writes=[("sclb", fam)])
                if fam == "wfc":
                    P.op("gpsimd", lambda g: g.memset(sclb[:, 3:4], 0.0),
                         writes=[("mxacc", "wfc"), ("mxacc", "wpj")])
            wq.append(th)

        def wq_pass2(fam, nch, ncc512, wslice512, a_lhs, b_rhs):
            for ch in range(nch):
                for fc in range(ncc512):
                    def th(ch=ch, fc=fc, fam=fam, wslice512=wslice512, a_lhs=a_lhs, b_rhs=b_rhs):
                        r = (ch * ncc512 + fc) % 2
                        wsl, wkey = wslice512(ch, fc)
                        P.op("tensor", lambda te: te.matmul(pb[2][:, :], a_lhs(ch), b_rhs(fc), start=True, stop=True),
                             reads=[("abm",)], writes=["pb2"], tag="lmm")
                        P.op("scalar", lambda s: s.activation(lt_m[:, 0, :], pb[2][:, :], AF.Copy, scale=2.0),
                             reads=["pb2"], writes=[("ltm",)], tag="lme")
                        P.op("vector", lambda v: v.tensor_scalar(lt_z[:, r, :], wsl, sclb[:, 1:2], None, ALU.mult),
                             reads=[wkey, ("sclb", fam)], writes=[("ltz", r)], tag="lz")
                        P.op("vector", lambda v: v.tensor_scalar(lt_f[:, r, :], lt_z[:, r, :], sclb[:, 0:1], None, ALU.mult),
                             reads=[("ltz", r), ("sclb", fam)], writes=[("ltf", r)], tag="lzs")
                        P.op("gpsimd", lambda g: g.tensor_tensor(wsl, lt_f[:, r, :], lt_m[:, 0, :], ALU.add),
                             reads=[("ltf", r), ("ltm",)], writes=[wkey], tag="lw")
                    wq.append(th)

        def fc_slice1024(ch, cc):
            return wtfc[:, ch, cc * D:(cc + 1) * D], ("wtfc", ch, cc)

        def fc_slice512(ch, fc):
            return wtfc[:, ch, fc * TT:(fc + 1) * TT], ("wtfc", ch, fc // 2)

        def pj_slice1024(ch, cc):
            return wtpj[:, ch * 4:(ch + 1) * 4, :].rearrange("p c t -> p (c t)")[:, 0:D] if False else (None, None)

        def pj_slice1024b(ch, cc):
            return wtpj[:, ch, :], ("wtpj", ch)

        def pj_slice512(ch, fc):
            return wtpj[:, ch, fc * TT:(fc + 1) * TT], ("wtpj", ch)

        def afc_lhs(ch):
            return abm2[:, ch * 128:(ch + 1) * 128]

        def bfc_rhs(fc):
            return abm[:, fc * TT:(fc + 1) * TT]

        def apj_lhs(ch):
            return abm[:, ch * 128:(ch + 1) * 128]

        def bpj_rhs(fc):
            return abm2[:, fc * TT:(fc + 1) * TT]

        def fc_mats():
            P.op("gpsimd", lambda g: g.memset(sclb[:, 3:4], 0.0), writes=[("mxacc", "wfc")])
            ld16(abm2[:, :], ("abm",), afc_d[:, :], "afc", "wfc")
            for q in range(4):
                ld16(abm[:, q * D:(q + 1) * D], ("abm",), bfcT_d[:, q * D:(q + 1) * D], f"bfcT{q}", "wfc")
        def pj_mats():
            for q in range(4):
                ld16(abm[:, q * D:(q + 1) * D], ("abm",), apj_d[:, q * D:(q + 1) * D], f"apj{q}", "wpj")
            ld16(abm2[:, :], ("abm",), bpjT_d[:, :], "bpjT", "wpj")

        wq.append(fc_mats)
        wq_pass1("wfc", wfcT_d, 8, fc_slice1024)
        wq_scale("wfc")
        wq_pass2("wfc", 8, 8, fc_slice512, afc_lhs, bfc_rhs)
        wq.append(pj_mats)
        wq_pass1("wpj", wpjT_d, 32, pj_slice1024b)
        wq_scale("wpj")
        wq_pass2("wpj", 32, 2, pj_slice512, apj_lhs, bpj_rhs)

        per = (len(wq) + NT_O - 1) // NT_O
        for i in range(NT_O):
            attn_block(i)
            for th in wq[i * per:(i + 1) * per]:
                th()

        if dbg_stage == "B":
            on_out = dout("on_out", [128, TOWN], BF16)
            wt_out = dout("wt_out", [128, 8, 4096], FP16)
            wt2_out = dout("wt2_out", [128, 32, 1024], FP16)
            for i in range(NT_O):
                P.dma("dbg", on_out[:, i * TT:(i + 1) * TT], onorm[:, i, :],
                      reads=[("onorm", i)], writes=[("out", "on", i)])
            P.dma("dbg", wt_out[:, :, :], wtfc[:, :, :],
                  reads=[("wtfc", ch, cc) for ch in range(8) for cc in range(4)], writes=[("out", "wt")])
            P.dma("dbg", wt2_out[:, :, :], wtpj[:, :, :],
                  reads=[("wtpj", ch) for ch in range(32)], writes=[("out", "wt2")])
            outs = ["on_out", "wt_out", "wt2_out"]

    if st >= 2:
        P.fence("B->C")
        P.dma("cinit", bstg[:, 0:32], bfc_t_d, writes=[("bstg", 0)])
        P.op("vector", lambda v: v.tensor_copy(bfc_t[:, :], bstg[:, 0:32]), reads=[("bstg", 0)], writes=["bfct"])
        P.dma("cinit", bstg[:, 32:40], bpj_t_d, writes=[("bstg", 1)])
        P.op("vector", lambda v: v.tensor_copy(bpj_t[:, :], bstg[:, 32:40]), reads=[("bstg", 1)], writes=["bpjt"])

        for tc in range(NT_C):
            rot = tc % 2
            t512 = tc // 2
            half = tc % 2
            cofs = half * TTC
            P.dma(f"mrldm{rot}", rowt[0:1, 2 * rot, 0:TTC], mr_d[0, t512, cofs:cofs + TTC],
                  reads=[("mrd", 0, t512)], writes=[("rowt", 2 * rot)], tag="mrld")
            P.dma(f"mrldr{rot}", rowt[0:1, 2 * rot + 1, 0:TTC], mr_d[1, t512, cofs:cofs + TTC],
                  reads=[("mrd", 1, t512)], writes=[("rowt", 2 * rot + 1)], tag="mrld")
            P.op("vector", lambda v, rot=rot: v.tensor_copy(rowb[0:1, 2 * rot, 0:TTC], rowt[0:1, 2 * rot, 0:TTC]),
                 reads=[("rowt", 2 * rot)], writes=[("rowb", 2 * rot)])
            P.op("vector", lambda v, rot=rot: v.tensor_copy(rowb[0:1, 2 * rot + 1, 0:TTC], rowt[0:1, 2 * rot + 1, 0:TTC]),
                 reads=[("rowt", 2 * rot + 1)], writes=[("rowb", 2 * rot + 1)])
            P.op("tensor", lambda te, rot=rot: te.matmul(pb[2][:, 0:TTC], ONE_ROW, rowb[0:1, 2 * rot, 0:TTC],
                                                         start=True, stop=True),
                 reads=[("rowb", 2 * rot), "ones"], writes=["pb2"])
            P.op("scalar", lambda s, rot=rot: s.activation(mbc[:, rot, :], pb[2][:, 0:TTC], AF.Copy),
                 reads=["pb2"], writes=[("mbc", rot)])
            P.op("tensor", lambda te, rot=rot: te.matmul(pb[3][:, 0:TTC], ONE_ROW, rowb[0:1, 2 * rot + 1, 0:TTC],
                                                         start=True, stop=True),
                 reads=[("rowb", 2 * rot + 1), "ones"], writes=["pb3"])
            P.op("scalar", lambda s, rot=rot: s.activation(rbc[:, rot, :], pb[3][:, 0:TTC], AF.Copy),
                 reads=["pb3"], writes=[("rbc", rot)])
            src3 = xown.rearrange("(c p) t -> p c t", p=128)
            P.dma(f"xca{rot}", xbt[:, rot, :, :], src3[:, :, tc * TTC:(tc + 1) * TTC],
                  queue="gpsimd", writes=[("xbt", rot, dc) for dc in range(8)], tag="ldxc")
            for dc in range(8):
                eng = "vector" if dc % 2 == 0 else "gpsimd"
                P.op(eng, lambda e, dc=dc, rot=rot: e.tensor_tensor(
                        xbt[:, rot, dc, :], xbt[:, rot, dc, :], mbc[:, rot, :], ALU.subtract),
                     reads=[("xbt", rot, dc), ("mbc", rot)], writes=[("xbt", rot, dc)])
                P.op(eng, lambda e, dc=dc, rot=rot: e.tensor_tensor(
                        xbt[:, rot, dc, :], xbt[:, rot, dc, :], rbc[:, rot, :], ALU.mult),
                     reads=[("xbt", rot, dc), ("rbc", rot)], writes=[("xbt", rot, dc)])
            P.dma("xcb", xrb[:, :, :], src3[:, :, tc * TTC:(tc + 1) * TTC],
                  writes=[("xrb",)], tag="ldxr")
            for f in range(32):
                bank = f % 8
                for dc in range(8):
                    P.op("tensor", lambda te, f=f, dc=dc, bank=bank, rot=rot: te.matmul(
                            pb[bank][:, 0:TTC], wtfc[:, dc, f * 128:(f + 1) * 128], xbt[:, rot, dc, :],
                            start=(dc == 0), stop=(dc == 7)),
                         reads=[("xbt", rot, dc), ("wtfc", dc, f // 8)], accum=(f"pb{bank}", dc == 0, dc == 7),
                         tag="fcmm")
                P.op("scalar", lambda s, f=f, bank=bank: s.activation(
                        hsb[:, f, :], pb[bank][:, 0:TTC], AF.Gelu, bias=bfc_t[:, f:f + 1], scale=1.0),
                     reads=[f"pb{bank}", "bfct"], writes=[("h", f)], tag="gelu")
            for do in range(8):
                bank = do % 8
                for fc in range(32):
                    P.op("tensor", lambda te, do=do, fc=fc, bank=bank: te.matmul(
                            pb[bank][:, 0:TTC], wtpj[:, fc, do * 128:(do + 1) * 128], hsb[:, fc, :],
                            start=(fc == 0), stop=False),
                         reads=[("h", fc), ("wtpj", fc)], accum=(f"pb{bank}", fc == 0, False), tag="pjmm")
                P.op("tensor", lambda te, do=do, bank=bank, t512=t512, cofs=cofs: te.matmul(
                        pb[bank][:, 0:TTC], wo_sb[:, do, :], onorm[:, t512, cofs:cofs + TTC],
                        start=False, stop=True),
                     reads=["wo", ("onorm", t512)], accum=(f"pb{bank}", False, True), tag="womm")

                P.op("scalar", lambda s, do=do, bank=bank, rot=rot: s.activation(
                        x1t[:, rot, do, :], pb[bank][:, 0:TTC], AF.Identity, bias=bpj_t[:, do:do + 1], scale=1.0),
                     reads=[f"pb{bank}", "bpjt"], writes=[("x1t", rot, do)], tag="pjev")
                eng = "vector" if do % 2 == 0 else "gpsimd"
                P.op(eng, lambda e, do=do, rot=rot: e.tensor_tensor(
                        x1t[:, rot, do, :], x1t[:, rot, do, :], xrb[:, do, :], ALU.add),
                     reads=[("x1t", rot, do), ("xrb",)], writes=[("x1t", rot, do)], tag="resadd")
            P.dma(f"x1st{rot}", x1_d[:, :, tc * TTC:(tc + 1) * TTC].rearrange("c p t -> p c t"), x1t[:, rot, :, :],
                  reads=[("x1t", rot, do) for do in range(8)], writes=[("x1d", tc)], tag="x1st")

        if dbg_stage == "C":
            x1_out = dout("x1_out", [8, 128, TOWN], BF16)
            P.dma("dbg", x1_out[:, :, :], x1_d[:, :, :],
                  reads=[("x1d", tc) for tc in range(NT_C)], writes=[("out", "x1")])
            outs = ["x1_out"]

    if st >= 3:
        P.fence("C->D")
        for wname, wdst, wsrc in (("wq", wqT, wqT_d), ("wk", wkT, wkT_d),
                                  ("wv", wvT, wvT_d), ("wl", wl1T, wl1T_d)):
            P.dma("dinitg", wdst[:, :, :], wsrc.rearrange("(c p) t -> p c t", p=128),
                  queue="gpsimd", writes=[("w1", wname)])
        P.dma("dinitg", bkvb[0:1, :, :], bkv_d.rearrange("a b -> (a b)").rearrange("(a b) -> a b", a=1).rearrange("a (c d) -> a c d", c=2), queue="gpsimd", writes=["bkvb"])
        P.dma("dinit", bq_t[:, :], bq_t_d, writes=["bqt"])
        P.dma("dinit", bl1_t[:, :], bl1_t_d, writes=["bl1t"])
        P.op("gpsimd", lambda g: g.memset(vpb[:, :, :, 64:65], 1.0), writes=["vpones"])
        P.op("vector", lambda v: v.memset(kvsb[:, :, :], 0.0), writes=["kvsb"])

        def phi_evac(psum_ap, out_ap, out_key, bias_col, pbkey, pr, bkey="bqt"):
            if bias_col is None:
                P.op("scalar", lambda sx: sx.activation(ph1[:, pr, :], psum_ap, AF.Relu),
                     reads=[pbkey], writes=[("ph1", pr)], tag="prelu")
                P.op("scalar", lambda sx: sx.activation(ph2[:, pr, :], psum_ap, AF.Exp),
                     reads=[pbkey], writes=[("ph2", pr)], tag="pexp")
            else:
                P.op("scalar", lambda sx: sx.activation(ph1[:, pr, :], psum_ap, AF.Relu,
                                                        bias=bias_col, scale=1.0),
                     reads=[pbkey, bkey], writes=[("ph1", pr)], tag="prelu")
                P.op("scalar", lambda sx: sx.activation(ph2[:, pr, :], psum_ap, AF.Exp,
                                                        bias=bias_col, scale=1.0),
                     reads=[pbkey, bkey], writes=[("ph2", pr)], tag="pexp")
            P.op("vector", lambda v: v.scalar_tensor_tensor(out_ap, ph2[:, pr, :], 1.0, ph1[:, pr, :],
                                                            ALU.min, ALU.add),
                 reads=[("ph1", pr), ("ph2", pr)], writes=[out_key], tag="phi")

        for it in range(NT_O):
            c0 = it * TT
            P.dma("x1r", x1rd[:, :, :], x1_d[:, :, c0:c0 + TT].rearrange("c p t -> p c t"),
                  reads=[("x1d", 2 * it), ("x1d", 2 * it + 1)], writes=["x1rd"], tag="x1ld")
            for dc in range(8):
                P.op("tensor", lambda te, dc=dc: te.matmul(pb[0][0:1, :], ONE_COL, x1rd[:, dc, :],
                                                           start=(dc == 0), stop=(dc == 7)),
                     reads=["x1rd", "ones"], accum=("pb0", dc == 0, dc == 7), tag="dsum")
            for dc in range(8):
                pr = dc % 2
                P.op("gpsimd", lambda g, dc=dc, pr=pr: g.tensor_tensor(ph3[:, pr, :], x1rd[:, dc, :], x1rd[:, dc, :], ALU.mult),
                     reads=["x1rd"], writes=[("ph3", pr)], tag="dsq")
                P.op("tensor", lambda te, dc=dc, pr=pr: te.matmul(pb[1][0:1, :], ONE_COL, ph3[:, pr, :],
                                                                  start=(dc == 0), stop=(dc == 7)),
                     reads=[("ph3", pr), "ones"], accum=("pb1", dc == 0, dc == 7), tag="dsumsq")
            m_ap, var_ap, s_ap, r_ap = (rowt[0:1, i, :] for i in range(4))
            m_bf, r_bf = rowb[0:1, 0, :], rowb[0:1, 1, :]
            P.op("vector", lambda v: v.tensor_scalar(m_ap, pb[0][0:1, :], 1.0 / D, None, ALU.mult),
                 reads=["pb0"], writes=[("rowt", 0)])
            P.op("vector", lambda v: v.tensor_tensor(var_ap, m_ap, m_ap, ALU.mult),
                 reads=[("rowt", 0)], writes=[("rowt", 1)])
            P.op("vector", lambda v: v.scalar_tensor_tensor(var_ap, pb[1][0:1, :], 1.0 / D, var_ap, ALU.mult, ALU.subtract),
                 reads=["pb1", ("rowt", 1)], writes=[("rowt", 1)])
            P.op("scalar", lambda sx: sx.activation(s_ap, var_ap, AF.Sqrt, bias=CCr("eps_ln"), scale=1.0),
                 reads=[("rowt", 1), "consts"], writes=[("rowt", 2)])
            P.op("vector", lambda v: v.reciprocal(r_ap, s_ap), reads=[("rowt", 2)], writes=[("rowt", 3)])
            P.op("vector", lambda v: v.tensor_copy(m_bf, m_ap), reads=[("rowt", 0)], writes=[("rowb", 0)])
            P.op("vector", lambda v: v.tensor_copy(r_bf, r_ap), reads=[("rowt", 3)], writes=[("rowb", 1)])
            P.op("tensor", lambda te: te.matmul(pb[2][:, :], ONE_ROW, m_bf, start=True, stop=True),
                 reads=[("rowb", 0), "ones"], writes=["pb2"])
            P.op("tensor", lambda te: te.matmul(pb[3][:, :], ONE_ROW, r_bf, start=True, stop=True),
                 reads=[("rowb", 1), "ones"], writes=["pb3"])
            P.op("scalar", lambda sx: sx.activation(mbd[:, :], pb[2][:, :], AF.Copy),
                 reads=["pb2"], writes=["mbd"])
            P.op("scalar", lambda sx: sx.activation(rbd[:, :], pb[3][:, :], AF.Copy),
                 reads=["pb3"], writes=["rbd"])
            for dc in range(8):
                eng = "vector" if dc % 2 == 0 else "gpsimd"
                P.op(eng, lambda e, dc=dc: e.tensor_tensor(xn1[:, dc, :], x1rd[:, dc, :], mbd[:, :], ALU.subtract),
                     reads=["x1rd", "mbd"], writes=[("xn1", dc)])
                P.op(eng, lambda e, dc=dc: e.tensor_tensor(xn1[:, dc, :], xn1[:, dc, :], rbd[:, :], ALU.mult),
                     reads=[("xn1", dc), "rbd"], writes=[("xn1", dc)])
            for oc in range(8):
                bank = oc % 2
                for dc in range(8):
                    P.op("tensor", lambda te, oc=oc, dc=dc, bank=bank: te.matmul(
                            pb[bank][:, :], wqT[:, dc, oc * 128:(oc + 1) * 128], xn1[:, dc, :],
                            start=(dc == 0), stop=(dc == 7)),
                         reads=[("xn1", dc), ("w1", "wq")], accum=(f"pb{bank}", dc == 0, dc == 7), tag="qpmm")
                phi_evac(pb[bank][:, :], qp[:, oc, c0:c0 + TT], ("qp", oc, it), bq_t[:, oc:oc + 1],
                         f"pb{bank}", oc % 2)
            for s4 in range(4):
                sr = s4 % 2
                for oh in range(2):
                    bank = 2 + oh
                    for dc in range(8):
                        P.op("tensor", lambda te, s4=s4, oh=oh, dc=dc, bank=bank: te.matmul(
                                pb[bank][:, :], xn1[:, dc, s4 * 128:(s4 + 1) * 128], wkT[:, dc, oh * 512:(oh + 1) * 512],
                                start=(dc == 0), stop=False),
                             reads=[("xn1", dc), ("w1", "wk")], accum=(f"pb{bank}", dc == 0, False), tag="kpmm")
                    P.op("tensor", lambda te, oh=oh, bank=bank: te.matmul(
                            pb[bank][:, :], ONE_ROW, bkvb[0:1, 0, oh * 512:(oh + 1) * 512],
                            start=False, stop=True),
                         reads=["bkvb", "ones"], accum=(f"pb{bank}", False, True), tag="kbias")
                    phi_evac(pb[bank][:, :], kpb[:, sr, oh * 512:(oh + 1) * 512], ("kpb", sr),
                             None, f"pb{bank}", oh)
                    bank2 = 2 + oh
                    for dc in range(8):
                        P.op("tensor", lambda te, s4=s4, oh=oh, dc=dc, bank2=bank2: te.matmul(
                                pb[bank2][:, :], xn1[:, dc, s4 * 128:(s4 + 1) * 128], wvT[:, dc, oh * 512:(oh + 1) * 512],
                                start=(dc == 0), stop=False),
                             reads=[("xn1", dc), ("w1", "wv")], accum=(f"pb{bank2}", dc == 0, False), tag="vpmm")
                    P.op("tensor", lambda te, oh=oh, bank2=bank2: te.matmul(
                            pb[bank2][:, :], ONE_ROW, bkvb[0:1, 1, oh * 512:(oh + 1) * 512],
                            start=False, stop=True),
                         reads=["bkvb", "ones"], accum=(f"pb{bank2}", False, True), tag="vbias")
                    P.op("scalar", lambda sx, sr=sr, oh=oh, bank2=bank2: sx.activation(
                            vpb[:, sr, oh * 8:(oh + 1) * 8, 0:64], pb[bank2][:, :], AF.Copy),
                         reads=[f"pb{bank2}"], writes=[("vpb", sr, oh)], tag="vev")
                for h in range(16):
                    bank = 4 + (h % 4)
                    colof = (h // 4) * 128
                    P.op("tensor", lambda te, h=h, sr=sr, bank=bank, colof=colof, s4=s4: te.matmul(
                            pb[bank][0:64, colof:colof + 65], kpb[:, sr, h * 64:(h + 1) * 64], vpb[:, sr, h, :],
                            start=(s4 == 0 and h // 4 == 0), stop=(s4 == 3 and h // 4 == 3),
                            skip_group_check=True),
                         reads=[("kpb", sr), ("vpb", sr, h // 8), "vpones"],
                         accum=(f"pb{bank}", s4 == 0 and h // 4 == 0, s4 == 3 and h // 4 == 3), tag="kvmm")
            for h in range(16):
                bank = 4 + (h % 4)
                colof = (h // 4) * 128
                P.op("vector", lambda v, h=h, bank=bank, colof=colof: v.scalar_tensor_tensor(
                        kvsb[:, h, :], pb[bank][0:64, colof:colof + 65], 0.0, kvsb[:, h, :], ALU.max, ALU.add),
                     reads=[f"pb{bank}", "kvsb"], writes=["kvsb"], tag="kvacc")
            for dc in range(8):
                pr = dc % 2
                P.op("gpsimd", lambda g, dc=dc, pr=pr: g.tensor_tensor(ph1[:, pr, :], xn1[:, dc, :], xn1[:, dc, :], ALU.mult),
                     reads=[("xn1", dc)], writes=[("ph1", pr)], tag="lsq")
                P.op("scalar", lambda sx, pr=pr: sx.activation(ph2[:, pr, :], ph1[:, pr, :], AF.Ln, bias=CC("eps2"), scale=1.0),
                     reads=[("ph1", pr), "consts"], writes=[("ph2", pr)], tag="lln")
                P.op("vector", lambda v, pr=pr: v.tensor_scalar(ph5[:, pr, :], ph2[:, pr, :], 0.5 / LN2, -0.5, ALU.mult, ALU.add),
                     reads=[("ph2", pr)], writes=[("ph5", pr)], tag="lei")
                P.op("scalar", lambda sx, pr=pr: sx.activation(ph6[:, pr, :], ph5[:, pr, :], AF.Relu, bias=CC("six"), scale=1.0),
                     reads=[("ph5", pr), "consts"], writes=[("ph6", pr)], tag="lrelu")
                P.op("scalar", lambda sx, pr=pr: sx.activation(ph3[:, pr, :], ph6[:, pr, :], AF.Exp, bias=CC("qexp"), scale=LN2),
                     reads=[("ph6", pr), "consts"], writes=[("ph3", pr)], tag="lq")
                P.op("scalar", lambda sx, pr=pr: sx.activation(ph7[:, pr, :], ph6[:, pr, :], AF.Exp, bias=CC("rexp"), scale=-LN2),
                     reads=[("ph6", pr), "consts"], writes=[("ph7", pr)], tag="lr")
                P.op("gpsimd", lambda g, dc=dc, pr=pr: g.tensor_tensor(ph2[:, pr, :], xn1[:, dc, :], ph7[:, pr, :], ALU.mult),
                     reads=[("xn1", dc), ("ph7", pr)], writes=[("ph2", pr)], tag="lu")
                P.op("vector", lambda v, pr=pr: v.tensor_scalar(ph5[:, pr, :], ph2[:, pr, :], 0.0, None, ALU.add),
                     reads=[("ph2", pr)], writes=[("ph5", pr)], tag="lk")
                P.op("vector", lambda v, pr=pr: v.tensor_scalar(ph6[:, pr, :], ph5[:, pr, :], 0.0, None, ALU.add),
                     reads=[("ph5", pr)], writes=[("ph6", pr)], tag="lkf")
                P.op("gpsimd", lambda g, dc=dc, pr=pr: g.tensor_tensor(xqt[:, dc, :], ph6[:, pr, :], ph3[:, pr, :], ALU.mult),
                     reads=[("ph6", pr), ("ph3", pr)], writes=[("xqt", dc)], tag="lxq")
            for oc in range(8):
                bank = oc % 2
                for dc in range(8):
                    P.op("tensor", lambda te, oc=oc, dc=dc, bank=bank: te.matmul(
                            pb[bank][:, :], wl1T[:, dc, oc * 128:(oc + 1) * 128], xqt[:, dc, :],
                            start=(dc == 0), stop=(dc == 7)),
                         reads=[("xqt", dc), ("w1", "wl")], accum=(f"pb{bank}", dc == 0, dc == 7), tag="f1mm")
                P.op("scalar", lambda sx, oc=oc, bank=bank: sx.activation(
                        ffnt[:, oc % 2, :], pb[bank][:, :], AF.Identity, bias=bl1_t[:, oc:oc + 1], scale=1.0),
                     reads=[f"pb{bank}", "bl1t"], writes=[("ffnt", oc % 2)], tag="f1ev")
                eng = "vector" if oc % 2 == 0 else "gpsimd"
                P.op(eng, lambda e, oc=oc: e.tensor_tensor(x1rd[:, oc, :], x1rd[:, oc, :], ffnt[:, oc % 2, :], ALU.add),
                     reads=["x1rd", ("ffnt", oc % 2)], writes=["x1rd"], tag="f1add")
            P.dma("x1ps", x1p_d[:, :, c0:c0 + TT].rearrange("c p t -> p c t"), x1rd[:, :, :],
                  reads=["x1rd"], writes=[("x1pd", it)], tag="x1pst")
        P.dma("kvcin", kvin_d[:, :], kvsb[:, :, :].rearrange("p h e -> p (h e)"),
              reads=["kvsb"], writes=["kvin"], tag="kvin")
        P.op("gpsimd", lambda g: g.collective_compute(
                "AllReduce", ALU.add, replica_groups=[[0, 1], [2, 3], [4, 5], [6, 7]],
                ins=[dram["kvin"]], outs=[dram["kvout"]]),
             reads=["kvin"], writes=["kvout"], axis="coll", tag="allreduce")

    if st >= 4:
        P.fence("D->E")
        kvo3 = dram["kvout"].rearrange("p (h e) -> p h e", e=65)
        P.op("vector", lambda v: v.memset(kvbd[:, :, :], 0.0),
             writes=[("kvbd", h) for h in range(16)], tag="kvbd0")
        P.op("vector", lambda v: v.memset(kvzd[:, :, :], 0.0),
             writes=[("kvzd", h) for h in range(16)], tag="kvzd0")
        for h in range(16):
            P.dma("kvcout", kvbd[(h % 2) * 64:(h % 2) * 64 + 64, h // 2, :][:, 0:64],
                  kvo3[:, h, 0:64], queue="gpsimd", reads=["kvout"], writes=[("kvbd", h)], tag="kvbd")
            P.dma("kvcout", kvzd[(h % 2) * 64:(h % 2) * 64 + 64, h // 2, (h % 2):(h % 2) + 1],
                  kvo3[:, h, 64:65], queue="gpsimd", reads=["kvout"], writes=[("kvzd", h)],
                  tag="kvzd", slow_ok=True)
        P.dma("einit", bo_tE[:, :], bo_t_d, writes=["boE"])
        P.dma("einit", blk2[:, :], blk2_d, writes=["blk2"])
        P.dma("einitg", woT[:, :, :], woT_d.rearrange("(c p) t -> p c t", p=128),
              queue="gpsimd", writes=[("w1", "wo")])
        for it in range(NT_O):
            c0 = it * TT
            P.dma("x1pr", x1prd[:, :, :], x1p_d[:, :, c0:c0 + TT].rearrange("c p t -> p c t"),
                  reads=[("x1pd", it)], writes=["x1prd"], tag="x1pld")
            for m in range(8):
                bank = m % 2
                P.op("tensor", lambda te, m=m, bank=bank, c0=c0: te.matmul(
                        pb[bank][:, :], kvbd[:, m, :], qp[:, m, c0:c0 + TT], start=True, stop=True),
                     reads=[("kvbd", 2 * m), ("kvbd", 2 * m + 1), ("qp", m, it)], writes=[f"pb{bank}"], tag="apre")
                P.op("tensor", lambda te, m=m, c0=c0: te.matmul(
                        pb[2][0:2, :], kvzd[:, m, :], qp[:, m, c0:c0 + TT], start=True, stop=True),
                     reads=[("kvzd", 2 * m), ("kvzd", 2 * m + 1), ("qp", m, it)], writes=["pb2"], tag="anrm")
                P.op("vector", lambda v: v.tensor_scalar(rbe[0:2, :], pb[2][0:2, :], 1e-6, None, ALU.add),
                     reads=["pb2"], writes=[("rbe", 0)], tag="eps")
                P.op("vector", lambda v: v.reciprocal(rbef[0:2, :], rbe[0:2, :]),
                     reads=[("rbe", 0)], writes=[("rbe", 1)])
                P.op("vector", lambda v: v.tensor_copy(nrb[:, :], rbef[0:2, :]),
                     reads=[("rbe", 1)], writes=["nrb"])
                P.op("tensor", lambda te: te.matmul(pb[3][:, :], blk2[:, :], nrb[:, :], start=True, stop=True),
                     reads=["nrb", "blk2"], writes=["pb3"], tag="ebc")
                P.op("scalar", lambda sx: sx.activation(rbef[:, :], pb[3][:, :], AF.Copy),
                     reads=["pb3"], writes=["rbef"], tag="ebce")
                P.op("vector", lambda v, m=m, bank=bank: v.tensor_tensor(
                        nrm[:, m, :], pb[bank][:, :], rbef[:, :], ALU.mult),
                     reads=[f"pb{bank}", "rbef"], writes=[("nrm", m)], tag="enorm")
            for do in range(8):
                bank = 4 + do % 2
                for oc in range(8):
                    P.op("tensor", lambda te, do=do, oc=oc, bank=bank: te.matmul(
                            pb[bank][:, :], woT[:, oc, do * 128:(do + 1) * 128], nrm[:, oc, :],
                            start=(oc == 0), stop=(oc == 7)),
                         reads=[("nrm", oc), ("w1", "wo")], accum=(f"pb{bank}", oc == 0, oc == 7), tag="womm2")
                P.op("scalar", lambda sx, do=do, bank=bank: sx.activation(
                        aev[:, do % 2, :], pb[bank][:, :], AF.Identity, bias=bo_tE[:, do:do + 1], scale=1.0),
                     reads=[f"pb{bank}", "boE"], writes=[("aev", do % 2)], tag="woev")
                P.op("vector", lambda v, do=do: v.tensor_tensor(
                        outb[:, do % 2, :], x1prd[:, do, :], aev[:, do % 2, :], ALU.add),
                     reads=["x1prd", ("aev", do % 2)], writes=[("outbf", do % 2)], tag="finadd")
                P.dma(f"yst{do % 2}", y_d[do, :, c0:c0 + TT], outb[:, do % 2, :],
                      reads=[("outbf", do % 2)], writes=[("out", "y", it, do)], tag="yst")

    P.barrier("sync", [k for k in P.last_writer if isinstance(k, tuple) and k[0] == "out"])
    em = P.finalize_and_emit(nc)
    em.close()
    stack.close()
    return nc, P, outs


def own_token_idx(p):
    return np.concatenate([np.arange((2 * i + p) * TT, (2 * i + p + 1) * TT) for i in range(8)])


def host_prep_core(I, core):
    b, p = core // 2, core % 2
    x = I["L0_attn_x"]
    xfull = np.ascontiguousarray(x[b].T)
    xown = np.ascontiguousarray(x[b].T[:, own_token_idx(p)])
    l0qkv = np.ascontiguousarray(np.concatenate(
        [I["L0_attn_Wq"], I["L0_attn_Wk"], I["L0_attn_Wv"]], axis=1))
    consts = np.zeros((128, NCONST), dtype=np.float32)
    for name, (col, val) in CONST_COLS.items():
        consts[:, col] = val
    onesbf = np.ones((128, 576), dtype=BF)
    onesf = np.ones((1, 128), dtype=np.float32)
    kk = np.arange(128)[:, None]
    tq = np.arange(TT)[None, :]
    diag = [(128 * m + kk <= tq).astype(np.float32) for m in range(4)]
    if p == 0:
        mlist = diag + [np.zeros((128, TT), np.float32)] * 4
    else:
        mlist = [np.ones((128, TT), np.float32)] * 4 + diag
    masks = np.stack(mlist, axis=1).astype(BF)
    d = dict(xfull=xfull, xown=xown, l0qkv=l0qkv, consts=consts,
             onesbf=onesbf, onesf=onesf, masks=masks,
             l0wo=np.ascontiguousarray(I["L0_attn_Wo"]),
             wfcT=np.ascontiguousarray(I["L0_ffn_W_fc"].T),
             wpjT=np.ascontiguousarray(I["L0_ffn_W_proj"].T),
             afc=np.ascontiguousarray(I["L0_ffn_A_fc"]),
             bfcT=np.ascontiguousarray(I["L0_ffn_B_fc"].T),
             apj=np.ascontiguousarray(I["L0_ffn_A_proj"]),
             bpjT=np.ascontiguousarray(I["L0_ffn_B_proj"].T),
             bfc_t=np.ascontiguousarray(I["L0_ffn_b_fc"].reshape(32, 128).T),
             bpj_t=np.ascontiguousarray(I["L0_ffn_b_proj"].reshape(8, 128).T),
             wqT=np.ascontiguousarray(I["L1_attn_Wq"].T),
             wkT=np.ascontiguousarray(I["L1_attn_Wk"].T),
             wvT=np.ascontiguousarray(I["L1_attn_Wv"].T),
             woT=np.ascontiguousarray(I["L1_attn_Wo"].T),
             bq_t=np.ascontiguousarray(I["L1_attn_bq"].reshape(8, 128).T),
             bkv=np.ascontiguousarray(np.stack([I["L1_attn_bk"], I["L1_attn_bv"]])),
             bo_t=np.ascontiguousarray(I["L1_attn_bo"].reshape(8, 128).T),
             wl1T=np.ascontiguousarray(I["L1_ffn_weight"].T),
             bl1_t=np.ascontiguousarray(I["L1_ffn_bias"].reshape(8, 128).T),
             blk2=np.concatenate([np.repeat([[1, 0]], 64, 1).reshape(1, 128),
                                  np.repeat([[0, 1]], 64, 1).reshape(1, 128)]).astype(BF))
    return d


_BUILD_CACHE = {}


def kernel(**inputs):
    from concourse.bass_utils import run_bass_kernel_spmd
    if "nc" not in _BUILD_CACHE:
        nc, P, _ = build(stage="E")
        _BUILD_CACHE["nc"] = nc
    nc = _BUILD_CACHE["nc"]
    in_maps = [host_prep_core(inputs, c) for c in range(8)]
    res = run_bass_kernel_spmd(nc, in_maps, core_ids=list(range(8)))
    out = np.zeros((4, 8192, 1024), dtype=np.float32)
    for c in range(8):
        b, p = c // 2, c % 2
        yc = np.asarray(res.results[c]["y"]).reshape(1024, 4096)
        out[b, own_token_idx(p), :] = yc.T
    return out


# version 04
# speedup vs baseline: 1.0857x; 1.0849x over previous
COMPUTE_ENGINES = ("tensor", "vector", "scalar", "gpsimd")


class Op:
    __slots__ = ("engine", "axis", "fn", "reads", "writes", "inc", "tag",
                 "dep_ops", "idx", "waits", "signaled", "cum", "G", "C")

    def __init__(self, engine, axis, fn, reads, writes, inc, tag):
        self.engine = engine
        self.axis = axis
        self.fn = fn
        self.reads = tuple(reads)
        self.writes = tuple(writes)
        self.inc = inc
        self.tag = tag
        self.waits = []
        self.signaled = False


class Prog:
    def __init__(self):
        self.ops = []
        self.last_writer = {}
        self.readers = {}
        self.sinks = set()
        self.last_fence = None

    def op(self, engine, fn, reads=(), writes=(), accum=None, tag="", axis=None, inc=1):
        o = Op(engine, axis or engine, fn, reads, writes, inc, tag)
        o.idx = len(self.ops)
        self.ops.append(o)
        deps = []
        for k in o.reads:
            w = self.last_writer.get(k)
            if w is not None:
                deps.append(w)
        for k in o.writes:
            w = self.last_writer.get(k)
            if w is not None:
                deps.append(w)
            for r in self.readers.get(k, ()):
                deps.append(r)
        if self.last_fence is not None and self.last_fence is not o:
            deps.append(self.last_fence)
        if accum is not None:
            key, is_start, is_stop = accum
            if is_start:
                w = self.last_writer.get(key)
                if w is not None:
                    deps.append(w)
                for r in self.readers.get(key, ()):
                    deps.append(r)
            if is_stop:
                self.last_writer[key] = o
                self.readers[key] = []
        seen = set()
        o.dep_ops = []
        for d in deps:
            if d is o or id(d) in seen:
                continue
            seen.add(id(d))
            o.dep_ops.append(d)
        for k in o.reads:
            self.readers.setdefault(k, []).append(o)
        for k in o.writes:
            self.last_writer[k] = o
            self.readers[k] = []
        return o

    def dma(self, family, out, in_, reads=(), writes=(), queue="sync", tag="", slow_ok=False):
        return self.op(queue, lambda eng: eng.dma_start(out=out, in_=in_,
                                                        allow_slow_non_contiguous=slow_ok),
                       reads=reads, writes=writes, tag=tag, axis=family, inc=16)

    def barrier(self, engine, keys, tag="barrier"):
        return self.op(engine, None, reads=keys, tag=tag)

    def fence(self, tag="fence"):
        keys = list(self.last_writer.keys())
        f = self.op("vector", lambda v: v.engine_nop(),
                    reads=keys, writes=keys, tag=tag)
        self.last_fence = f
        return f

    def finalize_and_emit(self, nc):
        for o in self.ops:
            if o.inc == 16 or o.axis == "coll":
                o.signaled = True
            for d in o.dep_ops:
                d.signaled = True
        axis_cum = {}
        for o in self.ops:
            if o.signaled:
                axis_cum[o.axis] = axis_cum.get(o.axis, 0) + o.inc
                o.cum = axis_cum[o.axis]
        def meet(a, b):
            for k, v in b.items():
                if a.get(k, 0) < v:
                    a[k] = v
        eng_G = {}
        for o in self.ops:
            need = {}
            for d in o.dep_ops:
                assert d.idx < o.idx
                meet(need, d.C)
            G_prev = eng_G.get(o.engine, {})
            waits = {}
            for ax, v in need.items():
                if ax == o.engine and o.engine != "sync" and o.axis == o.engine:
                    pass
                if G_prev.get(ax, 0) >= v:
                    continue
                waits[ax] = v
            o.waits = sorted(waits.items())
            G = dict(G_prev)
            meet(G, waits)
            eng_G[o.engine] = G
            C = dict(G)
            if o.signaled:
                v = axis_cum[o.axis] if o.axis in self.sinks else o.cum
                C[o.axis] = max(C.get(o.axis, 0), v)
            o.C = C
        engines = {}
        for o in self.ops:
            engines.setdefault(o.engine, []).append(o)
        self.stats = {e: dict(n_ops=len(v), n_sig=sum(1 for x in v if x.signaled),
                              n_waits=sum(len(x.waits) for x in v))
                      for e, v in engines.items()}
        self.stats["n_axes"] = len(axis_cum)

        import contextlib
        stack = contextlib.ExitStack()
        axes = sorted(axis_cum)
        sems = {a: stack.enter_context(nc.semaphore(f"sem_{a}")) for a in axes}
        blk = stack.enter_context(nc.Block())

        def make_body(eops):
            def body(eng):
                for o in eops:
                    for (ax, v) in o.waits:
                        eng.wait_ge(sems[ax], v)
                    if o.fn is not None:
                        ins = o.fn(eng)
                        if o.signaled:
                            ins.then_inc(sems[o.axis], o.inc)
                    else:
                        assert not o.signaled
            return body

        for ename, eops in engines.items():
            getattr(blk, ename)(make_body(eops))
        return stack


"""Megakernel builder (shared by dev runs and kernel.py).

One uniform program across 8 cores; per-core differences live in input data.
Feature-major streams: [d_model 128-chunks on partitions x tokens on free].

Phases:
  A: LN stats + K/V (full batch elem) -> HBM scratch; Q (own tokens); stats rows -> dram.
  B: causal flash attention (shifted-exp softmax, mask-mult) -> O_norm;
     stream+fake-quant W_fc/W_proj with LoRA/scale folded -> Wtot bf16 (resident).
  C: L0 FFN (gelu+proj) + Wo(O_norm) + x residual -> x1 -> HBM scratch.
  D: L1 LN + QKV (+phi) + KV/Z partials -> pairwise AllReduce; L1 FFN (e4m3)
     folded into x1 -> x1' scratch.
  E: (Qp@KV_aug / norm) @ Wo + bo + x1' -> y.
"""
import numpy as np
import ml_dtypes
import concourse.bass as bass
import concourse.mybir as mybir

F32 = mybir.dt.float32
FP16 = mybir.dt.float16
BF16 = mybir.dt.bfloat16
I32 = mybir.dt.int32
AF = mybir.ActivationFunctionType
ALU = mybir.AluOpType
AX = mybir.AxisListType.X
BF = ml_dtypes.bfloat16

D = 1024
T = 8192
TOWN = 4096
NT_G = 16
NT_O = 8
TT = 512
TTC = 256
NT_C = TOWN // TTC
DK = 128
LN2 = float(np.log(2.0))
SM_SCALE = float(128 ** -0.5)
KB = 1024

CONST_COLS = {
    "neg16": (0, -16.0),
    "eps_ln": (1, 1e-5),
    "qexp": (2, -9.0 * LN2),
    "rexp": (3, 9.0 * LN2),
    "six": (4, 6.0),
    "zero": (5, 0.0),
    "eps2": (6, 2.0 ** -24),
}
NCONST = 8

STAGES = ["A", "B", "C", "D", "E"]


def build(stage="E", dbg_stage=None):
    st = STAGES.index(stage)
    nc = bass.Bass("TRN2", target_bir_lowering=False)
    P = Prog()
    P.sinks = {"init", "dbg", "x1pst", "kvcin", "kvcout", "cinit", "dinit", "dinitg", "einit", "einitg",
               "winit", "einit", "mxstfc", "mxldfc", "mxstpj", "mxldpj"}

    dram = {}

    def din(name, shape, dt=F32):
        dram[name] = nc.dram_tensor(name, shape, dt, kind="ExternalInput").ap()
        return dram[name]

    def dint(name, shape, dt=F32):
        dram[name] = nc.dram_tensor(name, shape, dt).ap()
        return dram[name]

    def dout(name, shape, dt=F32):
        dram[name] = nc.dram_tensor(name, shape, dt, kind="ExternalOutput").ap()
        return dram[name]

    xfull = din("xfull", [D, T])
    xown = din("xown", [D, TOWN])
    l0qkv_d = din("l0qkv", [D, 384])
    l0wo_d = din("l0wo", [DK, D])
    consts_d = din("consts", [128, NCONST])
    onesbf_d = din("onesbf", [128, 576], BF16)
    onesf_d = din("onesf", [1, 128])
    masks_d = din("masks", [128, 8, TT], BF16)
    wfcT_d = din("wfcT", [D, 4096])
    wpjT_d = din("wpjT", [4096, D])
    afc_d = din("afc", [16, D])
    bfcT_d = din("bfcT", [16, 4096])
    apj_d = din("apj", [16, 4096])
    bpjT_d = din("bpjT", [16, D])
    bfc_t_d = din("bfc_t", [128, 32])
    bpj_t_d = din("bpj_t", [128, 8])
    wqT_d = din("wqT", [D, D])
    wkT_d = din("wkT", [D, D])
    wvT_d = din("wvT", [D, D])
    woT_d = din("woT", [D, D])
    bq_t_d = din("bq_t", [128, 8])
    bkv_d = din("bkv", [2, D])
    bo_t_d = din("bo_t", [128, 8])
    wl1T_d = din("wl1T", [D, D])
    bl1_t_d = din("bl1_t", [128, 8])
    blk2_d = din("blk2", [2, 128], BF16)

    kd = dint("kd", [64, 128, 128], BF16)
    vd = dint("vd", [64, 128, 128], BF16)
    mr_d = dint("mr_d", [2, NT_O, TT])
    mx_d = dint("mx_d", [2, 128])
    x1_d = dint("x1_d", [8, 128, TOWN], BF16)
    x1p_d = dint("x1p_d", [8, 128, TOWN], BF16)
    kvin_d = dint("kvin", [64, 1040])
    kvout_d = dint("kvout", [64, 1040])
    y_d = dout("y", [8, 128, TOWN])

    base0 = (nc.sbuf_base + 31) & ~31
    pad = base0 - nc.sbuf_base
    TOTAL = 207 * KB + 256
    nc.alloc_sbuf_tensor("arena", [128, TOTAL + pad], mybir.dt.uint8)

    def place(name, shape, dt, off):
        assert off % 32 == 0, (name, off)
        sz = int(np.prod(shape[1:])) * mybir.dt.size(dt)
        assert off + sz <= TOTAL, (name, off, sz, TOTAL)
        return nc.alloc_sbuf_tensor_at(name, shape, dt, offset=base0 + off)

    sc = place("sc", [128, NCONST], F32, 0)
    ones = place("ones", [128, 576], BF16, 64)
    wo_sb = place("wo_sb", [128, 8, 128], BF16, 1216)
    onesf_sb = place("onesf_sb", [1, 128], F32, 3264)
    BIGO = 3808

    xt = place("xt", [128, 2, 8, TT], F32, BIGO + 0)
    xb = place("xb", [128, 2, 8, TT], BF16, BIGO + 32 * KB)
    sq = place("sq", [128, 2, 8, TT], BF16, BIGO + 48 * KB)
    xn = place("xn", [128, 2, 8, TT], BF16, BIGO + 64 * KB)
    mb = place("mb", [128, 2, TT], BF16, BIGO + 80 * KB)
    rb = place("rb", [128, 2, TT], BF16, BIGO + 82 * KB)
    kevac = place("kevac", [128, 2, TT], BF16, BIGO + 84 * KB)
    vevac = place("vevac", [128, 2, 4, 128], BF16, BIGO + 86 * KB)
    l0w = place("l0w", [128, 8, 384], BF16, BIGO + 88 * KB)
    l0wf = place("l0wf", [128, 8, 384], F32, BIGO + 94 * KB)

    wtfc = place("wtfc", [128, 8, 4096], FP16, BIGO + 0)
    wtpj = place("wtpj", [128, 32, 1024], FP16, BIGO + 64 * KB)

    qp = place("qp", [128, 8, TOWN], BF16, BIGO + 0)
    wqT = place("wqT", [128, 8, D], BF16, BIGO + 64 * KB)
    wkT = place("wkT", [128, 8, D], BF16, BIGO + 80 * KB)
    wvT = place("wvT", [128, 8, D], BF16, BIGO + 96 * KB)
    wl1T = place("wl1T", [128, 8, D], BF16, BIGO + 112 * KB)
    woT = place("woT", [128, 8, D], BF16, BIGO + 64 * KB)
    wstgD = place("wstgD", [128, 4, D], F32, BIGO + 96 * KB)

    SCR = BIGO + 128 * KB
    rowt = place("rowt", [1, 4, TT], F32, SCR + 0)
    rowb = place("rowb", [1, 4, TT], BF16, SCR + 8 * KB)
    masks = place("masks", [128, 8, TT], BF16, SCR + 12 * KB)
    q_sb = place("q_sb", [128, NT_O, TT], BF16, SCR + 20 * KB)
    onorm = place("onorm", [128, NT_O, TT], BF16, SCR + 28 * KB)
    WK = SCR + 36 * KB
    pt = place("pt", [128, 4, TT], BF16, WK + 0)
    ksb = place("ksb", [128, 2, 8, 128], BF16, WK + 4 * KB)
    vsb = place("vsb", [128, 2, 8, 128], BF16, WK + 8 * KB)
    rbb = place("rbb", [128, TT], F32, WK + 12 * KB)
    wstg = place("wstg", [128, 2, D], F32, WK + 14 * KB)
    lt_z = place("lt_z", [128, 2, TT], I32, WK + 22 * KB)
    lt_f = place("lt_f", [128, 2, TT], FP16, WK + 26 * KB)
    abm = place("abm", [16, 4096], BF16, WK + 28 * KB)
    abm2 = place("abm2", [16, D], BF16, WK + 36 * KB)
    mxc = place("mxc", [128, 2], F32, WK + 38 * KB)
    sclb = place("sclb", [128, 8], F32, WK + 38 * KB + 32)
    hsb = place("hsb", [128, 32, TTC], FP16, WK + 0)
    x1t = place("x1t", [128, 2, 8, TTC], BF16, WK + 16 * KB)
    mbc = place("mbc", [128, 2, TTC], FP16, WK + 24 * KB)
    rbc = place("rbc", [128, 2, TTC], FP16, WK + 25 * KB)
    bfc_t = place("bfc_t", [128, 32], F32, WK + 26 * KB)
    bpj_t = place("bpj_t", [128, 8], F32, WK + 26 * KB + 160)
    bstg = place("bstg", [128, 64], F32, WK + 26 * KB + 224)
    xbt = place("xbt", [128, 2, 8, TTC], FP16, SCR + 12 * KB)
    xrb = place("xrb", [128, 8, TTC], F32, SCR + 20 * KB)
    DW = SCR + 12 * KB
    x1rd = place("x1rd", [128, 8, TT], BF16, DW + 0)
    xn1 = place("xn1", [128, 8, TT], BF16, DW + 8 * KB)
    xqt = place("xqt", [128, 8, TT], BF16, DW + 16 * KB)
    mbd = place("mbd", [128, TT], BF16, DW + 24 * KB)
    rbd = place("rbd", [128, TT], BF16, DW + 25 * KB)
    kpb = place("kpb", [128, 2, D], BF16, DW + 26 * KB)
    vpb = place("vpb", [128, 2, 16, 65], BF16, DW + 30 * KB)
    kvsb = place("kvsb", [64, 16, 65], F32, DW + 34 * KB + 96)
    ffnt = place("ffnt", [128, 2, TT], BF16, DW + 38 * KB + 192)
    bq_t = place("bq_t", [128, 8], F32, DW + 40 * KB + 192)
    bo_t = place("bo_t", [128, 8], F32, DW + 40 * KB + 224)
    bl1_t = place("bl1_t", [128, 8], F32, DW + 40 * KB + 256)
    bkvb = place("bkvb", [1, 2, D], BF16, DW + 40 * KB + 288)
    ph1 = place("ph1", [128, 2, TT], F32, DW + 44 * KB + 320)
    ph2 = place("ph2", [128, 2, TT], F32, DW + 48 * KB + 320)
    ph5 = place("ph5", [128, 2, TT], I32, DW + 52 * KB + 320)
    ph3 = place("ph3", [128, 2, TT], BF16, DW + 56 * KB + 320)
    ph6 = place("ph6", [128, 2, TT], BF16, DW + 58 * KB + 320)
    ph7 = place("ph7", [128, 2, TT], BF16, DW + 60 * KB + 320)
    nrm = place("nrm", [128, 8, TT], BF16, DW + 0)
    x1prd = place("x1prd", [128, 8, TT], BF16, DW + 8 * KB)
    outb = place("outb", [128, 2, TT], F32, DW + 16 * KB)
    kvf = place("kvf", [64, 16, 65], F32, DW + 20 * KB)
    kvbb = place("kvbb", [64, 16, 65], BF16, DW + 24 * KB + 96)
    rbe = place("rbe", [64, TT], F32, DW + 26 * KB + 256)
    aev = place("aev", [128, 2, TT], BF16, DW + 28 * KB + 256)
    bo_tE = place("bo_tE", [128, 8], F32, DW + 30 * KB + 256)
    kvbd = place("kvbd", [128, 8, 128], BF16, DW + 31 * KB + 256)
    kvzd = place("kvzd", [128, 8, 2], BF16, DW + 33 * KB + 256)
    blk2 = place("blk2", [2, 128], BF16, DW + 33 * KB + 320)
    nrb = place("nrb", [2, TT], BF16, DW + 34 * KB)
    rbef = place("rbef", [128, TT], F32, DW + 35 * KB + 32)

    import contextlib
    stack = contextlib.ExitStack()
    PSt = lambda *a: stack.enter_context(nc.psum_tensor(*a))
    pb = [PSt(f"pb{i}", [128, TT], F32) for i in range(8)]

    CC = lambda n: sc[:, CONST_COLS[n][0]:CONST_COLS[n][0] + 1]
    CCr = lambda n: sc[0:1, CONST_COLS[n][0]:CONST_COLS[n][0] + 1]
    ONE_COL = ones[:, 0:1]
    ONE_ROW = ones[0:1, 0:128]

    P.dma("init", sc[:], consts_d, writes=["consts"])
    P.dma("init", ones[:], onesbf_d, writes=["ones"])
    P.dma("init", onesf_sb[:], onesf_d, writes=["onesf"])
    P.dma("init", masks[:], masks_d, writes=["masks"])
    for dc in range(8):
        P.dma("init", l0wf[:, dc, :], l0qkv_d[dc * 128:(dc + 1) * 128, :], writes=[("l0wf", dc)])
    for dc in range(8):
        P.op("vector", lambda v, dc=dc: v.tensor_copy(l0w[:, dc, :], l0wf[:, dc, :]),
             reads=[("l0wf", dc)], writes=[("l0w", dc)])

    def ln_tile(src_ap, rot, save=None):
        P.dma(f"x{rot}", xb[:, rot, :, :], src_ap, queue="gpsimd",
              writes=[("xb", rot, dc) for dc in range(8)], tag="ldx")
        for dc in range(8):
            P.op("gpsimd", lambda g, dc=dc, rot=rot: g.tensor_tensor(sq[:, rot, dc, :], xb[:, rot, dc, :], xb[:, rot, dc, :], ALU.mult),
                 reads=[("xb", rot, dc)], writes=[("sq", rot, dc)], tag="sq")
        for dc in range(8):
            P.op("tensor", lambda te, dc=dc, rot=rot: te.matmul(pb[0][0:1, :], ONE_COL, xb[:, rot, dc, :],
                                                                start=(dc == 0), stop=(dc == 7)),
                 reads=[("xb", rot, dc), "ones"], accum=("pb0", dc == 0, dc == 7), tag="summ")
        for dc in range(8):
            P.op("tensor", lambda te, dc=dc, rot=rot: te.matmul(pb[1][0:1, :], ONE_COL, sq[:, rot, dc, :],
                                                                start=(dc == 0), stop=(dc == 7)),
                 reads=[("sq", rot, dc), "ones"], accum=("pb1", dc == 0, dc == 7), tag="sumsq")
        m_ap, var_ap, s_ap, r_ap = (rowt[0:1, i, :] for i in range(4))
        m_bf, r_bf = rowb[0:1, 0, :], rowb[0:1, 1, :]
        P.op("vector", lambda v: v.tensor_scalar(m_ap, pb[0][0:1, :], 1.0 / D, None, ALU.mult),
             reads=["pb0"], writes=[("rowt", 0)])
        P.op("vector", lambda v: v.tensor_tensor(var_ap, m_ap, m_ap, ALU.mult),
             reads=[("rowt", 0)], writes=[("rowt", 1)])
        P.op("vector", lambda v: v.scalar_tensor_tensor(var_ap, pb[1][0:1, :], 1.0 / D, var_ap, ALU.mult, ALU.subtract),
             reads=["pb1", ("rowt", 1)], writes=[("rowt", 1)])
        P.op("scalar", lambda s: s.activation(s_ap, var_ap, AF.Ln, bias=CCr("eps_ln"), scale=1.0),
             reads=[("rowt", 1), "consts"], writes=[("rowt", 2)])
        P.op("scalar", lambda s: s.activation(r_ap, s_ap, AF.Exp, scale=-0.5),
             reads=[("rowt", 2)], writes=[("rowt", 3)])
        P.op("vector", lambda v: v.tensor_copy(m_bf, m_ap), reads=[("rowt", 0)], writes=[("rowb", 0)])
        P.op("vector", lambda v: v.tensor_copy(r_bf, r_ap), reads=[("rowt", 3)], writes=[("rowb", 1)])
        if save is not None:
            P.dma("mrstm", mr_d[0, save, :], m_ap, reads=[("rowt", 0)], writes=[("mrd", 0, save)])
            P.dma("mrstr", mr_d[1, save, :], r_ap, reads=[("rowt", 3)], writes=[("mrd", 1, save)])
        P.op("tensor", lambda te: te.matmul(pb[2][:, :], ONE_ROW, m_bf, start=True, stop=True),
             reads=[("rowb", 0), "ones"], writes=["pb2"])
        P.op("tensor", lambda te: te.matmul(pb[3][:, :], ONE_ROW, r_bf, start=True, stop=True),
             reads=[("rowb", 1), "ones"], writes=["pb3"])
        P.op("scalar", lambda s, rot=rot: s.activation(mb[:, rot, :], pb[2][:, :], AF.Copy),
             reads=["pb2"], writes=[("mb", rot)])
        P.op("scalar", lambda s, rot=rot: s.activation(rb[:, rot, :], pb[3][:, :], AF.Copy),
             reads=["pb3"], writes=[("rb", rot)])
        for dc in range(8):
            eng = "vector" if dc % 2 == 0 else "gpsimd"
            P.op(eng, lambda e, dc=dc, rot=rot: e.tensor_tensor(xn[:, rot, dc, :], xb[:, rot, dc, :], mb[:, rot, :], ALU.subtract),
                 reads=[("xb", rot, dc), ("mb", rot)], writes=[("xn", rot, dc)])
            P.op(eng, lambda e, dc=dc, rot=rot: e.tensor_tensor(xn[:, rot, dc, :], xn[:, rot, dc, :], rb[:, rot, :], ALU.mult),
                 reads=[("xn", rot, dc), ("rb", rot)], writes=[("xn", rot, dc)])

    for t in range(NT_G):
        rot = t % 2
        ln_tile(xfull.rearrange("(c p) t -> p c t", p=128)[:, :, t * TT:(t + 1) * TT], rot)
        for dc in range(8):
            P.op("tensor", lambda te, dc=dc, rot=rot: te.matmul(pb[4][:, :], l0w[:, dc, 128:256], xn[:, rot, dc, :],
                                                                start=(dc == 0), stop=(dc == 7)),
                 reads=[("xn", rot, dc), ("l0w", dc)], accum=("pb4", dc == 0, dc == 7), tag="kmm")
        P.op("scalar", lambda s, rot=rot: s.activation(kevac[:, rot, :], pb[4][:, :], AF.Copy),
             reads=["pb4"], writes=[("kevac", rot)])
        P.dma(f"kst{rot}", kd[4 * t:4 * t + 4, :, :].rearrange("j p t -> p j t"), kevac[:, rot, :],
              reads=[("kevac", rot)], writes=[("kdt", t)])
        for s4 in range(4):
            bank = 5 + (s4 % 2)
            for dc in range(8):
                P.op("tensor", lambda te, dc=dc, s4=s4, bank=bank, rot=rot: te.matmul(
                        pb[bank][:, 0:128], xn[:, rot, dc, s4 * 128:(s4 + 1) * 128], l0w[:, dc, 256:384],
                        start=(dc == 0), stop=(dc == 7)),
                     reads=[("xn", rot, dc), ("l0w", dc)], accum=(f"pb{bank}", dc == 0, dc == 7), tag="vmm")
            P.op("scalar", lambda s, bank=bank, s4=s4, rot=rot: s.activation(vevac[:, rot, s4, :], pb[bank][:, 0:128], AF.Copy),
                 reads=[f"pb{bank}"], writes=[("vevac", rot, s4)])
        P.dma(f"vst{rot}", vd[4 * t:4 * t + 4, :, :].rearrange("j p t -> p j t"), vevac[:, rot, :, :],
              reads=[("vevac", rot, s4) for s4 in range(4)], writes=[("vdt", t)])
    for t in range(NT_O):
        rot = t % 2
        ln_tile(xown.rearrange("(c p) t -> p c t", p=128)[:, :, t * TT:(t + 1) * TT], rot, save=t)
        for dc in range(8):
            P.op("tensor", lambda te, dc=dc, rot=rot: te.matmul(pb[7][:, :], l0w[:, dc, 0:128], xn[:, rot, dc, :],
                                                                start=(dc == 0), stop=(dc == 7)),
                 reads=[("xn", rot, dc), ("l0w", dc)], accum=("pb7", dc == 0, dc == 7), tag="qmm")
        P.op("scalar", lambda s, t=t: s.activation(q_sb[:, t, :], pb[7][:, :], AF.Copy),
             reads=["pb7"], writes=[("q", t)])

    outs = []
    if dbg_stage == "A":
        q_out = dout("q_out", [128, TOWN], BF16)
        mr_out = dout("mr_out", [2, NT_O, TT])
        k_out = dout("k_out", [64, 128, 128], BF16)
        v_out = dout("v_out", [64, 128, 128], BF16)
        for t in range(NT_O):
            P.dma("dbg", q_out[:, t * TT:(t + 1) * TT], q_sb[:, t, :],
                  reads=[("q", t)], writes=[("out", "q", t)])
        P.dma("dbg", mr_out[:, :, :], mr_d[:, :, :],
              reads=[("mrd", i, t) for i in range(2) for t in range(NT_O)], writes=[("out", "mr")])
        P.dma("dbg", k_out[:, :, :], kd[:, :, :], reads=[("kdt", t) for t in range(NT_G)], writes=[("out", "k")])
        P.dma("dbg", v_out[:, :, :], vd[:, :, :], reads=[("vdt", t) for t in range(NT_G)], writes=[("out", "v")])
        outs = ["q_out", "mr_out", "k_out", "v_out"]

    if st >= 1:
        P.fence("A->B")
        P.dma("winit", l0wf[:, 0, 0:256].rearrange("p (c t) -> p c t", c=2), None, writes=[]) if False else None
        wof = l0wf
        P.dma("winit", wof[:, 0, 0:128], l0wo_d[:, 0:1024].rearrange("p (c t) -> p c t", c=8)[:, 0, :][:, 0:128], writes=[]) if False else None
        P.dma("winit", wof[:, 0:3, :].rearrange("p c t -> p (c t)")[:, 0:1024], l0wo_d[:, :],
              writes=[("wof",)])
        P.op("vector", lambda v: v.tensor_copy(wo_sb[:, :, :].rearrange("p c t -> p (c t)"),
                                               wof[:, 0:3, :].rearrange("p c t -> p (c t)")[:, 0:1024]),
             reads=[("wof",)], writes=["wo"])

        def ld16(dstslice, key, src_ap, stg_key, fam):
            P.dma("abld", wstg[0:16, 0, :], src_ap, writes=[("wstg", 0)])
            P.op("vector", lambda v: v.tensor_copy(dstslice, wstg[0:16, 0, :]),
                 reads=[("wstg", 0)], writes=[key])

        SBANKS = [0, 1, 4, 5]

        def attn_block(i):
            nk = 8 * i + 8
            for g in range(nk // 8):
                gp = g % 2
                P.dma(f"kldg{gp}", ksb[:, gp, :, :], kd[8 * g:8 * g + 8, :, :].rearrange("j p t -> p j t"),
                      reads=[("kdt", 2 * g), ("kdt", 2 * g + 1)],
                      writes=[("ksb", gp, jr) for jr in range(8)], tag="kld")
                P.dma(f"vldg{gp}", vsb[:, gp, :, :], vd[8 * g:8 * g + 8, :, :].rearrange("j p t -> p j t"),
                      reads=[("vdt", 2 * g), ("vdt", 2 * g + 1)],
                      writes=[("vsb", gp, jr) for jr in range(8)], tag="vld")
            for j in range(nk):
                jr = j % 8
                gp = (j // 8) % 2
                pr = j % 4
                sbank = SBANKS[j % 4]
                P.op("tensor", lambda te, i=i, jr=jr, gp=gp, sbank=sbank: te.matmul(
                        pb[sbank][:, :], ksb[:, gp, jr, :], q_sb[:, i, :], start=True, stop=True),
                     reads=[("ksb", gp, jr), ("q", i)], writes=[f"pb{sbank}"], tag="smm")
                P.op("scalar", lambda s, pr=pr, sbank=sbank: s.activation(
                        pt[:, pr, :], pb[sbank][:, :], AF.Exp, bias=CC("neg16"), scale=SM_SCALE),
                     reads=[f"pb{sbank}", "consts"], writes=[("pt", pr)], tag="exp")
                if j >= 8 * i:
                    mi = j - 8 * i
                    P.op("vector", lambda v, pr=pr, mi=mi: v.tensor_tensor(
                            pt[:, pr, :], pt[:, pr, :], masks[:, mi, :], ALU.mult),
                         reads=[("pt", pr), "masks"], writes=[("pt", pr)], tag="mask")
                P.op("tensor", lambda te, jr=jr, gp=gp, pr=pr, j=j, nk=nk: te.matmul(pb[6][:, :], vsb[:, gp, jr, :], pt[:, pr, :],
                                                                       start=(j == 0), stop=(j == nk - 1)),
                     reads=[("vsb", gp, jr), ("pt", pr)], accum=("pb6", j == 0, j == nk - 1), tag="omm")
                P.op("tensor", lambda te, pr=pr, j=j, nk=nk: te.matmul(pb[7][0:1, :], ONE_COL, pt[:, pr, :],
                                                                       start=(j == 0), stop=(j == nk - 1)),
                     reads=[("pt", pr), "ones"], accum=("pb7", j == 0, j == nk - 1), tag="dmm")
            P.op("scalar", lambda sx: sx.activation(rowt[0:1, 0, :], pb[7][0:1, :], AF.Ln),
                 reads=["pb7"], writes=[("rowt", 0)])
            P.op("scalar", lambda sx: sx.activation(rowt[0:1, 0, :], rowt[0:1, 0, :], AF.Exp, scale=-1.0),
                 reads=[("rowt", 0)], writes=[("rowt", 0)])
            P.op("vector", lambda v: v.tensor_copy(rowb[0:1, 2, :], rowt[0:1, 0, :]),
                 reads=[("rowt", 0)], writes=[("rowb", 2)])
            P.op("tensor", lambda te: te.matmul(pb[3][:, :], ONE_ROW, rowb[0:1, 2, :], start=True, stop=True),
                 reads=[("rowb", 2), "ones"], writes=["pb3"])
            P.op("scalar", lambda s: s.activation(rbb[:, :], pb[3][:, :], AF.Copy),
                 reads=["pb3"], writes=["rbb"])
            P.op("vector", lambda v, i=i: v.tensor_tensor(onorm[:, i, :], pb[6][:, :], rbb[:, :], ALU.mult),
                 reads=["pb6", "rbb"], writes=[("onorm", i)], tag="onorm")

        wq = []

        def wq_pass1(fam, src_d, nch, wslice):
            for ch in range(nch):
                for cc in range(4096 // D if fam == "wfc" else 1):
                    def th(ch=ch, cc=cc, fam=fam, src_d=src_d, wslice=wslice):
                        r = (ch + cc) % 2
                        P.dma(f"w{fam}{r}", wstg[:, r, :], src_d[ch * 128:(ch + 1) * 128, cc * D:(cc + 1) * D],
                              writes=[("wstg", r)], tag="wld")
                        P.op("vector", lambda v: v.tensor_reduce(mxc[:, r:r + 1], wstg[:, r, :], AX, ALU.max,
                                                                 apply_absolute_value=True),
                             reads=[("wstg", r)], writes=[("mxcol", fam, r)], tag="wmx")
                        P.op("vector", lambda v: v.tensor_tensor(sclb[:, 3:4], sclb[:, 3:4], mxc[:, r:r + 1], ALU.max),
                             reads=[("mxcol", fam, r), ("mxacc", fam)], writes=[("mxacc", fam)], tag="wmxa")
                        dst, dkey = wslice(ch, cc)
                        P.op("vector", lambda v: v.tensor_copy(dst, wstg[:, r, :]),
                             reads=[("wstg", r)], writes=[dkey], tag="wcast")
                    wq.append(th)

        def wq_scale(fam):
            def th(fam=fam):
                mslot = 0 if fam == "wfc" else 1
                P.dma("mxstfc" if fam == "wfc" else "mxstpj",
                      mx_d[mslot, :], sclb[:, 3:4], reads=[("mxacc", fam)], writes=[("mxd", fam)])
                P.dma("mxldfc" if fam == "wfc" else "mxldpj",
                      rowt[0:1, 0, 0:128], mx_d[mslot, :], reads=[("mxd", fam)], writes=[("rowt", 0)])
                P.op("vector", lambda v: v.tensor_reduce(rowt[0:1, 1, 0:1], rowt[0:1, 0, 0:128], AX, ALU.max,
                                                         apply_absolute_value=True),
                     reads=[("rowt", 0)], writes=[("rowt", 1)])
                P.op("vector", lambda v: v.tensor_scalar(rowt[0:1, 1, 1:2], rowt[0:1, 1, 0:1], 1.0 / 127.0, None, ALU.mult),
                     reads=[("rowt", 1)], writes=[("rowt", 1)])
                P.op("scalar", lambda sx: sx.activation(rowt[0:1, 1, 2:3], rowt[0:1, 1, 1:2], AF.Ln),
                     reads=[("rowt", 1)], writes=[("rowt", 1)])
                P.op("scalar", lambda sx: sx.activation(rowt[0:1, 1, 2:3], rowt[0:1, 1, 2:3], AF.Exp, scale=-1.0),
                     reads=[("rowt", 1)], writes=[("rowt", 1)])
                P.op("tensor", lambda te: te.matmul(pb[3][:, 0:2], onesf_sb[0:1, :], rowt[0:1, 1, 1:3],
                                                    start=True, stop=True),
                     reads=[("rowt", 1), ("rowt", 1), "onesf"], writes=["pb3"])
                P.op("scalar", lambda s: s.activation(sclb[:, 0:2], pb[3][:, 0:2], AF.Copy),
                     reads=["pb3"], writes=[("sclb", fam)])
                if fam == "wfc":
                    P.op("gpsimd", lambda g: g.memset(sclb[:, 3:4], 0.0),
                         writes=[("mxacc", "wfc"), ("mxacc", "wpj")])
            wq.append(th)

        def wq_pass2(fam, nch, ncc512, wslice512, a_lhs, b_rhs):
            for ch in range(nch):
                for fc in range(ncc512):
                    def th(ch=ch, fc=fc, fam=fam, wslice512=wslice512, a_lhs=a_lhs, b_rhs=b_rhs):
                        r = (ch * ncc512 + fc) % 2
                        wsl, wkey = wslice512(ch, fc)
                        P.op("tensor", lambda te: te.matmul(pb[2][:, :], a_lhs(ch), b_rhs(fc), start=True, stop=True),
                             reads=[("abm",)], writes=["pb2"], tag="lmm")
                        P.op("vector", lambda v: v.tensor_scalar(lt_z[:, r, :], wsl, sclb[:, 1:2], None, ALU.mult),
                             reads=[wkey, ("sclb", fam)], writes=[("ltz", r)], tag="lz")
                        P.op("scalar", lambda s: s.activation(wsl, pb[2][:, :], AF.Copy, scale=2.0),
                             reads=["pb2"], writes=[wkey], tag="lme")
                        P.op("vector", lambda v: v.tensor_scalar(lt_f[:, r, :], lt_z[:, r, :], sclb[:, 0:1], None, ALU.mult),
                             reads=[("ltz", r), ("sclb", fam)], writes=[("ltf", r)], tag="lzs")
                        P.op("gpsimd", lambda g: g.tensor_tensor(wsl, lt_f[:, r, :], wsl, ALU.add),
                             reads=[("ltf", r), wkey], writes=[wkey], tag="lw")
                    wq.append(th)

        def fc_slice1024(ch, cc):
            return wtfc[:, ch, cc * D:(cc + 1) * D], ("wtfc", ch, cc)

        def fc_slice512(ch, fc):
            return wtfc[:, ch, fc * TT:(fc + 1) * TT], ("wtfc", ch, fc // 2)

        def pj_slice1024(ch, cc):
            return wtpj[:, ch * 4:(ch + 1) * 4, :].rearrange("p c t -> p (c t)")[:, 0:D] if False else (None, None)

        def pj_slice1024b(ch, cc):
            return wtpj[:, ch, :], ("wtpj", ch)

        def pj_slice512(ch, fc):
            return wtpj[:, ch, fc * TT:(fc + 1) * TT], ("wtpj", ch)

        def afc_lhs(ch):
            return abm2[:, ch * 128:(ch + 1) * 128]

        def bfc_rhs(fc):
            return abm[:, fc * TT:(fc + 1) * TT]

        def apj_lhs(ch):
            return abm[:, ch * 128:(ch + 1) * 128]

        def bpj_rhs(fc):
            return abm2[:, fc * TT:(fc + 1) * TT]

        def fc_mats():
            P.op("gpsimd", lambda g: g.memset(sclb[:, 3:4], 0.0), writes=[("mxacc", "wfc")])
            ld16(abm2[:, :], ("abm",), afc_d[:, :], "afc", "wfc")
            for q in range(4):
                ld16(abm[:, q * D:(q + 1) * D], ("abm",), bfcT_d[:, q * D:(q + 1) * D], f"bfcT{q}", "wfc")
        def pj_mats():
            for q in range(4):
                ld16(abm[:, q * D:(q + 1) * D], ("abm",), apj_d[:, q * D:(q + 1) * D], f"apj{q}", "wpj")
            ld16(abm2[:, :], ("abm",), bpjT_d[:, :], "bpjT", "wpj")

        wq.append(fc_mats)
        wq_pass1("wfc", wfcT_d, 8, fc_slice1024)
        wq_scale("wfc")
        wq_pass2("wfc", 8, 8, fc_slice512, afc_lhs, bfc_rhs)
        wq.append(pj_mats)
        wq_pass1("wpj", wpjT_d, 32, pj_slice1024b)
        wq_scale("wpj")
        wq_pass2("wpj", 32, 2, pj_slice512, apj_lhs, bpj_rhs)

        per = (len(wq) + NT_O - 1) // NT_O
        for i in range(NT_O):
            attn_block(i)
            for th in wq[i * per:(i + 1) * per]:
                th()

        if dbg_stage == "B":
            on_out = dout("on_out", [128, TOWN], BF16)
            wt_out = dout("wt_out", [128, 8, 4096], FP16)
            wt2_out = dout("wt2_out", [128, 32, 1024], FP16)
            for i in range(NT_O):
                P.dma("dbg", on_out[:, i * TT:(i + 1) * TT], onorm[:, i, :],
                      reads=[("onorm", i)], writes=[("out", "on", i)])
            P.dma("dbg", wt_out[:, :, :], wtfc[:, :, :],
                  reads=[("wtfc", ch, cc) for ch in range(8) for cc in range(4)], writes=[("out", "wt")])
            P.dma("dbg", wt2_out[:, :, :], wtpj[:, :, :],
                  reads=[("wtpj", ch) for ch in range(32)], writes=[("out", "wt2")])
            outs = ["on_out", "wt_out", "wt2_out"]

    if st >= 2:
        P.fence("B->C")
        P.dma("cinit", bstg[:, 0:32], bfc_t_d, writes=[("bstg", 0)])
        P.op("vector", lambda v: v.tensor_copy(bfc_t[:, :], bstg[:, 0:32]), reads=[("bstg", 0)], writes=["bfct"])
        P.dma("cinit", bstg[:, 32:40], bpj_t_d, writes=[("bstg", 1)])
        P.op("vector", lambda v: v.tensor_copy(bpj_t[:, :], bstg[:, 32:40]), reads=[("bstg", 1)], writes=["bpjt"])

        for tc in range(NT_C):
            rot = tc % 2
            t512 = tc // 2
            half = tc % 2
            cofs = half * TTC
            P.dma(f"mrldm{rot}", rowt[0:1, 2 * rot, 0:TTC], mr_d[0, t512, cofs:cofs + TTC],
                  reads=[("mrd", 0, t512)], writes=[("rowt", 2 * rot)], tag="mrld")
            P.dma(f"mrldr{rot}", rowt[0:1, 2 * rot + 1, 0:TTC], mr_d[1, t512, cofs:cofs + TTC],
                  reads=[("mrd", 1, t512)], writes=[("rowt", 2 * rot + 1)], tag="mrld")
            P.op("vector", lambda v, rot=rot: v.tensor_copy(rowb[0:1, 2 * rot, 0:TTC], rowt[0:1, 2 * rot, 0:TTC]),
                 reads=[("rowt", 2 * rot)], writes=[("rowb", 2 * rot)])
            P.op("vector", lambda v, rot=rot: v.tensor_copy(rowb[0:1, 2 * rot + 1, 0:TTC], rowt[0:1, 2 * rot + 1, 0:TTC]),
                 reads=[("rowt", 2 * rot + 1)], writes=[("rowb", 2 * rot + 1)])
            P.op("tensor", lambda te, rot=rot: te.matmul(pb[2][:, 0:TTC], ONE_ROW, rowb[0:1, 2 * rot, 0:TTC],
                                                         start=True, stop=True),
                 reads=[("rowb", 2 * rot), "ones"], writes=["pb2"])
            P.op("scalar", lambda s, rot=rot: s.activation(mbc[:, rot, :], pb[2][:, 0:TTC], AF.Copy),
                 reads=["pb2"], writes=[("mbc", rot)])
            P.op("tensor", lambda te, rot=rot: te.matmul(pb[3][:, 0:TTC], ONE_ROW, rowb[0:1, 2 * rot + 1, 0:TTC],
                                                         start=True, stop=True),
                 reads=[("rowb", 2 * rot + 1), "ones"], writes=["pb3"])
            P.op("scalar", lambda s, rot=rot: s.activation(rbc[:, rot, :], pb[3][:, 0:TTC], AF.Copy),
                 reads=["pb3"], writes=[("rbc", rot)])
            src3 = xown.rearrange("(c p) t -> p c t", p=128)
            P.dma(f"xca{rot}", xbt[:, rot, :, :], src3[:, :, tc * TTC:(tc + 1) * TTC],
                  queue="gpsimd", writes=[("xbt", rot, dc) for dc in range(8)], tag="ldxc")
            for dc in range(8):
                eng = "vector" if dc % 2 == 0 else "gpsimd"
                P.op(eng, lambda e, dc=dc, rot=rot: e.tensor_tensor(
                        xbt[:, rot, dc, :], xbt[:, rot, dc, :], mbc[:, rot, :], ALU.subtract),
                     reads=[("xbt", rot, dc), ("mbc", rot)], writes=[("xbt", rot, dc)])
                P.op(eng, lambda e, dc=dc, rot=rot: e.tensor_tensor(
                        xbt[:, rot, dc, :], xbt[:, rot, dc, :], rbc[:, rot, :], ALU.mult),
                     reads=[("xbt", rot, dc), ("rbc", rot)], writes=[("xbt", rot, dc)])
            P.dma("xcb", xrb[:, :, :], src3[:, :, tc * TTC:(tc + 1) * TTC],
                  writes=[("xrb",)], tag="ldxr")
            for f in range(32):
                bank = f % 8
                for dc in range(8):
                    P.op("tensor", lambda te, f=f, dc=dc, bank=bank, rot=rot: te.matmul(
                            pb[bank][:, 0:TTC], wtfc[:, dc, f * 128:(f + 1) * 128], xbt[:, rot, dc, :],
                            start=(dc == 0), stop=(dc == 7)),
                         reads=[("xbt", rot, dc), ("wtfc", dc, f // 8)], accum=(f"pb{bank}", dc == 0, dc == 7),
                         tag="fcmm")
                P.op("scalar", lambda s, f=f, bank=bank: s.activation(
                        hsb[:, f, :], pb[bank][:, 0:TTC], AF.Gelu, bias=bfc_t[:, f:f + 1], scale=1.0),
                     reads=[f"pb{bank}", "bfct"], writes=[("h", f)], tag="gelu")
            for do in range(8):
                bank = do % 8
                for fc in range(32):
                    P.op("tensor", lambda te, do=do, fc=fc, bank=bank: te.matmul(
                            pb[bank][:, 0:TTC], wtpj[:, fc, do * 128:(do + 1) * 128], hsb[:, fc, :],
                            start=(fc == 0), stop=False),
                         reads=[("h", fc), ("wtpj", fc)], accum=(f"pb{bank}", fc == 0, False), tag="pjmm")
                P.op("tensor", lambda te, do=do, bank=bank, t512=t512, cofs=cofs: te.matmul(
                        pb[bank][:, 0:TTC], wo_sb[:, do, :], onorm[:, t512, cofs:cofs + TTC],
                        start=False, stop=True),
                     reads=["wo", ("onorm", t512)], accum=(f"pb{bank}", False, True), tag="womm")

                P.op("scalar", lambda s, do=do, bank=bank, rot=rot: s.activation(
                        x1t[:, rot, do, :], pb[bank][:, 0:TTC], AF.Identity, bias=bpj_t[:, do:do + 1], scale=1.0),
                     reads=[f"pb{bank}", "bpjt"], writes=[("x1t", rot, do)], tag="pjev")
                eng = "vector" if do % 2 == 0 else "gpsimd"
                P.op(eng, lambda e, do=do, rot=rot: e.tensor_tensor(
                        x1t[:, rot, do, :], x1t[:, rot, do, :], xrb[:, do, :], ALU.add),
                     reads=[("x1t", rot, do), ("xrb",)], writes=[("x1t", rot, do)], tag="resadd")
            P.dma(f"x1st{rot}", x1_d[:, :, tc * TTC:(tc + 1) * TTC].rearrange("c p t -> p c t"), x1t[:, rot, :, :],
                  reads=[("x1t", rot, do) for do in range(8)], writes=[("x1d", tc)], tag="x1st")

        if dbg_stage == "C":
            x1_out = dout("x1_out", [8, 128, TOWN], BF16)
            P.dma("dbg", x1_out[:, :, :], x1_d[:, :, :],
                  reads=[("x1d", tc) for tc in range(NT_C)], writes=[("out", "x1")])
            outs = ["x1_out"]

    if st >= 3:
        P.fence("C->D")
        for wname, wdst, wsrc in (("wq", wqT, wqT_d), ("wk", wkT, wkT_d),
                                  ("wv", wvT, wvT_d), ("wl", wl1T, wl1T_d)):
            P.dma("dinitg", wdst[:, :, :], wsrc.rearrange("(c p) t -> p c t", p=128),
                  queue="gpsimd", writes=[("w1", wname)])
        P.dma("dinitg", bkvb[0:1, :, :], bkv_d.rearrange("a b -> (a b)").rearrange("(a b) -> a b", a=1).rearrange("a (c d) -> a c d", c=2), queue="gpsimd", writes=["bkvb"])
        P.dma("dinit", bq_t[:, :], bq_t_d, writes=["bqt"])
        P.dma("dinit", bl1_t[:, :], bl1_t_d, writes=["bl1t"])
        P.op("gpsimd", lambda g: g.memset(vpb[:, :, :, 64:65], 1.0), writes=["vpones"])
        P.op("vector", lambda v: v.memset(kvsb[:, :, :], 0.0), writes=["kvsb"])

        def phi_evac(psum_ap, out_ap, out_key, bias_col, pbkey, pr, bkey="bqt"):
            if bias_col is None:
                P.op("scalar", lambda sx: sx.activation(ph1[:, pr, :], psum_ap, AF.Relu),
                     reads=[pbkey], writes=[("ph1", pr)], tag="prelu")
                P.op("scalar", lambda sx: sx.activation(ph2[:, pr, :], psum_ap, AF.Exp),
                     reads=[pbkey], writes=[("ph2", pr)], tag="pexp")
            else:
                P.op("scalar", lambda sx: sx.activation(ph1[:, pr, :], psum_ap, AF.Relu,
                                                        bias=bias_col, scale=1.0),
                     reads=[pbkey, bkey], writes=[("ph1", pr)], tag="prelu")
                P.op("scalar", lambda sx: sx.activation(ph2[:, pr, :], psum_ap, AF.Exp,
                                                        bias=bias_col, scale=1.0),
                     reads=[pbkey, bkey], writes=[("ph2", pr)], tag="pexp")
            P.op("vector", lambda v: v.scalar_tensor_tensor(out_ap, ph2[:, pr, :], 1.0, ph1[:, pr, :],
                                                            ALU.min, ALU.add),
                 reads=[("ph1", pr), ("ph2", pr)], writes=[out_key], tag="phi")

        for it in range(NT_O):
            c0 = it * TT
            P.dma("x1r", x1rd[:, :, :], x1_d[:, :, c0:c0 + TT].rearrange("c p t -> p c t"),
                  reads=[("x1d", 2 * it), ("x1d", 2 * it + 1)], writes=["x1rd"], tag="x1ld")
            for dc in range(8):
                P.op("tensor", lambda te, dc=dc: te.matmul(pb[0][0:1, :], ONE_COL, x1rd[:, dc, :],
                                                           start=(dc == 0), stop=(dc == 7)),
                     reads=["x1rd", "ones"], accum=("pb0", dc == 0, dc == 7), tag="dsum")
            for dc in range(8):
                pr = dc % 2
                P.op("gpsimd", lambda g, dc=dc, pr=pr: g.tensor_tensor(ph3[:, pr, :], x1rd[:, dc, :], x1rd[:, dc, :], ALU.mult),
                     reads=["x1rd"], writes=[("ph3", pr)], tag="dsq")
                P.op("tensor", lambda te, dc=dc, pr=pr: te.matmul(pb[1][0:1, :], ONE_COL, ph3[:, pr, :],
                                                                  start=(dc == 0), stop=(dc == 7)),
                     reads=[("ph3", pr), "ones"], accum=("pb1", dc == 0, dc == 7), tag="dsumsq")
            m_ap, var_ap, s_ap, r_ap = (rowt[0:1, i, :] for i in range(4))
            m_bf, r_bf = rowb[0:1, 0, :], rowb[0:1, 1, :]
            P.op("vector", lambda v: v.tensor_scalar(m_ap, pb[0][0:1, :], 1.0 / D, None, ALU.mult),
                 reads=["pb0"], writes=[("rowt", 0)])
            P.op("vector", lambda v: v.tensor_tensor(var_ap, m_ap, m_ap, ALU.mult),
                 reads=[("rowt", 0)], writes=[("rowt", 1)])
            P.op("vector", lambda v: v.scalar_tensor_tensor(var_ap, pb[1][0:1, :], 1.0 / D, var_ap, ALU.mult, ALU.subtract),
                 reads=["pb1", ("rowt", 1)], writes=[("rowt", 1)])
            P.op("scalar", lambda sx: sx.activation(s_ap, var_ap, AF.Ln, bias=CCr("eps_ln"), scale=1.0),
                 reads=[("rowt", 1), "consts"], writes=[("rowt", 2)])
            P.op("scalar", lambda sx: sx.activation(r_ap, s_ap, AF.Exp, scale=-0.5),
                 reads=[("rowt", 2)], writes=[("rowt", 3)])
            P.op("vector", lambda v: v.tensor_copy(m_bf, m_ap), reads=[("rowt", 0)], writes=[("rowb", 0)])
            P.op("vector", lambda v: v.tensor_copy(r_bf, r_ap), reads=[("rowt", 3)], writes=[("rowb", 1)])
            P.op("tensor", lambda te: te.matmul(pb[2][:, :], ONE_ROW, m_bf, start=True, stop=True),
                 reads=[("rowb", 0), "ones"], writes=["pb2"])
            P.op("tensor", lambda te: te.matmul(pb[3][:, :], ONE_ROW, r_bf, start=True, stop=True),
                 reads=[("rowb", 1), "ones"], writes=["pb3"])
            P.op("scalar", lambda sx: sx.activation(mbd[:, :], pb[2][:, :], AF.Copy),
                 reads=["pb2"], writes=["mbd"])
            P.op("scalar", lambda sx: sx.activation(rbd[:, :], pb[3][:, :], AF.Copy),
                 reads=["pb3"], writes=["rbd"])
            for dc in range(8):
                eng = "vector" if dc % 2 == 0 else "gpsimd"
                P.op(eng, lambda e, dc=dc: e.tensor_tensor(xn1[:, dc, :], x1rd[:, dc, :], mbd[:, :], ALU.subtract),
                     reads=["x1rd", "mbd"], writes=[("xn1", dc)])
                P.op(eng, lambda e, dc=dc: e.tensor_tensor(xn1[:, dc, :], xn1[:, dc, :], rbd[:, :], ALU.mult),
                     reads=[("xn1", dc), "rbd"], writes=[("xn1", dc)])
            for oc in range(8):
                bank = oc % 2
                for dc in range(8):
                    P.op("tensor", lambda te, oc=oc, dc=dc, bank=bank: te.matmul(
                            pb[bank][:, :], wqT[:, dc, oc * 128:(oc + 1) * 128], xn1[:, dc, :],
                            start=(dc == 0), stop=(dc == 7)),
                         reads=[("xn1", dc), ("w1", "wq")], accum=(f"pb{bank}", dc == 0, dc == 7), tag="qpmm")
                phi_evac(pb[bank][:, :], qp[:, oc, c0:c0 + TT], ("qp", oc, it), bq_t[:, oc:oc + 1],
                         f"pb{bank}", oc % 2)
            for s4 in range(4):
                sr = s4 % 2
                for oh in range(2):
                    bank = 2 + oh
                    for dc in range(8):
                        P.op("tensor", lambda te, s4=s4, oh=oh, dc=dc, bank=bank: te.matmul(
                                pb[bank][:, :], xn1[:, dc, s4 * 128:(s4 + 1) * 128], wkT[:, dc, oh * 512:(oh + 1) * 512],
                                start=(dc == 0), stop=False),
                             reads=[("xn1", dc), ("w1", "wk")], accum=(f"pb{bank}", dc == 0, False), tag="kpmm")
                    P.op("tensor", lambda te, oh=oh, bank=bank: te.matmul(
                            pb[bank][:, :], ONE_ROW, bkvb[0:1, 0, oh * 512:(oh + 1) * 512],
                            start=False, stop=True),
                         reads=["bkvb", "ones"], accum=(f"pb{bank}", False, True), tag="kbias")
                    phi_evac(pb[bank][:, :], kpb[:, sr, oh * 512:(oh + 1) * 512], ("kpb", sr),
                             None, f"pb{bank}", oh)
                    bank2 = 2 + oh
                    for dc in range(8):
                        P.op("tensor", lambda te, s4=s4, oh=oh, dc=dc, bank2=bank2: te.matmul(
                                pb[bank2][:, :], xn1[:, dc, s4 * 128:(s4 + 1) * 128], wvT[:, dc, oh * 512:(oh + 1) * 512],
                                start=(dc == 0), stop=False),
                             reads=[("xn1", dc), ("w1", "wv")], accum=(f"pb{bank2}", dc == 0, False), tag="vpmm")
                    P.op("tensor", lambda te, oh=oh, bank2=bank2: te.matmul(
                            pb[bank2][:, :], ONE_ROW, bkvb[0:1, 1, oh * 512:(oh + 1) * 512],
                            start=False, stop=True),
                         reads=["bkvb", "ones"], accum=(f"pb{bank2}", False, True), tag="vbias")
                    P.op("scalar", lambda sx, sr=sr, oh=oh, bank2=bank2: sx.activation(
                            vpb[:, sr, oh * 8:(oh + 1) * 8, 0:64], pb[bank2][:, :], AF.Copy),
                         reads=[f"pb{bank2}"], writes=[("vpb", sr, oh)], tag="vev")
                for h in range(16):
                    bank = 4 + (h % 4)
                    colof = (h // 4) * 128
                    P.op("tensor", lambda te, h=h, sr=sr, bank=bank, colof=colof, s4=s4: te.matmul(
                            pb[bank][0:64, colof:colof + 65], kpb[:, sr, h * 64:(h + 1) * 64], vpb[:, sr, h, :],
                            start=(s4 == 0 and h // 4 == 0), stop=(s4 == 3 and h // 4 == 3),
                            skip_group_check=True),
                         reads=[("kpb", sr), ("vpb", sr, h // 8), "vpones"],
                         accum=(f"pb{bank}", s4 == 0 and h // 4 == 0, s4 == 3 and h // 4 == 3), tag="kvmm")
            for h in range(16):
                bank = 4 + (h % 4)
                colof = (h // 4) * 128
                P.op("vector", lambda v, h=h, bank=bank, colof=colof: v.scalar_tensor_tensor(
                        kvsb[:, h, :], pb[bank][0:64, colof:colof + 65], 0.0, kvsb[:, h, :], ALU.max, ALU.add),
                     reads=[f"pb{bank}", "kvsb"], writes=["kvsb"], tag="kvacc")
            for dc in range(8):
                pr = dc % 2
                P.op("gpsimd", lambda g, dc=dc, pr=pr: g.tensor_tensor(ph1[:, pr, :], xn1[:, dc, :], xn1[:, dc, :], ALU.mult),
                     reads=[("xn1", dc)], writes=[("ph1", pr)], tag="lsq")
                P.op("scalar", lambda sx, pr=pr: sx.activation(ph2[:, pr, :], ph1[:, pr, :], AF.Ln, bias=CC("eps2"), scale=1.0),
                     reads=[("ph1", pr), "consts"], writes=[("ph2", pr)], tag="lln")
                P.op("vector", lambda v, pr=pr: v.tensor_scalar(ph5[:, pr, :], ph2[:, pr, :], 0.5 / LN2, -0.5, ALU.mult, ALU.add),
                     reads=[("ph2", pr)], writes=[("ph5", pr)], tag="lei")
                P.op("scalar", lambda sx, pr=pr: sx.activation(ph6[:, pr, :], ph5[:, pr, :], AF.Relu, bias=CC("six"), scale=1.0),
                     reads=[("ph5", pr), "consts"], writes=[("ph6", pr)], tag="lrelu")
                P.op("scalar", lambda sx, pr=pr: sx.activation(ph3[:, pr, :], ph6[:, pr, :], AF.Exp, bias=CC("qexp"), scale=LN2),
                     reads=[("ph6", pr), "consts"], writes=[("ph3", pr)], tag="lq")
                P.op("scalar", lambda sx, pr=pr: sx.activation(ph7[:, pr, :], ph6[:, pr, :], AF.Exp, bias=CC("rexp"), scale=-LN2),
                     reads=[("ph6", pr), "consts"], writes=[("ph7", pr)], tag="lr")
                P.op("gpsimd", lambda g, dc=dc, pr=pr: g.tensor_tensor(ph2[:, pr, :], xn1[:, dc, :], ph7[:, pr, :], ALU.mult),
                     reads=[("xn1", dc), ("ph7", pr)], writes=[("ph2", pr)], tag="lu")
                P.op("vector", lambda v, pr=pr: v.tensor_scalar(ph5[:, pr, :], ph2[:, pr, :], 0.0, None, ALU.add),
                     reads=[("ph2", pr)], writes=[("ph5", pr)], tag="lk")
                P.op("vector", lambda v, pr=pr: v.tensor_scalar(ph6[:, pr, :], ph5[:, pr, :], 0.0, None, ALU.add),
                     reads=[("ph5", pr)], writes=[("ph6", pr)], tag="lkf")
                P.op("gpsimd", lambda g, dc=dc, pr=pr: g.tensor_tensor(xqt[:, dc, :], ph6[:, pr, :], ph3[:, pr, :], ALU.mult),
                     reads=[("ph6", pr), ("ph3", pr)], writes=[("xqt", dc)], tag="lxq")
            for oc in range(8):
                bank = oc % 2
                for dc in range(8):
                    P.op("tensor", lambda te, oc=oc, dc=dc, bank=bank: te.matmul(
                            pb[bank][:, :], wl1T[:, dc, oc * 128:(oc + 1) * 128], xqt[:, dc, :],
                            start=(dc == 0), stop=(dc == 7)),
                         reads=[("xqt", dc), ("w1", "wl")], accum=(f"pb{bank}", dc == 0, dc == 7), tag="f1mm")
                P.op("scalar", lambda sx, oc=oc, bank=bank: sx.activation(
                        ffnt[:, oc % 2, :], pb[bank][:, :], AF.Identity, bias=bl1_t[:, oc:oc + 1], scale=1.0),
                     reads=[f"pb{bank}", "bl1t"], writes=[("ffnt", oc % 2)], tag="f1ev")
                eng = "vector" if oc % 2 == 0 else "gpsimd"
                P.op(eng, lambda e, oc=oc: e.tensor_tensor(x1rd[:, oc, :], x1rd[:, oc, :], ffnt[:, oc % 2, :], ALU.add),
                     reads=["x1rd", ("ffnt", oc % 2)], writes=["x1rd"], tag="f1add")
            P.dma("x1ps", x1p_d[:, :, c0:c0 + TT].rearrange("c p t -> p c t"), x1rd[:, :, :],
                  reads=["x1rd"], writes=[("x1pd", it)], tag="x1pst")
        P.dma("kvcin", kvin_d[:, :], kvsb[:, :, :].rearrange("p h e -> p (h e)"),
              reads=["kvsb"], writes=["kvin"], tag="kvin")
        P.op("gpsimd", lambda g: g.collective_compute(
                "AllReduce", ALU.add, replica_groups=[[0, 1], [2, 3], [4, 5], [6, 7]],
                ins=[dram["kvin"]], outs=[dram["kvout"]]),
             reads=["kvin"], writes=["kvout"], axis="coll", tag="allreduce")

    if st >= 4:
        P.fence("D->E")
        kvo3 = dram["kvout"].rearrange("p (h e) -> p h e", e=65)
        P.op("vector", lambda v: v.memset(kvbd[:, :, :], 0.0),
             writes=[("kvbd", h) for h in range(16)], tag="kvbd0")
        P.op("vector", lambda v: v.memset(kvzd[:, :, :], 0.0),
             writes=[("kvzd", h) for h in range(16)], tag="kvzd0")
        for h in range(16):
            P.dma("kvcout", kvbd[(h % 2) * 64:(h % 2) * 64 + 64, h // 2, :][:, 0:64],
                  kvo3[:, h, 0:64], queue="gpsimd", reads=["kvout"], writes=[("kvbd", h)], tag="kvbd")
            P.dma("kvcout", kvzd[(h % 2) * 64:(h % 2) * 64 + 64, h // 2, (h % 2):(h % 2) + 1],
                  kvo3[:, h, 64:65], queue="gpsimd", reads=["kvout"], writes=[("kvzd", h)],
                  tag="kvzd", slow_ok=True)
        P.dma("einit", bo_tE[:, :], bo_t_d, writes=["boE"])
        P.dma("einit", blk2[:, :], blk2_d, writes=["blk2"])
        P.dma("einitg", woT[:, :, :], woT_d.rearrange("(c p) t -> p c t", p=128),
              queue="gpsimd", writes=[("w1", "wo")])
        for it in range(NT_O):
            c0 = it * TT
            P.dma("x1pr", x1prd[:, :, :], x1p_d[:, :, c0:c0 + TT].rearrange("c p t -> p c t"),
                  reads=[("x1pd", it)], writes=["x1prd"], tag="x1pld")
            for m in range(8):
                bank = m % 2
                P.op("tensor", lambda te, m=m, bank=bank, c0=c0: te.matmul(
                        pb[bank][:, :], kvbd[:, m, :], qp[:, m, c0:c0 + TT], start=True, stop=True),
                     reads=[("kvbd", 2 * m), ("kvbd", 2 * m + 1), ("qp", m, it)], writes=[f"pb{bank}"], tag="apre")
                P.op("tensor", lambda te, m=m, c0=c0: te.matmul(
                        pb[2][0:2, :], kvzd[:, m, :], qp[:, m, c0:c0 + TT], start=True, stop=True),
                     reads=[("kvzd", 2 * m), ("kvzd", 2 * m + 1), ("qp", m, it)], writes=["pb2"], tag="anrm")
                P.op("vector", lambda v: v.tensor_scalar(rbe[0:2, :], pb[2][0:2, :], 1e-6, None, ALU.add),
                     reads=["pb2"], writes=[("rbe", 0)], tag="eps")
                P.op("scalar", lambda sx: sx.activation(rbef[0:2, :], rbe[0:2, :], AF.Ln),
                     reads=[("rbe", 0)], writes=[("rbe", 1)])
                P.op("scalar", lambda sx: sx.activation(rbef[0:2, :], rbef[0:2, :], AF.Exp, scale=-1.0),
                     reads=[("rbe", 1)], writes=[("rbe", 1)])
                P.op("vector", lambda v: v.tensor_copy(nrb[:, :], rbef[0:2, :]),
                     reads=[("rbe", 1)], writes=["nrb"])
                P.op("tensor", lambda te: te.matmul(pb[3][:, :], blk2[:, :], nrb[:, :], start=True, stop=True),
                     reads=["nrb", "blk2"], writes=["pb3"], tag="ebc")
                P.op("scalar", lambda sx: sx.activation(rbef[:, :], pb[3][:, :], AF.Copy),
                     reads=["pb3"], writes=["rbef"], tag="ebce")
                P.op("vector", lambda v, m=m, bank=bank: v.tensor_tensor(
                        nrm[:, m, :], pb[bank][:, :], rbef[:, :], ALU.mult),
                     reads=[f"pb{bank}", "rbef"], writes=[("nrm", m)], tag="enorm")
            for do in range(8):
                bank = 4 + do % 2
                for oc in range(8):
                    P.op("tensor", lambda te, do=do, oc=oc, bank=bank: te.matmul(
                            pb[bank][:, :], woT[:, oc, do * 128:(do + 1) * 128], nrm[:, oc, :],
                            start=(oc == 0), stop=(oc == 7)),
                         reads=[("nrm", oc), ("w1", "wo")], accum=(f"pb{bank}", oc == 0, oc == 7), tag="womm2")
                P.op("scalar", lambda sx, do=do, bank=bank: sx.activation(
                        aev[:, do % 2, :], pb[bank][:, :], AF.Identity, bias=bo_tE[:, do:do + 1], scale=1.0),
                     reads=[f"pb{bank}", "boE"], writes=[("aev", do % 2)], tag="woev")
                P.op("vector", lambda v, do=do: v.tensor_tensor(
                        outb[:, do % 2, :], x1prd[:, do, :], aev[:, do % 2, :], ALU.add),
                     reads=["x1prd", ("aev", do % 2)], writes=[("outbf", do % 2)], tag="finadd")
                P.dma(f"yst{do % 2}", y_d[do, :, c0:c0 + TT], outb[:, do % 2, :],
                      reads=[("outbf", do % 2)], writes=[("out", "y", it, do)], tag="yst")

    P.barrier("sync", [k for k in P.last_writer if isinstance(k, tuple) and k[0] == "out"])
    em = P.finalize_and_emit(nc)
    em.close()
    stack.close()
    return nc, P, outs


def own_token_idx(p):
    return np.concatenate([np.arange((2 * i + p) * TT, (2 * i + p + 1) * TT) for i in range(8)])


def host_prep_core(I, core):
    b, p = core // 2, core % 2
    x = I["L0_attn_x"]
    xfull = np.ascontiguousarray(x[b].T)
    xown = np.ascontiguousarray(x[b].T[:, own_token_idx(p)])
    l0qkv = np.ascontiguousarray(np.concatenate(
        [I["L0_attn_Wq"], I["L0_attn_Wk"], I["L0_attn_Wv"]], axis=1))
    consts = np.zeros((128, NCONST), dtype=np.float32)
    for name, (col, val) in CONST_COLS.items():
        consts[:, col] = val
    onesbf = np.ones((128, 576), dtype=BF)
    onesf = np.ones((1, 128), dtype=np.float32)
    kk = np.arange(128)[:, None]
    tq = np.arange(TT)[None, :]
    diag = [(128 * m + kk <= tq).astype(np.float32) for m in range(4)]
    if p == 0:
        mlist = diag + [np.zeros((128, TT), np.float32)] * 4
    else:
        mlist = [np.ones((128, TT), np.float32)] * 4 + diag
    masks = np.stack(mlist, axis=1).astype(BF)
    d = dict(xfull=xfull, xown=xown, l0qkv=l0qkv, consts=consts,
             onesbf=onesbf, onesf=onesf, masks=masks,
             l0wo=np.ascontiguousarray(I["L0_attn_Wo"]),
             wfcT=np.ascontiguousarray(I["L0_ffn_W_fc"].T),
             wpjT=np.ascontiguousarray(I["L0_ffn_W_proj"].T),
             afc=np.ascontiguousarray(I["L0_ffn_A_fc"]),
             bfcT=np.ascontiguousarray(I["L0_ffn_B_fc"].T),
             apj=np.ascontiguousarray(I["L0_ffn_A_proj"]),
             bpjT=np.ascontiguousarray(I["L0_ffn_B_proj"].T),
             bfc_t=np.ascontiguousarray(I["L0_ffn_b_fc"].reshape(32, 128).T),
             bpj_t=np.ascontiguousarray(I["L0_ffn_b_proj"].reshape(8, 128).T),
             wqT=np.ascontiguousarray(I["L1_attn_Wq"].T),
             wkT=np.ascontiguousarray(I["L1_attn_Wk"].T),
             wvT=np.ascontiguousarray(I["L1_attn_Wv"].T),
             woT=np.ascontiguousarray(I["L1_attn_Wo"].T),
             bq_t=np.ascontiguousarray(I["L1_attn_bq"].reshape(8, 128).T),
             bkv=np.ascontiguousarray(np.stack([I["L1_attn_bk"], I["L1_attn_bv"]])),
             bo_t=np.ascontiguousarray(I["L1_attn_bo"].reshape(8, 128).T),
             wl1T=np.ascontiguousarray(I["L1_ffn_weight"].T),
             bl1_t=np.ascontiguousarray(I["L1_ffn_bias"].reshape(8, 128).T),
             blk2=np.concatenate([np.repeat([[1, 0]], 64, 1).reshape(1, 128),
                                  np.repeat([[0, 1]], 64, 1).reshape(1, 128)]).astype(BF))
    return d


_BUILD_CACHE = {}


def kernel(**inputs):
    from concourse.bass_utils import run_bass_kernel_spmd
    if "nc" not in _BUILD_CACHE:
        nc, P, _ = build(stage="E")
        _BUILD_CACHE["nc"] = nc
    nc = _BUILD_CACHE["nc"]
    in_maps = [host_prep_core(inputs, c) for c in range(8)]
    res = run_bass_kernel_spmd(nc, in_maps, core_ids=list(range(8)))
    out = np.zeros((4, 8192, 1024), dtype=np.float32)
    for c in range(8):
        b, p = c // 2, c % 2
        yc = np.asarray(res.results[c]["y"]).reshape(1024, 4096)
        out[b, own_token_idx(p), :] = yc.T
    return out


# version 06
# speedup vs baseline: 1.7826x; 1.6419x over previous
COMPUTE_ENGINES = ("tensor", "vector", "scalar", "gpsimd")


class Op:
    __slots__ = ("engine", "axis", "fn", "reads", "writes", "inc", "tag",
                 "dep_ops", "idx", "waits", "signaled", "cum", "G", "C")

    def __init__(self, engine, axis, fn, reads, writes, inc, tag):
        self.engine = engine
        self.axis = axis
        self.fn = fn
        self.reads = tuple(reads)
        self.writes = tuple(writes)
        self.inc = inc
        self.tag = tag
        self.waits = []
        self.signaled = False


class Prog:
    def __init__(self):
        self.ops = []
        self.last_writer = {}
        self.readers = {}
        self.sinks = set()
        self.last_fence = None

    def op(self, engine, fn, reads=(), writes=(), accum=None, tag="", axis=None, inc=1):
        o = Op(engine, axis or engine, fn, reads, writes, inc, tag)
        o.idx = len(self.ops)
        self.ops.append(o)
        deps = []
        for k in o.reads:
            w = self.last_writer.get(k)
            if w is not None:
                deps.append(w)
        for k in o.writes:
            w = self.last_writer.get(k)
            if w is not None:
                deps.append(w)
            for r in self.readers.get(k, ()):
                deps.append(r)
        if self.last_fence is not None and self.last_fence is not o:
            deps.append(self.last_fence)
        if accum is not None:
            key, is_start, is_stop = accum
            if is_start:
                w = self.last_writer.get(key)
                if w is not None:
                    deps.append(w)
                for r in self.readers.get(key, ()):
                    deps.append(r)
            if is_stop:
                self.last_writer[key] = o
                self.readers[key] = []
        seen = set()
        o.dep_ops = []
        for d in deps:
            if d is o or id(d) in seen:
                continue
            seen.add(id(d))
            o.dep_ops.append(d)
        for k in o.reads:
            self.readers.setdefault(k, []).append(o)
        for k in o.writes:
            self.last_writer[k] = o
            self.readers[k] = []
        return o

    def dma(self, family, out, in_, reads=(), writes=(), queue="sync", tag="", slow_ok=False):
        return self.op(queue, lambda eng: eng.dma_start(out=out, in_=in_,
                                                        allow_slow_non_contiguous=slow_ok),
                       reads=reads, writes=writes, tag=tag, axis=family, inc=16)

    def barrier(self, engine, keys, tag="barrier"):
        return self.op(engine, None, reads=keys, tag=tag)

    def fence(self, tag="fence"):
        keys = list(self.last_writer.keys())
        f = self.op("vector", lambda v: v.engine_nop(),
                    reads=keys, writes=keys, tag=tag)
        self.last_fence = f
        return f

    def finalize_and_emit(self, nc):
        for o in self.ops:
            if o.inc == 16 or o.axis == "coll":
                o.signaled = True
            for d in o.dep_ops:
                d.signaled = True
        axis_cum = {}
        for o in self.ops:
            if o.signaled:
                axis_cum[o.axis] = axis_cum.get(o.axis, 0) + o.inc
                o.cum = axis_cum[o.axis]
        def meet(a, b):
            for k, v in b.items():
                if a.get(k, 0) < v:
                    a[k] = v
        eng_G = {}
        for o in self.ops:
            need = {}
            for d in o.dep_ops:
                assert d.idx < o.idx
                meet(need, d.C)
            G_prev = eng_G.get(o.engine, {})
            waits = {}
            for ax, v in need.items():
                if G_prev.get(ax, 0) >= v:
                    continue
                waits[ax] = v
            o.waits = sorted(waits.items())
            G = dict(G_prev)
            meet(G, waits)
            eng_G[o.engine] = G
            C = dict(G)
            if o.signaled:
                v = axis_cum[o.axis] if o.axis in self.sinks else o.cum
                C[o.axis] = max(C.get(o.axis, 0), v)
            o.C = C
        engines = {}
        for o in self.ops:
            engines.setdefault(o.engine, []).append(o)
        self.stats = {e: dict(n_ops=len(v), n_sig=sum(1 for x in v if x.signaled),
                              n_waits=sum(len(x.waits) for x in v))
                      for e, v in engines.items()}
        self.stats["n_axes"] = len(axis_cum)

        import contextlib
        stack = contextlib.ExitStack()
        axes = sorted(axis_cum)
        sems = {a: stack.enter_context(nc.semaphore(f"sem_{a}")) for a in axes}
        blk = stack.enter_context(nc.Block())

        def make_body(eops):
            def body(eng):
                for o in eops:
                    for (ax, v) in o.waits:
                        eng.wait_ge(sems[ax], v)
                    if o.fn is not None:
                        ins = o.fn(eng)
                        if o.signaled:
                            ins.then_inc(sems[o.axis], o.inc)
                    else:
                        assert not o.signaled
            return body

        for ename, eops in engines.items():
            getattr(blk, ename)(make_body(eops))
        return stack


import numpy as np
import ml_dtypes
import concourse.bass as bass
import concourse.mybir as mybir

F32 = mybir.dt.float32
FP16 = mybir.dt.float16
BF16 = mybir.dt.bfloat16
FP8 = mybir.dt.float8e4
I32 = mybir.dt.int32
AF = mybir.ActivationFunctionType
ALU = mybir.AluOpType
AX = mybir.AxisListType.X
BF = ml_dtypes.bfloat16

D = 1024
T = 8192
TOWN = 4096
NT_G = 16
NT_O = 8
TT = 512
TTC = 256
NT_C = TOWN // TTC
DK = 128
LN2 = float(np.log(2.0))
SM_SCALE = float(128 ** -0.5)
KB = 1024

CONST_COLS = {
    "neg16": (0, -16.0),
    "eps_ln": (1, 1e-5),
    "eps_z": (2, 1e-6),
    "six": (4, 6.0),
    "zero": (5, 0.0),
}
NCONST = 8

STAGES = ["A", "B", "C", "D", "E"]


def build(stage="E", dbg_stage=None):
    st = STAGES.index(stage)
    nc = bass.Bass("TRN2", target_bir_lowering=False)
    P = Prog()
    P.sinks = {"init", "dbg", "kvcin", "kvcout", "cinit", "dinit", "dinitg", "einit", "einitg",
               "winit", "mxstfc", "mxldfc", "mxstpj", "mxldpj"}

    dram = {}

    def din(name, shape, dt=F32):
        dram[name] = nc.dram_tensor(name, shape, dt, kind="ExternalInput").ap()
        return dram[name]

    def dint(name, shape, dt=F32):
        dram[name] = nc.dram_tensor(name, shape, dt).ap()
        return dram[name]

    def dout(name, shape, dt=F32):
        dram[name] = nc.dram_tensor(name, shape, dt, kind="ExternalOutput").ap()
        return dram[name]

    xfull = din("xfull", [D, T])
    xown = din("xown", [D, TOWN])
    l0qkv_d = din("l0qkv", [D, 384])
    l0wo_d = din("l0wo", [DK, D])
    consts_d = din("consts", [128, NCONST])
    onesbf_d = din("onesbf", [128, 576], BF16)
    onesf_d = din("onesf", [1, 128])
    masks_d = din("masks", [128, 8, TT], BF16)
    wfcT_d = din("wfcT", [D, 4096])
    wpjT_d = din("wpjT", [4096, D])
    afc_d = din("afc", [16, D])
    bfcT_d = din("bfcT", [16, 4096])
    apj_d = din("apj", [16, 4096])
    bpjT_d = din("bpjT", [16, D])
    bfc_t_d = din("bfc_t", [128, 32])
    bpj_t_d = din("bpj_t", [128, 8])
    wqT_d = din("wqT", [D, D])
    wkT_d = din("wkT", [D, D])
    wvT_d = din("wvT", [D, D])
    woT_d = din("woT", [D, D])
    bq_t_d = din("bq_t", [128, 8])
    bkv_d = din("bkv", [2, D])
    bo_t_d = din("bo_t", [128, 8])
    wl1T_d = din("wl1T", [D, D])
    bl1_t_d = din("bl1_t", [128, 8])
    blk2_d = din("blk2", [2, 128], BF16)

    kd = dint("kd", [64, 128, 128], BF16)
    vd = dint("vd", [64, 128, 128], BF16)
    mr_d = dint("mr_d", [2, NT_O, TT])
    mx_d = dint("mx_d", [2, 128])
    x1_d = dint("x1_d", [8, 128, TOWN], BF16)
    x1p_d = dint("x1p_d", [8, 128, TOWN], BF16)
    kvin_d = dint("kvin", [64, 1040])
    kvout_d = dint("kvout", [64, 1040])
    y_d = dout("y", [8, 128, TOWN])

    base0 = (nc.sbuf_base + 31) & ~31
    pad = base0 - nc.sbuf_base
    TOTAL = 207 * KB + 256
    nc.alloc_sbuf_tensor("arena", [128, TOTAL + pad], mybir.dt.uint8)

    def place(name, shape, dt, off):
        assert off % 32 == 0, (name, off)
        sz = int(np.prod(shape[1:])) * mybir.dt.size(dt)
        assert off + sz <= TOTAL, (name, off, sz, TOTAL)
        return nc.alloc_sbuf_tensor_at(name, shape, dt, offset=base0 + off)

    sc = place("sc", [128, NCONST], F32, 0)
    ones = place("ones", [128, 576], BF16, 32)
    onesf_sb = place("onesf_sb", [1, 128], F32, 1184)
    wo_sb = place("wo_sb", [128, 8, 128], BF16, 1696)
    w1r = place("w1r", [1, 384], BF16, 3744)
    BIGO = 4512

    xb = place("xb", [128, 3, 8, TT], BF16, BIGO + 0)
    sqb = place("sqb", [128, 2, 8, TT], BF16, BIGO + 24576)
    l0w = place("l0w", [128, 8, 384], BF16, BIGO + 40960)
    l0wf = place("l0wf", [128, 8, 384], F32, BIGO + 47104)
    rbA = place("rbA", [128, 2, TT], BF16, BIGO + 59392)
    kevac = place("kevac", [128, 2, TT], BF16, BIGO + 61440)
    vevac = place("vevac", [128, 2, 4, 128], BF16, BIGO + 63488)
    wtfc = place("wtfc", [128, 8, 4096], FP16, BIGO + 0)
    wtpj = place("wtpj", [128, 32, 1024], FP16, BIGO + 65536)
    qp = place("qp", [128, 8, TOWN], BF16, BIGO + 0)
    wqT = place("wqT", [128, 8, D], BF16, BIGO + 65536)
    wkT = place("wkT", [128, 8, D], BF16, BIGO + 81920)
    wvT = place("wvT", [128, 8, D], BF16, BIGO + 98304)
    wl8 = place("wl8", [128, 8, D], FP8, BIGO + 114688)
    wstgD = place("wstgD", [128, 2, D], F32, BIGO + 122880)
    woT = place("woT", [128, 8, D], BF16, BIGO + 65536)

    SCRO = BIGO + 131072
    rowt = place("rowt", [1, 4, TT], F32, SCRO + 0)
    rowb = place("rowb", [1, 4, TT], BF16, SCRO + 8192)
    rcol = place("rcol", [128, 2, 4], F32, SCRO + 12288)
    DW0 = SCRO + 12320
    masks = place("masks", [128, 8, TT], BF16, DW0 + 0)
    q_sb = place("q_sb", [128, NT_O, TT], BF16, DW0 + 8192)
    onorm = place("onorm", [128, NT_O, TT], BF16, DW0 + 16384)
    WKO = DW0 + 24576

    pt = place("pt", [128, 4, TT], BF16, WKO + 0)
    ksb = place("ksb", [128, 4, 4, 128], BF16, WKO + 4096)
    vsb = place("vsb", [128, 4, 4, 128], BF16, WKO + 8192)
    rbb = place("rbb", [128, TT], F32, WKO + 12288)
    wstg = place("wstg", [128, 2, D], F32, WKO + 14336)
    lt_z = place("lt_z", [128, 2, TT], I32, WKO + 22528)
    lt_f = place("lt_f", [128, 2, TT], FP16, WKO + 26624)
    abm = place("abm", [16, 4096], BF16, WKO + 28672)
    abm2 = place("abm2", [16, D], BF16, WKO + 36864)
    mxc = place("mxc", [128, 2], F32, WKO + 38912)
    sclb = place("sclb", [128, 8], F32, WKO + 38944)
    hsb = place("hsb", [128, 32, TTC], FP16, WKO + 0)
    x1t = place("x1t", [128, 2, 8, TTC], BF16, WKO + 16384)
    mbc = place("mbc", [128, 2, TTC], FP16, WKO + 24576)
    rbc = place("rbc", [128, 2, TTC], FP16, WKO + 25600)
    bfc_t = place("bfc_t", [128, 32], F32, WKO + 26624)
    bpj_t = place("bpj_t", [128, 8], F32, WKO + 26752)
    bstg = place("bstg", [128, 64], F32, WKO + 26784)
    xrb_b = place("xrb_b", [128, 8, TTC], F32, WKO + 27040)
    xbt = place("xbt", [128, 2, 8, TTC], FP16, DW0 + 0)
    xrb_a = place("xrb_a", [128, 8, TTC], F32, DW0 + 8192)
    xrb = [xrb_a, xrb_b]
    x1rd = place("x1rd", [128, 2, 8, TT], BF16, DW0 + 0)
    xn1 = place("xn1", [128, 8, TT], BF16, DW0 + 16384)
    xq8 = place("xq8", [128, 2, 8, TT], FP8, DW0 + 24576)
    ph1 = place("ph1", [128, 2, TT], F32, DW0 + 32768)
    ph2 = place("ph2", [128, 2, TT], F32, DW0 + 36864)
    ph3 = place("ph3", [128, 2, TT], BF16, DW0 + 40960)
    mbd = place("mbd", [128, TT], BF16, DW0 + 43008)
    rbd = place("rbd", [128, TT], BF16, DW0 + 44032)
    kpb = place("kpb", [128, 2, D], BF16, DW0 + 45056)
    vpb = place("vpb", [128, 2, 16, 65], BF16, DW0 + 49152)
    kvsb = place("kvsb", [64, 16, 65], F32, DW0 + 53312)
    ffnt = place("ffnt", [128, 2, TT], BF16, DW0 + 57472)
    bq_t = place("bq_t", [128, 8], F32, DW0 + 59520)
    bl1_t = place("bl1_t", [128, 8], F32, DW0 + 59552)
    bkvb = place("bkvb", [1, 2, D], BF16, DW0 + 59584)
    nrm = place("nrm", [128, 2, 8, TT], BF16, DW0 + 0)
    x1prd = place("x1prd", [128, 2, 8, TT], BF16, DW0 + 16384)
    presb = place("presb", [128, 8, TT], BF16, DW0 + 32768)
    outb = place("outb", [128, 2, TT], F32, DW0 + 40960)
    rbe2 = place("rbe2", [2, 2, TT], F32, DW0 + 45056)
    rbf2 = place("rbf2", [2, 2, TT], F32, DW0 + 49152)
    nrb2 = place("nrb2", [2, 2, TT], BF16, DW0 + 53248)
    rbefm = place("rbefm", [128, 2, TT], BF16, DW0 + 55296)
    aev = place("aev", [128, 2, TT], BF16, DW0 + 57344)
    kvbd = place("kvbd", [128, 8, 128], BF16, DW0 + 59392)
    kvzd = place("kvzd", [128, 8, 2], BF16, DW0 + 61440)
    blk2 = place("blk2", [2, 128], BF16, DW0 + 61472)
    bo_tE = place("bo_tE", [128, 8], F32, DW0 + 61728)

    import contextlib
    stack = contextlib.ExitStack()
    PSt = lambda *a: stack.enter_context(nc.psum_tensor(*a))
    pb = [PSt(f"pb{i}", [128, TT], F32) for i in range(8)]

    CC = lambda n: sc[:, CONST_COLS[n][0]:CONST_COLS[n][0] + 1]
    CCr = lambda n: sc[0:1, CONST_COLS[n][0]:CONST_COLS[n][0] + 1]
    ONE_COL = ones[:, 0:1]
    ONE_ROW = ones[0:1, 0:128]
    ONE_1 = ones[0:1, 0:1]

    P.dma("init", sc[:], consts_d, writes=["consts"])
    P.dma("init", ones[:], onesbf_d, writes=["ones"])
    P.dma("init", onesf_sb[:], onesf_d, writes=["onesf"])
    P.dma("init", masks[:], masks_d, writes=["masks"])
    for dc in range(8):
        P.dma("init", l0wf[:, dc, :], l0qkv_d[dc * 128:(dc + 1) * 128, :], writes=[("l0wf", dc)])
    for dc in range(8):
        P.op("vector", lambda v, dc=dc: v.tensor_copy(l0w[:, dc, :], l0wf[:, dc, :]),
             reads=[("l0wf", dc)], writes=[("l0w", dc)])
    for dc in range(8):
        P.op("tensor", lambda te, dc=dc: te.matmul(pb[2][0:1, 0:384], ONE_COL, l0w[:, dc, :],
                                                   start=(dc == 0), stop=(dc == 7)),
             reads=[("l0w", dc), "ones"], accum=("pb2", dc == 0, dc == 7), tag="w1mm")
    P.op("scalar", lambda s: s.activation(w1r[0:1, :], pb[2][0:1, 0:384], AF.Copy),
         reads=["pb2"], writes=["w1r"])

    xsrc = xfull.rearrange("(c p) t -> p c t", p=128)

    def ldx(t):
        r3 = t % 3
        P.dma(f"x{r3}", xb[:, r3, :, :], xsrc[:, :, t * TT:(t + 1) * TT], queue="gpsimd",
              writes=[("xb", r3, dc) for dc in range(8)], tag="ldx")

    ldx(0)
    ldx(1)

    prevproj = [None]
    for t in range(NT_G + 1):
        rot = t % 2
        r3 = t % 3
        if t < NT_G:
            for dc in range(8):
                eng = "vector" if dc % 2 == 0 else "gpsimd"
                P.op(eng, lambda e, dc=dc, rot=rot, r3=r3: e.tensor_tensor(
                        sqb[:, rot, dc, :], xb[:, r3, dc, :], xb[:, r3, dc, :], ALU.mult),
                     reads=[("xb", r3, dc)], writes=[("sq", rot, dc)], tag="sq")
            for dc in range(8):
                P.op("tensor", lambda te, dc=dc, r3=r3: te.matmul(pb[0][0:1, :], ONE_COL, xb[:, r3, dc, :],
                                                                  start=(dc == 0), stop=(dc == 7)),
                     reads=[("xb", r3, dc), "ones"], accum=("pb0", dc == 0, dc == 7), tag="summ")
            for dc in range(8):
                P.op("tensor", lambda te, dc=dc, rot=rot: te.matmul(pb[1][0:1, :], ONE_COL, sqb[:, rot, dc, :],
                                                                    start=(dc == 0), stop=(dc == 7)),
                     reads=[("sq", rot, dc), "ones"], accum=("pb1", dc == 0, dc == 7), tag="sumsq")
        if prevproj[0] is not None:
            prevproj[0]()
            prevproj[0] = None
        if t < NT_G:
            m_ap, var_ap, s_ap, r_ap = (rowt[0:1, i, :] for i in range(4))
            P.op("vector", lambda v: v.tensor_scalar(m_ap, pb[0][0:1, :], 1.0 / D, None, ALU.mult),
                 reads=["pb0"], writes=[("rowt", 0)])
            P.op("vector", lambda v: v.tensor_tensor(var_ap, m_ap, m_ap, ALU.mult),
                 reads=[("rowt", 0)], writes=[("rowt", 1)])
            P.op("vector", lambda v: v.scalar_tensor_tensor(var_ap, pb[1][0:1, :], 1.0 / D, var_ap, ALU.mult, ALU.subtract),
                 reads=["pb1", ("rowt", 1)], writes=[("rowt", 1)])
            P.op("scalar", lambda s: s.activation(s_ap, var_ap, AF.Ln, bias=CCr("eps_ln"), scale=1.0),
                 reads=[("rowt", 1), "consts"], writes=[("rowt", 2)])
            P.op("scalar", lambda s: s.activation(r_ap, s_ap, AF.Exp, scale=-0.5),
                 reads=[("rowt", 2)], writes=[("rowt", 3)])
            if t < NT_O:
                P.dma("mrstm", mr_d[0, t, :], m_ap, reads=[("rowt", 0)], writes=[("mrd", 0, t)])
                P.dma("mrstr", mr_d[1, t, :], r_ap, reads=[("rowt", 3)], writes=[("mrd", 1, t)])
            P.op("vector", lambda v: v.tensor_scalar(rowb[0:1, 0, :], m_ap, -1.0, None, ALU.mult),
                 reads=[("rowt", 0)], writes=[("rowb", 0)])
            P.op("vector", lambda v: v.tensor_copy(rowb[0:1, 1, :], r_ap),
                 reads=[("rowt", 3)], writes=[("rowb", 1)])
            P.op("tensor", lambda te: te.matmul(pb[3][:, :], ONE_ROW, rowb[0:1, 1, :], start=True, stop=True),
                 reads=[("rowb", 1), "ones"], writes=["pb3"])
            P.op("scalar", lambda s, rot=rot: s.activation(rbA[:, rot, :], pb[3][:, :], AF.Copy),
                 reads=["pb3"], writes=[("rbA", rot)])
            for s4 in range(4):
                P.op("tensor", lambda te, s4=s4: te.matmul(pb[2][:, s4:s4 + 1], rowb[0:1, 1, s4 * 128:(s4 + 1) * 128],
                                                           ONE_1, start=True, stop=True),
                     reads=[("rowb", 1), "ones"], writes=["pb2"], tag="rcolmm")
            P.op("scalar", lambda s, rot=rot: s.activation(rcol[:, rot, :], pb[2][:, 0:4], AF.Copy),
                 reads=["pb2"], writes=[("rcol", rot)])

            def mk_proj(t=t, rot=rot, r3=r3):
                def proj():
                    for dc in range(8):
                        P.op("tensor", lambda te, dc=dc: te.matmul(pb[4][:, :], l0w[:, dc, 128:256], xb[:, r3, dc, :],
                                                                   start=(dc == 0), stop=False),
                             reads=[("xb", r3, dc), ("l0w", dc)], accum=("pb4", dc == 0, False), tag="kmm")
                    P.op("tensor", lambda te: te.matmul(pb[4][:, :], w1r[0:1, 128:256], rowb[0:1, 0, :],
                                                        start=False, stop=True),
                         reads=["w1r", ("rowb", 0)], accum=("pb4", False, True), tag="kout")
                    P.op("scalar", lambda s: s.activation(kevac[:, rot, :], pb[4][:, :], AF.Copy),
                         reads=["pb4"], writes=[("kevac", rot)])
                    P.op("vector", lambda v: v.tensor_tensor(kevac[:, rot, :], kevac[:, rot, :], rbA[:, rot, :], ALU.mult),
                         reads=[("kevac", rot), ("rbA", rot)], writes=[("kevac", rot)])
                    P.dma(f"kst{rot}", kd[4 * t:4 * t + 4, :, :].rearrange("j p t -> p j t"), kevac[:, rot, :],
                          reads=[("kevac", rot)], writes=[("kdt", t)])
                    for s4 in range(4):
                        bank = 5 + s4 % 2
                        for dc in range(8):
                            P.op("tensor", lambda te, dc=dc, s4=s4, bank=bank: te.matmul(
                                    pb[bank][:, 0:128], xb[:, r3, dc, s4 * 128:(s4 + 1) * 128], l0w[:, dc, 256:384],
                                    start=(dc == 0), stop=False),
                                 reads=[("xb", r3, dc), ("l0w", dc)], accum=(f"pb{bank}", dc == 0, False), tag="vmm")
                        P.op("tensor", lambda te, s4=s4, bank=bank: te.matmul(
                                pb[bank][:, 0:128], rowb[0:1, 0, s4 * 128:(s4 + 1) * 128], w1r[0:1, 256:384],
                                start=False, stop=True),
                             reads=[("rowb", 0), "w1r"], accum=(f"pb{bank}", False, True), tag="vout")
                        P.op("scalar", lambda s, s4=s4, bank=bank: s.activation(
                                vevac[:, rot, s4, :], pb[bank][:, 0:128], AF.Copy, scale=rcol[:, rot, s4:s4 + 1]),
                             reads=[f"pb{bank}", ("rcol", rot)], writes=[("vevac", rot, s4)])
                    P.dma(f"vst{rot}", vd[4 * t:4 * t + 4, :, :].rearrange("j p t -> p j t"), vevac[:, rot, :, :],
                          reads=[("vevac", rot, s4) for s4 in range(4)], writes=[("vdt", t)])
                    if t < NT_O:
                        for dc in range(8):
                            P.op("tensor", lambda te, dc=dc: te.matmul(pb[7][:, :], l0w[:, dc, 0:128], xb[:, r3, dc, :],
                                                                       start=(dc == 0), stop=False),
                                 reads=[("xb", r3, dc), ("l0w", dc)], accum=("pb7", dc == 0, False), tag="qmm")
                        P.op("tensor", lambda te: te.matmul(pb[7][:, :], w1r[0:1, 0:128], rowb[0:1, 0, :],
                                                            start=False, stop=True),
                             reads=["w1r", ("rowb", 0)], accum=("pb7", False, True), tag="qout")
                        P.op("scalar", lambda s: s.activation(q_sb[:, t, :], pb[7][:, :], AF.Copy),
                             reads=["pb7"], writes=[("q", t)])
                        P.op("vector", lambda v: v.tensor_tensor(q_sb[:, t, :], q_sb[:, t, :], rbA[:, rot, :], ALU.mult),
                             reads=[("q", t), ("rbA", rot)], writes=[("q", t)])
                return proj
            prevproj[0] = mk_proj()
            if t + 2 < NT_G:
                ldx(t + 2)

    outs = []
    if dbg_stage == "A":
        q_out = dout("q_out", [128, TOWN], BF16)
        mr_out = dout("mr_out", [2, NT_O, TT])
        k_out = dout("k_out", [64, 128, 128], BF16)
        v_out = dout("v_out", [64, 128, 128], BF16)
        for t in range(NT_O):
            P.dma("dbg", q_out[:, t * TT:(t + 1) * TT], q_sb[:, t, :],
                  reads=[("q", t)], writes=[("out", "q", t)])
        P.dma("dbg", mr_out[:, :, :], mr_d[:, :, :],
              reads=[("mrd", i, t) for i in range(2) for t in range(NT_O)], writes=[("out", "mr")])
        P.dma("dbg", k_out[:, :, :], kd[:, :, :], reads=[("kdt", t) for t in range(NT_G)], writes=[("out", "k")])
        P.dma("dbg", v_out[:, :, :], vd[:, :, :], reads=[("vdt", t) for t in range(NT_G)], writes=[("out", "v")])
        outs = ["q_out", "mr_out", "k_out", "v_out"]

    if st >= 1:
        P.fence("A->B")
        wof = l0wf
        P.dma("winit", wof[:, 0:3, :].rearrange("p c t -> p (c t)")[:, 0:1024], l0wo_d[:, :],
              writes=[("wof",)])
        P.op("vector", lambda v: v.tensor_copy(wo_sb[:, :, :].rearrange("p c t -> p (c t)"),
                                               wof[:, 0:3, :].rearrange("p c t -> p (c t)")[:, 0:1024]),
             reads=[("wof",)], writes=["wo"])

        def ld16(dstslice, key, src_ap):
            P.dma("abld", wstg[0:16, 0, :], src_ap, writes=[("wstg", 0)])
            P.op("vector", lambda v: v.tensor_copy(dstslice, wstg[0:16, 0, :]),
                 reads=[("wstg", 0)], writes=[key])

        wq = []

        def wq_pass1(fam, src_d, nch, ncc, wslice):
            for ch in range(nch):
                for cc in range(ncc):
                    def th(ch=ch, cc=cc, fam=fam, src_d=src_d, wslice=wslice):
                        r = (ch + cc) % 2
                        P.dma(f"w{fam}{r}", wstg[:, r, :], src_d[ch * 128:(ch + 1) * 128, cc * D:(cc + 1) * D],
                              writes=[("wstg", r)], tag="wld")
                        P.op("vector", lambda v: v.tensor_reduce(mxc[:, r:r + 1], wstg[:, r, :], AX, ALU.max,
                                                                 apply_absolute_value=True),
                             reads=[("wstg", r)], writes=[("mxcol", fam, r)], tag="wmx")
                        P.op("vector", lambda v: v.tensor_tensor(sclb[:, 3:4], sclb[:, 3:4], mxc[:, r:r + 1], ALU.max),
                             reads=[("mxcol", fam, r), ("mxacc", fam)], writes=[("mxacc", fam)], tag="wmxa")
                        dst, dkey = wslice(ch, cc)
                        P.op("vector", lambda v: v.tensor_copy(dst, wstg[:, r, :]),
                             reads=[("wstg", r)], writes=[dkey], tag="wcast")
                    wq.append(th)

        def wq_scale(fam):
            def th(fam=fam):
                mslot = 0 if fam == "wfc" else 1
                P.dma("mxstfc" if fam == "wfc" else "mxstpj",
                      mx_d[mslot, :], sclb[:, 3:4], reads=[("mxacc", fam)], writes=[("mxd", fam)])
                P.dma("mxldfc" if fam == "wfc" else "mxldpj",
                      rowt[0:1, 0, 0:128], mx_d[mslot, :], reads=[("mxd", fam)], writes=[("rowt", 0)])
                P.op("vector", lambda v: v.tensor_reduce(rowt[0:1, 1, 0:1], rowt[0:1, 0, 0:128], AX, ALU.max,
                                                         apply_absolute_value=True),
                     reads=[("rowt", 0)], writes=[("rowt", 1)])
                P.op("vector", lambda v: v.tensor_scalar(rowt[0:1, 1, 1:2], rowt[0:1, 1, 0:1], 1.0 / 127.0, None, ALU.mult),
                     reads=[("rowt", 1)], writes=[("rowt", 1)])
                P.op("scalar", lambda sx: sx.activation(rowt[0:1, 1, 2:3], rowt[0:1, 1, 1:2], AF.Ln),
                     reads=[("rowt", 1)], writes=[("rowt", 1)])
                P.op("scalar", lambda sx: sx.activation(rowt[0:1, 1, 2:3], rowt[0:1, 1, 2:3], AF.Exp, scale=-1.0),
                     reads=[("rowt", 1)], writes=[("rowt", 1)])
                P.op("tensor", lambda te: te.matmul(pb[3][:, 0:2], onesf_sb[0:1, :], rowt[0:1, 1, 1:3],
                                                    start=True, stop=True),
                     reads=[("rowt", 1), "onesf"], writes=["pb3"])
                P.op("scalar", lambda s: s.activation(sclb[:, 0:2], pb[3][:, 0:2], AF.Copy),
                     reads=["pb3"], writes=[("sclb", fam)])
                if fam == "wfc":
                    P.op("gpsimd", lambda g: g.memset(sclb[:, 3:4], 0.0),
                         writes=[("mxacc", "wfc"), ("mxacc", "wpj")])
            wq.append(th)

        def wq_pass2(fam, nch, ncc512, wslice512, a_lhs, b_rhs):
            for ch in range(nch):
                for fc in range(ncc512):
                    def th(ch=ch, fc=fc, fam=fam, wslice512=wslice512, a_lhs=a_lhs, b_rhs=b_rhs):
                        r = (ch * ncc512 + fc) % 2
                        wsl, wkey = wslice512(ch, fc)
                        P.op("tensor", lambda te: te.matmul(pb[2][:, :], a_lhs(ch), b_rhs(fc), start=True, stop=True),
                             reads=[("abm",)], writes=["pb2"], tag="lmm")
                        P.op("vector", lambda v: v.tensor_scalar(lt_z[:, r, :], wsl, sclb[:, 1:2], None, ALU.mult),
                             reads=[wkey, ("sclb", fam)], writes=[("ltz", r)], tag="lz")
                        P.op("scalar", lambda s: s.activation(wsl, pb[2][:, :], AF.Copy, scale=2.0),
                             reads=["pb2"], writes=[wkey], tag="lme")
                        P.op("vector", lambda v: v.tensor_scalar(lt_f[:, r, :], lt_z[:, r, :], sclb[:, 0:1], None, ALU.mult),
                             reads=[("ltz", r), ("sclb", fam)], writes=[("ltf", r)], tag="lzs")
                        P.op("gpsimd", lambda g: g.tensor_tensor(wsl, lt_f[:, r, :], wsl, ALU.add),
                             reads=[("ltf", r), wkey], writes=[wkey], tag="lw")
                    wq.append(th)

        def fc_slice1024(ch, cc):
            return wtfc[:, ch, cc * D:(cc + 1) * D], ("wtfc", ch, cc)

        def fc_slice512(ch, fc):
            return wtfc[:, ch, fc * TT:(fc + 1) * TT], ("wtfc", ch, fc // 2)

        def pj_slice1024(ch, cc):
            return wtpj[:, ch, :], ("wtpj", ch)

        def pj_slice512(ch, fc):
            return wtpj[:, ch, fc * TT:(fc + 1) * TT], ("wtpj", ch)

        def afc_lhs(ch):
            return abm2[:, ch * 128:(ch + 1) * 128]

        def bfc_rhs(fc):
            return abm[:, fc * TT:(fc + 1) * TT]

        def apj_lhs(ch):
            return abm[:, ch * 128:(ch + 1) * 128]

        def bpj_rhs(fc):
            return abm2[:, fc * TT:(fc + 1) * TT]

        def fc_mats():
            P.op("gpsimd", lambda g: g.memset(sclb[:, 3:4], 0.0), writes=[("mxacc", "wfc")])
            ld16(abm2[:, :], ("abm",), afc_d[:, :])
            for q in range(4):
                ld16(abm[:, q * D:(q + 1) * D], ("abm",), bfcT_d[:, q * D:(q + 1) * D])

        def pj_mats():
            for q in range(4):
                ld16(abm[:, q * D:(q + 1) * D], ("abm",), apj_d[:, q * D:(q + 1) * D])
            ld16(abm2[:, :], ("abm",), bpjT_d[:, :])

        wq.append(fc_mats)
        wq_pass1("wfc", wfcT_d, 8, 4, fc_slice1024)
        wq_scale("wfc")
        wq_pass2("wfc", 8, 8, fc_slice512, afc_lhs, bfc_rhs)
        wq.append(pj_mats)
        wq_pass1("wpj", wpjT_d, 32, 1, pj_slice1024)
        wq_scale("wpj")
        wq_pass2("wpj", 32, 2, pj_slice512, apj_lhs, bpj_rhs)

        SBANKS = [0, 1, 4, 5]
        total_steps = sum(8 * i + 8 for i in range(NT_O))
        thunk_idx = 0
        step_no = 0
        pending_chain = [None]

        for i in range(NT_O):
            nk = 8 * i + 8
            ntile = 2 * i + 2

            def tile_load(ti, i=i):
                s = ti % 4
                tid = ti if ti <= i else 8 + (ti - i - 1)
                kj0 = 4 * tid
                P.dma(f"kld{s}", ksb[:, s, :, :], kd[kj0:kj0 + 4, :, :].rearrange("j p t -> p j t"),
                      reads=[("kdt", tid)], writes=[("ksb", s)], tag="kld")
                P.dma(f"vld{s}", vsb[:, s, :, :], vd[kj0:kj0 + 4, :, :].rearrange("j p t -> p j t"),
                      reads=[("vdt", tid)], writes=[("vsb", s)], tag="vld")

            def emit_ov(j, i=i, nk=nk, ntile=ntile):
                ti = j // 4
                s = ti % 4
                q = j % 4
                P.op("tensor", lambda te, s=s, q=q, j=j, nk=nk: te.matmul(
                        pb[6][:, :], vsb[:, s, q, :], pt[:, j % 4, :], start=(j == 0), stop=(j == nk - 1)),
                     reads=[("vsb", s), ("pt", j % 4)], accum=("pb6", j == 0, j == nk - 1), tag="omm")
                P.op("tensor", lambda te, j=j, nk=nk: te.matmul(pb[7][0:1, :], ONE_COL, pt[:, j % 4, :],
                                                                start=(j == 0), stop=(j == nk - 1)),
                     reads=[("pt", j % 4), "ones"], accum=("pb7", j == 0, j == nk - 1), tag="dmm")
                if q == 3 and ti + 4 < ntile:
                    tile_load(ti + 4)

            for ti in range(min(4, ntile)):
                tile_load(ti)
            for j in range(nk):
                ti = j // 4
                s = ti % 4
                q = j % 4
                sbank = SBANKS[j % 4]
                P.op("tensor", lambda te, s=s, q=q, sbank=sbank, i=i: te.matmul(
                        pb[sbank][:, :], ksb[:, s, q, :], q_sb[:, i, :], start=True, stop=True),
                     reads=[("ksb", s), ("q", i)], writes=[f"pb{sbank}"], tag="smm")
                P.op("scalar", lambda sx, j=j, sbank=sbank: sx.activation(
                        pt[:, j % 4, :], pb[sbank][:, :], AF.Exp, bias=CC("neg16"), scale=SM_SCALE),
                     reads=[f"pb{sbank}", "consts"], writes=[("pt", j % 4)], tag="exp")
                half = 0 if ti <= i else 1
                tau = ti if half == 0 else ti - i - 1
                if tau == i:
                    mi = q if half == 0 else 4 + q
                    P.op("vector", lambda v, j=j, mi=mi: v.tensor_tensor(
                            pt[:, j % 4, :], pt[:, j % 4, :], masks[:, mi, :], ALU.mult),
                         reads=[("pt", j % 4), "masks"], writes=[("pt", j % 4)], tag="mask")
                if j >= 2:
                    emit_ov(j - 2)
                step_no += 1
                if pending_chain[0] is not None and j == 1:
                    pending_chain[0]()
                    pending_chain[0] = None
                while thunk_idx < len(wq) and thunk_idx * total_steps <= step_no * len(wq):
                    wq[thunk_idx]()
                    thunk_idx += 1
            emit_ov(nk - 2)
            emit_ov(nk - 1)

            def mk_chain(i=i):
                def chain():
                    P.op("scalar", lambda sx: sx.activation(rowt[0:1, 0, :], pb[7][0:1, :], AF.Ln),
                         reads=["pb7"], writes=[("rowt", 0)])
                    P.op("scalar", lambda sx: sx.activation(rowt[0:1, 0, :], rowt[0:1, 0, :], AF.Exp, scale=-1.0),
                         reads=[("rowt", 0)], writes=[("rowt", 0)])
                    P.op("vector", lambda v: v.tensor_copy(rowb[0:1, 2, :], rowt[0:1, 0, :]),
                         reads=[("rowt", 0)], writes=[("rowb", 2)])
                    P.op("tensor", lambda te: te.matmul(pb[3][:, :], ONE_ROW, rowb[0:1, 2, :], start=True, stop=True),
                         reads=[("rowb", 2), "ones"], writes=["pb3"])
                    P.op("scalar", lambda s: s.activation(rbb[:, :], pb[3][:, :], AF.Copy),
                         reads=["pb3"], writes=["rbb"])
                    P.op("vector", lambda v: v.tensor_tensor(onorm[:, i, :], pb[6][:, :], rbb[:, :], ALU.mult),
                         reads=["pb6", "rbb"], writes=[("onorm", i)], tag="onorm")
                return chain
            pending_chain[0] = mk_chain()
        pending_chain[0]()
        pending_chain[0] = None
        while thunk_idx < len(wq):
            wq[thunk_idx]()
            thunk_idx += 1

        if dbg_stage == "B":
            on_out = dout("on_out", [128, TOWN], BF16)
            wt_out = dout("wt_out", [128, 8, 4096], FP16)
            wt2_out = dout("wt2_out", [128, 32, 1024], FP16)
            for i in range(NT_O):
                P.dma("dbg", on_out[:, i * TT:(i + 1) * TT], onorm[:, i, :],
                      reads=[("onorm", i)], writes=[("out", "on", i)])
            P.dma("dbg", wt_out[:, :, :], wtfc[:, :, :],
                  reads=[("wtfc", ch, cc) for ch in range(8) for cc in range(4)], writes=[("out", "wt")])
            P.dma("dbg", wt2_out[:, :, :], wtpj[:, :, :],
                  reads=[("wtpj", ch) for ch in range(32)], writes=[("out", "wt2")])
            outs = ["on_out", "wt_out", "wt2_out"]

    if st >= 2:
        P.fence("B->C")
        P.dma("cinit", bstg[:, 0:32], bfc_t_d, writes=[("bstg", 0)])
        P.op("vector", lambda v: v.tensor_copy(bfc_t[:, :], bstg[:, 0:32]), reads=[("bstg", 0)], writes=["bfct"])
        P.dma("cinit", bstg[:, 32:40], bpj_t_d, writes=[("bstg", 1)])
        P.op("vector", lambda v: v.tensor_copy(bpj_t[:, :], bstg[:, 32:40]), reads=[("bstg", 1)], writes=["bpjt"])
        src3 = xown.rearrange("(c p) t -> p c t", p=128)

        def c_prep(tc):
            rot = tc % 2
            t512 = tc // 2
            cofs = (tc % 2) * TTC
            P.dma(f"mrldm{rot}", rowt[0:1, 2 * rot, 0:TTC], mr_d[0, t512, cofs:cofs + TTC],
                  reads=[("mrd", 0, t512)], writes=[("rowt", 2 * rot)], tag="mrld")
            P.dma(f"mrldr{rot}", rowt[0:1, 2 * rot + 1, 0:TTC], mr_d[1, t512, cofs:cofs + TTC],
                  reads=[("mrd", 1, t512)], writes=[("rowt", 2 * rot + 1)], tag="mrld")
            P.op("vector", lambda v, rot=rot: v.tensor_copy(rowb[0:1, 2 * rot, 0:TTC], rowt[0:1, 2 * rot, 0:TTC]),
                 reads=[("rowt", 2 * rot)], writes=[("rowb", 2 * rot)])
            P.op("vector", lambda v, rot=rot: v.tensor_copy(rowb[0:1, 2 * rot + 1, 0:TTC], rowt[0:1, 2 * rot + 1, 0:TTC]),
                 reads=[("rowt", 2 * rot + 1)], writes=[("rowb", 2 * rot + 1)])
            P.op("tensor", lambda te, rot=rot: te.matmul(pb[2][:, 0:TTC], ONE_ROW, rowb[0:1, 2 * rot, 0:TTC],
                                                         start=True, stop=True),
                 reads=[("rowb", 2 * rot), "ones"], writes=["pb2"])
            P.op("scalar", lambda s, rot=rot: s.activation(mbc[:, rot, :], pb[2][:, 0:TTC], AF.Copy),
                 reads=["pb2"], writes=[("mbc", rot)])
            P.op("tensor", lambda te, rot=rot: te.matmul(pb[3][:, 0:TTC], ONE_ROW, rowb[0:1, 2 * rot + 1, 0:TTC],
                                                         start=True, stop=True),
                 reads=[("rowb", 2 * rot + 1), "ones"], writes=["pb3"])
            P.op("scalar", lambda s, rot=rot: s.activation(rbc[:, rot, :], pb[3][:, 0:TTC], AF.Copy),
                 reads=["pb3"], writes=[("rbc", rot)])
            P.dma(f"xca{rot}", xbt[:, rot, :, :], src3[:, :, tc * TTC:(tc + 1) * TTC],
                  queue="gpsimd", writes=[("xbt", rot, dc) for dc in range(8)], tag="ldxc")
            for dc in range(8):
                eng = "vector" if dc % 2 == 0 else "gpsimd"
                P.op(eng, lambda e, dc=dc, rot=rot: e.tensor_tensor(
                        xbt[:, rot, dc, :], xbt[:, rot, dc, :], mbc[:, rot, :], ALU.subtract),
                     reads=[("xbt", rot, dc), ("mbc", rot)], writes=[("xbt", rot, dc)])
                P.op(eng, lambda e, dc=dc, rot=rot: e.tensor_tensor(
                        xbt[:, rot, dc, :], xbt[:, rot, dc, :], rbc[:, rot, :], ALU.mult),
                     reads=[("xbt", rot, dc), ("rbc", rot)], writes=[("xbt", rot, dc)])
            P.dma(f"xcb{rot}", xrb[rot][:, :, :], src3[:, :, tc * TTC:(tc + 1) * TTC],
                  writes=[("xrb", rot)], tag="ldxr")

        c_prep(0)
        for tc in range(NT_C):
            rot = tc % 2
            t512 = tc // 2
            cofs = (tc % 2) * TTC
            for f in range(32):
                bank = f % 8
                for dc in range(8):
                    P.op("tensor", lambda te, f=f, dc=dc, bank=bank, rot=rot: te.matmul(
                            pb[bank][:, 0:TTC], wtfc[:, dc, f * 128:(f + 1) * 128], xbt[:, rot, dc, :],
                            start=(dc == 0), stop=(dc == 7)),
                         reads=[("xbt", rot, dc), ("wtfc", dc, f // 8)], accum=(f"pb{bank}", dc == 0, dc == 7),
                         tag="fcmm")
                P.op("scalar", lambda s, f=f, bank=bank: s.activation(
                        hsb[:, f, :], pb[bank][:, 0:TTC], AF.Gelu, bias=bfc_t[:, f:f + 1], scale=1.0),
                     reads=[f"pb{bank}", "bfct"], writes=[("h", f)], tag="gelu")
            if tc + 1 < NT_C:
                c_prep(tc + 1)
            for do in range(8):
                bank = do % 8
                for fc in range(32):
                    P.op("tensor", lambda te, do=do, fc=fc, bank=bank: te.matmul(
                            pb[bank][:, 0:TTC], wtpj[:, fc, do * 128:(do + 1) * 128], hsb[:, fc, :],
                            start=(fc == 0), stop=False),
                         reads=[("h", fc), ("wtpj", fc)], accum=(f"pb{bank}", fc == 0, False), tag="pjmm")
                P.op("tensor", lambda te, do=do, bank=bank, t512=t512, cofs=cofs: te.matmul(
                        pb[bank][:, 0:TTC], wo_sb[:, do, :], onorm[:, t512, cofs:cofs + TTC],
                        start=False, stop=True),
                     reads=["wo", ("onorm", t512)], accum=(f"pb{bank}", False, True), tag="womm")
                P.op("scalar", lambda s, do=do, bank=bank, rot=rot: s.activation(
                        x1t[:, rot, do, :], pb[bank][:, 0:TTC], AF.Identity, bias=bpj_t[:, do:do + 1], scale=1.0),
                     reads=[f"pb{bank}", "bpjt"], writes=[("x1t", rot, do)], tag="pjev")
                eng = "vector" if do % 2 == 0 else "gpsimd"
                P.op(eng, lambda e, do=do, rot=rot: e.tensor_tensor(
                        x1t[:, rot, do, :], x1t[:, rot, do, :], xrb[rot][:, do, :], ALU.add),
                     reads=[("x1t", rot, do), ("xrb", rot)], writes=[("x1t", rot, do)], tag="resadd")
            P.dma(f"x1st{rot}", x1_d[:, :, tc * TTC:(tc + 1) * TTC].rearrange("c p t -> p c t"), x1t[:, rot, :, :],
                  reads=[("x1t", rot, do) for do in range(8)], writes=[("x1d", tc)], tag="x1st")

        if dbg_stage == "C":
            x1_out = dout("x1_out", [8, 128, TOWN], BF16)
            P.dma("dbg", x1_out[:, :, :], x1_d[:, :, :],
                  reads=[("x1d", tc) for tc in range(NT_C)], writes=[("out", "x1")])
            outs = ["x1_out"]

    if st >= 3:
        P.fence("C->D")
        for wname, wdst, wsrc in (("wq", wqT, wqT_d), ("wk", wkT, wkT_d), ("wv", wvT, wvT_d)):
            P.dma("dinitg", wdst[:, :, :], wsrc.rearrange("(c p) t -> p c t", p=128),
                  queue="gpsimd", writes=[("w1", wname)])
        for dc in range(8):
            r = dc % 2
            P.dma(f"wl8s{r}", wstgD[:, r, :], wl1T_d[dc * 128:(dc + 1) * 128, :],
                  writes=[("wstgD", r)], tag="wl8ld")
            P.op("vector", lambda v, dc=dc, r=r: v.tensor_copy(wl8[:, dc, :], wstgD[:, r, :]),
                 reads=[("wstgD", r)], writes=[("wl8", dc)], tag="wl8c")
        P.dma("dinitg", bkvb[0:1, :, :], bkv_d.rearrange("a b -> (a b)").rearrange("(a b) -> a b", a=1).rearrange("a (c d) -> a c d", c=2), queue="gpsimd", writes=["bkvb"])
        P.dma("dinit", bq_t[:, :], bq_t_d, writes=["bqt"])
        P.dma("dinit", bl1_t[:, :], bl1_t_d, writes=["bl1t"])
        P.op("gpsimd", lambda g: g.memset(vpb[:, :, :, 64:65], 1.0), writes=["vpones"])
        P.op("vector", lambda v: v.memset(kvsb[:, :, :], 0.0), writes=["kvsb"])

        def phi_evac(psum_ap, out_ap, out_key, bias_col, pbkey, pr, bkey="bqt"):
            if bias_col is None:
                P.op("scalar", lambda sx: sx.activation(ph1[:, pr, :], psum_ap, AF.Relu),
                     reads=[pbkey], writes=[("ph1", pr)], tag="prelu")
                P.op("scalar", lambda sx: sx.activation(ph2[:, pr, :], psum_ap, AF.Exp),
                     reads=[pbkey], writes=[("ph2", pr)], tag="pexp")
            else:
                P.op("scalar", lambda sx: sx.activation(ph1[:, pr, :], psum_ap, AF.Relu,
                                                        bias=bias_col, scale=1.0),
                     reads=[pbkey, bkey], writes=[("ph1", pr)], tag="prelu")
                P.op("scalar", lambda sx: sx.activation(ph2[:, pr, :], psum_ap, AF.Exp,
                                                        bias=bias_col, scale=1.0),
                     reads=[pbkey, bkey], writes=[("ph2", pr)], tag="pexp")
            P.op("vector", lambda v: v.scalar_tensor_tensor(out_ap, ph2[:, pr, :], 1.0, ph1[:, pr, :],
                                                            ALU.min, ALU.add),
                 reads=[("ph1", pr), ("ph2", pr)], writes=[out_key], tag="phi")

        def x1ld(it):
            rot = it % 2
            c0 = it * TT
            P.dma(f"x1r{rot}", x1rd[:, rot, :, :], x1_d[:, :, c0:c0 + TT].rearrange("c p t -> p c t"),
                  reads=[("x1d", 2 * it), ("x1d", 2 * it + 1)],
                  writes=[("x1rd", rot)], tag="x1ld")

        x1ld(0)
        x1ld(1)
        for it in range(NT_O + 1):
            rot = it % 2
            rotp = (it - 1) % 2
            c0 = it * TT
            if it < NT_O:
                for dc in range(8):
                    P.op("tensor", lambda te, dc=dc, rot=rot: te.matmul(pb[0][0:1, :], ONE_COL, x1rd[:, rot, dc, :],
                                                                        start=(dc == 0), stop=(dc == 7)),
                         reads=[("x1rd", rot), "ones"], accum=("pb0", dc == 0, dc == 7), tag="dsum")
                for dc in range(8):
                    pr = dc % 2
                    eng = "vector" if dc % 2 == 0 else "gpsimd"
                    P.op(eng, lambda g, dc=dc, pr=pr, rot=rot: g.tensor_tensor(
                            ph3[:, pr, :], x1rd[:, rot, dc, :], x1rd[:, rot, dc, :], ALU.mult),
                         reads=[("x1rd", rot)], writes=[("ph3", pr)], tag="dsq")
                    P.op("tensor", lambda te, dc=dc, pr=pr: te.matmul(pb[1][0:1, :], ONE_COL, ph3[:, pr, :],
                                                                      start=(dc == 0), stop=(dc == 7)),
                         reads=[("ph3", pr), "ones"], accum=("pb1", dc == 0, dc == 7), tag="dsumsq")
            if it >= 1:
                for oc in range(8):
                    bank = 2 + oc % 2
                    for dc in range(8):
                        P.op("tensor", lambda te, oc=oc, dc=dc, bank=bank, rotp=rotp: te.matmul(
                                pb[bank][:, :], wl8[:, dc, oc * 128:(oc + 1) * 128], xq8[:, rotp, dc, :],
                                start=(dc == 0), stop=(dc == 7)),
                             reads=[("xq8", rotp), ("wl8", dc)], accum=(f"pb{bank}", dc == 0, dc == 7), tag="f1mm")
                    P.op("scalar", lambda sx, oc=oc, bank=bank: sx.activation(
                            ffnt[:, oc % 2, :], pb[bank][:, :], AF.Identity, bias=bl1_t[:, oc:oc + 1], scale=1.0),
                         reads=[f"pb{bank}", "bl1t"], writes=[("ffnt", oc % 2)], tag="f1ev")
                    eng = "vector" if oc % 2 == 0 else "gpsimd"
                    P.op(eng, lambda e, oc=oc, rotp=rotp: e.tensor_tensor(
                            x1rd[:, rotp, oc, :], x1rd[:, rotp, oc, :], ffnt[:, oc % 2, :], ALU.add),
                         reads=[("x1rd", rotp), ("ffnt", oc % 2)], writes=[("x1rd", rotp)], tag="f1add")
                P.dma(f"x1ps{rotp}", x1p_d[:, :, (it - 1) * TT:it * TT].rearrange("c p t -> p c t"),
                      x1rd[:, rotp, :, :],
                      reads=[("x1rd", rotp)], writes=[("x1pd", it - 1)], tag="x1pst")
                if it + 1 < NT_O:
                    x1ld(it + 1)
            if it < NT_O:
                m_ap, var_ap, s_ap, r_ap = (rowt[0:1, i, :] for i in range(4))
                m_bf, r_bf = rowb[0:1, 0, :], rowb[0:1, 1, :]
                P.op("vector", lambda v: v.tensor_scalar(m_ap, pb[0][0:1, :], 1.0 / D, None, ALU.mult),
                     reads=["pb0"], writes=[("rowt", 0)])
                P.op("vector", lambda v: v.tensor_tensor(var_ap, m_ap, m_ap, ALU.mult),
                     reads=[("rowt", 0)], writes=[("rowt", 1)])
                P.op("vector", lambda v: v.scalar_tensor_tensor(var_ap, pb[1][0:1, :], 1.0 / D, var_ap, ALU.mult, ALU.subtract),
                     reads=["pb1", ("rowt", 1)], writes=[("rowt", 1)])
                P.op("scalar", lambda sx: sx.activation(s_ap, var_ap, AF.Ln, bias=CCr("eps_ln"), scale=1.0),
                     reads=[("rowt", 1), "consts"], writes=[("rowt", 2)])
                P.op("scalar", lambda sx: sx.activation(r_ap, s_ap, AF.Exp, scale=-0.5),
                     reads=[("rowt", 2)], writes=[("rowt", 3)])
                P.op("vector", lambda v: v.tensor_copy(m_bf, m_ap), reads=[("rowt", 0)], writes=[("rowb", 0)])
                P.op("vector", lambda v: v.tensor_copy(r_bf, r_ap), reads=[("rowt", 3)], writes=[("rowb", 1)])
                P.op("tensor", lambda te: te.matmul(pb[0][:, :], ONE_ROW, m_bf, start=True, stop=True),
                     reads=[("rowb", 0), "ones"], writes=["pb0"])
                P.op("tensor", lambda te: te.matmul(pb[1][:, :], ONE_ROW, r_bf, start=True, stop=True),
                     reads=[("rowb", 1), "ones"], writes=["pb1"])
                P.op("scalar", lambda sx: sx.activation(mbd[:, :], pb[0][:, :], AF.Copy),
                     reads=["pb0"], writes=["mbd"])
                P.op("scalar", lambda sx: sx.activation(rbd[:, :], pb[1][:, :], AF.Copy),
                     reads=["pb1"], writes=["rbd"])
                for dc in range(8):
                    eng = "vector" if dc % 2 == 0 else "gpsimd"
                    P.op(eng, lambda e, dc=dc, rot=rot: e.tensor_tensor(xn1[:, dc, :], x1rd[:, rot, dc, :], mbd[:, :], ALU.subtract),
                         reads=[("x1rd", rot), "mbd"], writes=[("xn1", dc)])
                    P.op(eng, lambda e, dc=dc: e.tensor_tensor(xn1[:, dc, :], xn1[:, dc, :], rbd[:, :], ALU.mult),
                         reads=[("xn1", dc), "rbd"], writes=[("xn1", dc)])
                for dc in range(8):
                    eng = "vector" if dc % 2 == 0 else "gpsimd"
                    P.op(eng, lambda e, dc=dc, rot=rot: e.tensor_copy(xq8[:, rot, dc, :], xn1[:, dc, :]),
                         reads=[("xn1", dc)], writes=[("xq8", rot)], tag="xq8c")
                for oc in range(8):
                    bank = oc % 2
                    for dc in range(8):
                        P.op("tensor", lambda te, oc=oc, dc=dc, bank=bank: te.matmul(
                                pb[bank][:, :], wqT[:, dc, oc * 128:(oc + 1) * 128], xn1[:, dc, :],
                                start=(dc == 0), stop=(dc == 7)),
                             reads=[("xn1", dc), ("w1", "wq")], accum=(f"pb{bank}", dc == 0, dc == 7), tag="qpmm")
                    phi_evac(pb[bank][:, :], qp[:, oc, c0:c0 + TT], ("qp", oc, it), bq_t[:, oc:oc + 1],
                             f"pb{bank}", oc % 2)
                for s4 in range(4):
                    sr = s4 % 2
                    for oh in range(2):
                        bank = 2 + oh
                        for dc in range(8):
                            P.op("tensor", lambda te, s4=s4, oh=oh, dc=dc, bank=bank: te.matmul(
                                    pb[bank][:, :], xn1[:, dc, s4 * 128:(s4 + 1) * 128], wkT[:, dc, oh * 512:(oh + 1) * 512],
                                    start=(dc == 0), stop=False),
                                 reads=[("xn1", dc), ("w1", "wk")], accum=(f"pb{bank}", dc == 0, False), tag="kpmm")
                        P.op("tensor", lambda te, oh=oh, bank=bank: te.matmul(
                                pb[bank][:, :], ONE_ROW, bkvb[0:1, 0, oh * 512:(oh + 1) * 512],
                                start=False, stop=True),
                             reads=["bkvb", "ones"], accum=(f"pb{bank}", False, True), tag="kbias")
                        phi_evac(pb[bank][:, :], kpb[:, sr, oh * 512:(oh + 1) * 512], ("kpb", sr),
                                 None, f"pb{bank}", oh)
                        bank2 = 2 + oh
                        for dc in range(8):
                            P.op("tensor", lambda te, s4=s4, oh=oh, dc=dc, bank2=bank2: te.matmul(
                                    pb[bank2][:, :], xn1[:, dc, s4 * 128:(s4 + 1) * 128], wvT[:, dc, oh * 512:(oh + 1) * 512],
                                    start=(dc == 0), stop=False),
                                 reads=[("xn1", dc), ("w1", "wv")], accum=(f"pb{bank2}", dc == 0, False), tag="vpmm")
                        P.op("tensor", lambda te, oh=oh, bank2=bank2: te.matmul(
                                pb[bank2][:, :], ONE_ROW, bkvb[0:1, 1, oh * 512:(oh + 1) * 512],
                                start=False, stop=True),
                             reads=["bkvb", "ones"], accum=(f"pb{bank2}", False, True), tag="vbias")
                        P.op("scalar", lambda sx, sr=sr, oh=oh, bank2=bank2: sx.activation(
                                vpb[:, sr, oh * 8:(oh + 1) * 8, 0:64], pb[bank2][:, :], AF.Copy),
                             reads=[f"pb{bank2}"], writes=[("vpb", sr, oh)], tag="vev")
                    for h in range(16):
                        bank = 4 + (h % 4)
                        colof = (h // 4) * 128
                        P.op("tensor", lambda te, h=h, sr=sr, bank=bank, colof=colof, s4=s4: te.matmul(
                                pb[bank][0:64, colof:colof + 65], kpb[:, sr, h * 64:(h + 1) * 64], vpb[:, sr, h, :],
                                start=(s4 == 0 and h // 4 == 0), stop=(s4 == 3 and h // 4 == 3),
                                skip_group_check=True),
                             reads=[("kpb", sr), ("vpb", sr, h // 8), "vpones"],
                             accum=(f"pb{bank}", s4 == 0 and h // 4 == 0, s4 == 3 and h // 4 == 3), tag="kvmm")
                for h in range(16):
                    bank = 4 + (h % 4)
                    colof = (h // 4) * 128
                    P.op("vector", lambda v, h=h, bank=bank, colof=colof: v.scalar_tensor_tensor(
                            kvsb[:, h, :], pb[bank][0:64, colof:colof + 65], 0.0, kvsb[:, h, :], ALU.max, ALU.add),
                         reads=[f"pb{bank}", "kvsb"], writes=["kvsb"], tag="kvacc")
        P.dma("kvcin", kvin_d[:, :], kvsb[:, :, :].rearrange("p h e -> p (h e)"),
              reads=["kvsb"], writes=["kvin"], tag="kvin")
        P.op("gpsimd", lambda g: g.collective_compute(
                "AllReduce", ALU.add, replica_groups=[[0, 1], [2, 3], [4, 5], [6, 7]],
                ins=[dram["kvin"]], outs=[dram["kvout"]]),
             reads=["kvin"], writes=["kvout"], axis="coll", tag="allreduce")

    if st >= 4:
        P.fence("D->E")
        kvo3 = dram["kvout"].rearrange("p (h e) -> p h e", e=65)
        P.op("vector", lambda v: v.memset(kvbd[:, :, :], 0.0),
             writes=[("kvbd", h) for h in range(16)], tag="kvbd0")
        P.op("vector", lambda v: v.memset(kvzd[:, :, :], 0.0),
             writes=[("kvzd", h) for h in range(16)], tag="kvzd0")
        for h in range(16):
            P.dma("kvcout", kvbd[(h % 2) * 64:(h % 2) * 64 + 64, h // 2, :][:, 0:64],
                  kvo3[:, h, 0:64], queue="gpsimd", reads=["kvout"], writes=[("kvbd", h)], tag="kvbd")
            P.dma("kvcout", kvzd[(h % 2) * 64:(h % 2) * 64 + 64, h // 2, (h % 2):(h % 2) + 1],
                  kvo3[:, h, 64:65], queue="gpsimd", reads=["kvout"], writes=[("kvzd", h)],
                  tag="kvzd", slow_ok=True)
        P.dma("einit", bo_tE[:, :], bo_t_d, writes=["boE"])
        P.dma("einit", blk2[:, :], blk2_d, writes=["blk2"])
        P.dma("einitg", woT[:, :, :], woT_d.rearrange("(c p) t -> p c t", p=128),
              queue="gpsimd", writes=[("w1", "wo")])

        def x1pld(it):
            rot = it % 2
            c0 = it * TT
            P.dma(f"x1p{rot}", x1prd[:, rot, :, :], x1p_d[:, :, c0:c0 + TT].rearrange("c p t -> p c t"),
                  reads=[("x1pd", it)], writes=[("x1prd", rot)], tag="x1pld")

        x1pld(0)
        x1pld(1)

        def e_pre(it):
            rot = it % 2
            c0 = it * TT
            for m in range(8):
                bank = m % 2
                mr = m % 2
                P.op("tensor", lambda te, m=m, bank=bank, c0=c0, it=it: te.matmul(
                        pb[bank][:, :], kvbd[:, m, :], qp[:, m, c0:c0 + TT], start=True, stop=True),
                     reads=[("kvbd", 2 * m), ("kvbd", 2 * m + 1), ("qp", m, it)], writes=[f"pb{bank}"], tag="apre")
                P.op("scalar", lambda sx, m=m, bank=bank: sx.activation(presb[:, m, :], pb[bank][:, :], AF.Copy),
                     reads=[f"pb{bank}"], writes=[("presb", m)], tag="apev")
                zbank = 2 + m % 2
                P.op("tensor", lambda te, m=m, c0=c0, it=it, zbank=zbank: te.matmul(
                        pb[zbank][0:2, :], kvzd[:, m, :], qp[:, m, c0:c0 + TT], start=True, stop=True),
                     reads=[("kvzd", 2 * m), ("kvzd", 2 * m + 1), ("qp", m, it)], writes=[f"pb{zbank}"], tag="anrm")
                P.op("vector", lambda v, mr=mr, zbank=zbank: v.tensor_scalar(
                        rbe2[0:2, mr, :], pb[zbank][0:2, :], 1e-6, None, ALU.add),
                     reads=[f"pb{zbank}"], writes=[("rbe2", mr)], tag="zeps")
                P.op("scalar", lambda sx, mr=mr: sx.activation(rbf2[0:2, mr, :], rbe2[0:2, mr, :], AF.Ln),
                     reads=[("rbe2", mr)], writes=[("rbf2", mr)], tag="zln")
                P.op("scalar", lambda sx, mr=mr: sx.activation(rbf2[0:2, mr, :], rbf2[0:2, mr, :], AF.Exp, scale=-1.0),
                     reads=[("rbf2", mr)], writes=[("rbf2", mr)], tag="zexp")
                P.op("vector", lambda v, mr=mr: v.tensor_copy(nrb2[:, mr, :], rbf2[0:2, mr, :]),
                     reads=[("rbf2", mr)], writes=[("nrb2", mr)], tag="zbf")
                if m >= 1:
                    emit_ebc(m - 1, rot)
            emit_ebc(7, rot)

        def emit_ebc(m, rot):
            zbank = 2 + m % 2
            mr = m % 2
            P.op("tensor", lambda te, mr=mr, zbank=zbank: te.matmul(
                    pb[zbank][:, :], blk2[:, :], nrb2[:, mr, :], start=True, stop=True),
                 reads=[("nrb2", mr), "blk2"], writes=[f"pb{zbank}"], tag="ebc")
            P.op("vector", lambda v, mr=mr, zbank=zbank: v.tensor_copy(rbefm[:, mr, :], pb[zbank][:, :]),
                 reads=[f"pb{zbank}"], writes=[("rbefm", mr)], tag="ebce")
            eng = "vector" if m % 2 == 0 else "gpsimd"
            P.op(eng, lambda e, m=m, mr=mr, rot=rot: e.tensor_tensor(
                    nrm[:, rot, m, :], presb[:, m, :], rbefm[:, mr, :], ALU.mult),
                 reads=[("presb", m), ("rbefm", mr)], writes=[("nrm", rot, m)], tag="enorm")

        e_pre(0)
        for it in range(1, NT_O + 1):
            rotp = (it - 1) % 2
            c0p = (it - 1) * TT
            if it < NT_O:
                e_pre(it)
            for do in range(8):
                bank = 4 + do % 2
                for oc in range(8):
                    P.op("tensor", lambda te, do=do, oc=oc, bank=bank, rotp=rotp: te.matmul(
                            pb[bank][:, :], woT[:, oc, do * 128:(do + 1) * 128], nrm[:, rotp, oc, :],
                            start=(oc == 0), stop=(oc == 7)),
                         reads=[("nrm", rotp, oc), ("w1", "wo")], accum=(f"pb{bank}", oc == 0, oc == 7), tag="womm2")
                P.op("scalar", lambda sx, do=do, bank=bank: sx.activation(
                        aev[:, do % 2, :], pb[bank][:, :], AF.Identity, bias=bo_tE[:, do:do + 1], scale=1.0),
                     reads=[f"pb{bank}", "boE"], writes=[("aev", do % 2)], tag="woev")
                P.op("vector", lambda v, do=do, rotp=rotp: v.tensor_tensor(
                        outb[:, do % 2, :], x1prd[:, rotp, do, :], aev[:, do % 2, :], ALU.add),
                     reads=[("x1prd", rotp), ("aev", do % 2)], writes=[("outbf", do % 2)], tag="finadd")
                P.dma(f"yst{do % 2}", y_d[do, :, c0p:c0p + TT], outb[:, do % 2, :],
                      reads=[("outbf", do % 2)], writes=[("out", "y", it - 1, do)], tag="yst")
            if it + 1 < NT_O:
                x1pld(it + 1)

    P.barrier("sync", [k for k in P.last_writer if isinstance(k, tuple) and k[0] == "out"])
    em = P.finalize_and_emit(nc)
    em.close()
    stack.close()
    return nc, P, outs


def own_token_idx(p):
    return np.concatenate([np.arange((2 * i + p) * TT, (2 * i + p + 1) * TT) for i in range(8)])


def host_prep_core(I, core):
    b, p = core // 2, core % 2
    x = I["L0_attn_x"]
    perm = np.concatenate([own_token_idx(p), own_token_idx(1 - p)])
    xT = np.ascontiguousarray(x[b].T)
    xfull = np.ascontiguousarray(xT[:, perm])
    xown = np.ascontiguousarray(xT[:, own_token_idx(p)])
    l0qkv = np.ascontiguousarray(np.concatenate(
        [I["L0_attn_Wq"], I["L0_attn_Wk"], I["L0_attn_Wv"]], axis=1))
    consts = np.zeros((128, NCONST), dtype=np.float32)
    for name, (col, val) in CONST_COLS.items():
        consts[:, col] = val
    onesbf = np.ones((128, 576), dtype=BF)
    onesf = np.ones((1, 128), dtype=np.float32)
    kk = np.arange(128)[:, None]
    tq = np.arange(TT)[None, :]
    diag = [(128 * m + kk <= tq).astype(np.float32) for m in range(4)]
    if p == 0:
        mlist = diag + [np.zeros((128, TT), np.float32)] * 4
    else:
        mlist = diag + [np.ones((128, TT), np.float32)] * 4
    masks = np.stack(mlist, axis=1).astype(BF)
    d = dict(xfull=xfull, xown=xown, l0qkv=l0qkv, consts=consts,
             onesbf=onesbf, onesf=onesf, masks=masks,
             l0wo=np.ascontiguousarray(I["L0_attn_Wo"]),
             wfcT=np.ascontiguousarray(I["L0_ffn_W_fc"].T),
             wpjT=np.ascontiguousarray(I["L0_ffn_W_proj"].T),
             afc=np.ascontiguousarray(I["L0_ffn_A_fc"]),
             bfcT=np.ascontiguousarray(I["L0_ffn_B_fc"].T),
             apj=np.ascontiguousarray(I["L0_ffn_A_proj"]),
             bpjT=np.ascontiguousarray(I["L0_ffn_B_proj"].T),
             bfc_t=np.ascontiguousarray(I["L0_ffn_b_fc"].reshape(32, 128).T),
             bpj_t=np.ascontiguousarray(I["L0_ffn_b_proj"].reshape(8, 128).T),
             wqT=np.ascontiguousarray(I["L1_attn_Wq"].T),
             wkT=np.ascontiguousarray(I["L1_attn_Wk"].T),
             wvT=np.ascontiguousarray(I["L1_attn_Wv"].T),
             woT=np.ascontiguousarray(I["L1_attn_Wo"].T),
             bq_t=np.ascontiguousarray(I["L1_attn_bq"].reshape(8, 128).T),
             bkv=np.ascontiguousarray(np.stack([I["L1_attn_bk"], I["L1_attn_bv"]])),
             bo_t=np.ascontiguousarray(I["L1_attn_bo"].reshape(8, 128).T),
             wl1T=np.ascontiguousarray(I["L1_ffn_weight"].T),
             bl1_t=np.ascontiguousarray(I["L1_ffn_bias"].reshape(8, 128).T),
             blk2=np.concatenate([np.repeat([[1, 0]], 64, 1).reshape(1, 128),
                                  np.repeat([[0, 1]], 64, 1).reshape(1, 128)]).astype(BF))
    return d


_BUILD_CACHE = {}


def kernel(**inputs):
    from concourse.bass_utils import run_bass_kernel_spmd
    if "nc" not in _BUILD_CACHE:
        nc, P, _ = build(stage="E")
        _BUILD_CACHE["nc"] = nc
    nc = _BUILD_CACHE["nc"]
    in_maps = [host_prep_core(inputs, c) for c in range(8)]
    res = run_bass_kernel_spmd(nc, in_maps, core_ids=list(range(8)))
    out = np.zeros((4, 8192, 1024), dtype=np.float32)
    for c in range(8):
        b, p = c // 2, c % 2
        yc = np.asarray(res.results[c]["y"]).reshape(1024, 4096)
        out[b, own_token_idx(p), :] = yc.T
    return out


# version 07
# speedup vs baseline: 1.9274x; 1.0812x over previous
COMPUTE_ENGINES = ("tensor", "vector", "scalar", "gpsimd")


class Op:
    __slots__ = ("engine", "axis", "fn", "reads", "writes", "inc", "tag",
                 "dep_ops", "idx", "waits", "signaled", "cum", "G", "C")

    def __init__(self, engine, axis, fn, reads, writes, inc, tag):
        self.engine = engine
        self.axis = axis
        self.fn = fn
        self.reads = tuple(reads)
        self.writes = tuple(writes)
        self.inc = inc
        self.tag = tag
        self.waits = []
        self.signaled = False


class Prog:
    def __init__(self):
        self.ops = []
        self.last_writer = {}
        self.readers = {}
        self.sinks = set()
        self.last_fence = None

    def op(self, engine, fn, reads=(), writes=(), accum=None, tag="", axis=None, inc=1):
        o = Op(engine, axis or engine, fn, reads, writes, inc, tag)
        o.idx = len(self.ops)
        self.ops.append(o)
        deps = []
        for k in o.reads:
            w = self.last_writer.get(k)
            if w is not None:
                deps.append(w)
        for k in o.writes:
            w = self.last_writer.get(k)
            if w is not None:
                deps.append(w)
            for r in self.readers.get(k, ()):
                deps.append(r)
        if self.last_fence is not None and self.last_fence is not o:
            deps.append(self.last_fence)
        if accum is not None:
            key, is_start, is_stop = accum
            if is_start:
                w = self.last_writer.get(key)
                if w is not None:
                    deps.append(w)
                for r in self.readers.get(key, ()):
                    deps.append(r)
            if is_stop:
                self.last_writer[key] = o
                self.readers[key] = []
        seen = set()
        o.dep_ops = []
        for d in deps:
            if d is o or id(d) in seen:
                continue
            seen.add(id(d))
            o.dep_ops.append(d)
        for k in o.reads:
            self.readers.setdefault(k, []).append(o)
        for k in o.writes:
            self.last_writer[k] = o
            self.readers[k] = []
        return o

    def dma(self, family, out, in_, reads=(), writes=(), queue="sync", tag="", slow_ok=False):
        return self.op(queue, lambda eng: eng.dma_start(out=out, in_=in_,
                                                        allow_slow_non_contiguous=slow_ok),
                       reads=reads, writes=writes, tag=tag, axis=family, inc=16)

    def barrier(self, engine, keys, tag="barrier"):
        return self.op(engine, None, reads=keys, tag=tag)

    def fence(self, tag="fence"):
        keys = list(self.last_writer.keys())
        f = self.op("vector", lambda v: v.engine_nop(),
                    reads=keys, writes=keys, tag=tag)
        self.last_fence = f
        return f

    def finalize_and_emit(self, nc):
        for o in self.ops:
            if o.inc == 16 or o.axis == "coll":
                o.signaled = True
            for d in o.dep_ops:
                d.signaled = True
        axis_cum = {}
        for o in self.ops:
            if o.signaled:
                axis_cum[o.axis] = axis_cum.get(o.axis, 0) + o.inc
                o.cum = axis_cum[o.axis]
        def meet(a, b):
            for k, v in b.items():
                if a.get(k, 0) < v:
                    a[k] = v
        eng_G = {}
        for o in self.ops:
            need = {}
            for d in o.dep_ops:
                assert d.idx < o.idx
                meet(need, d.C)
            G_prev = eng_G.get(o.engine, {})
            waits = {}
            for ax, v in need.items():
                if G_prev.get(ax, 0) >= v:
                    continue
                waits[ax] = v
            o.waits = sorted(waits.items())
            G = dict(G_prev)
            meet(G, waits)
            eng_G[o.engine] = G
            C = dict(G)
            if o.signaled:
                v = axis_cum[o.axis] if o.axis in self.sinks else o.cum
                C[o.axis] = max(C.get(o.axis, 0), v)
            o.C = C
        engines = {}
        for o in self.ops:
            engines.setdefault(o.engine, []).append(o)
        self.stats = {e: dict(n_ops=len(v), n_sig=sum(1 for x in v if x.signaled),
                              n_waits=sum(len(x.waits) for x in v))
                      for e, v in engines.items()}
        self.stats["n_axes"] = len(axis_cum)

        import contextlib
        stack = contextlib.ExitStack()
        axes = sorted(axis_cum)
        sems = {a: stack.enter_context(nc.semaphore(f"sem_{a}")) for a in axes}
        blk = stack.enter_context(nc.Block())

        def make_body(eops):
            def body(eng):
                for o in eops:
                    for (ax, v) in o.waits:
                        eng.wait_ge(sems[ax], v)
                    if o.fn is not None:
                        ins = o.fn(eng)
                        if o.signaled:
                            ins.then_inc(sems[o.axis], o.inc)
                    else:
                        assert not o.signaled
            return body

        for ename, eops in engines.items():
            getattr(blk, ename)(make_body(eops))
        return stack


import numpy as np
import ml_dtypes
import concourse.bass as bass
import concourse.mybir as mybir

F32 = mybir.dt.float32
FP16 = mybir.dt.float16
BF16 = mybir.dt.bfloat16
FP8 = mybir.dt.float8e4
I32 = mybir.dt.int32
AF = mybir.ActivationFunctionType
ALU = mybir.AluOpType
AX = mybir.AxisListType.X
BF = ml_dtypes.bfloat16

D = 1024
T = 8192
TOWN = 4096
NT_G = 16
NT_O = 8
TT = 512
TTC = 256
NT_C = TOWN // TTC
DK = 128
LN2 = float(np.log(2.0))
SM_SCALE = float(128 ** -0.5)
KB = 1024

CONST_COLS = {
    "neg16": (0, -16.0),
    "eps_ln": (1, 1e-5),
    "eps_z": (2, 1e-6),
    "six": (4, 6.0),
    "zero": (5, 0.0),
}
NCONST = 8

STAGES = ["A", "B", "C", "D", "E"]


def build(stage="E", dbg_stage=None):
    st = STAGES.index(stage)
    nc = bass.Bass("TRN2", target_bir_lowering=False)
    P = Prog()
    P.sinks = {"init", "dbg", "kvcin", "kvcout", "cinit", "dinit", "dinitg", "einit", "einitg",
               "winit", "mxstfc", "mxldfc", "mxstpj", "mxldpj"}

    dram = {}

    def din(name, shape, dt=F32):
        dram[name] = nc.dram_tensor(name, shape, dt, kind="ExternalInput").ap()
        return dram[name]

    def dint(name, shape, dt=F32):
        dram[name] = nc.dram_tensor(name, shape, dt).ap()
        return dram[name]

    def dout(name, shape, dt=F32):
        dram[name] = nc.dram_tensor(name, shape, dt, kind="ExternalOutput").ap()
        return dram[name]

    xfull = din("xfull", [D, T])
    xown = din("xown", [D, TOWN])
    l0qkv_d = din("l0qkv", [D, 384])
    l0wo_d = din("l0wo", [DK, D])
    consts_d = din("consts", [128, NCONST])
    onesbf_d = din("onesbf", [128, 576], BF16)
    onesf_d = din("onesf", [1, 128])
    masks_d = din("masks", [128, 8, TT], BF16)
    wfcT_d = din("wfcT", [D, 4096])
    wpjT_d = din("wpjT", [4096, D])
    afc_d = din("afc", [16, D])
    bfcT_d = din("bfcT", [16, 4096])
    apj_d = din("apj", [16, 4096])
    bpjT_d = din("bpjT", [16, D])
    bfc_t_d = din("bfc_t", [128, 32])
    bpj_t_d = din("bpj_t", [128, 8])
    wqT_d = din("wqT", [D, D])
    wkT_d = din("wkT", [D, D])
    wvT_d = din("wvT", [D, D])
    woT_d = din("woT", [D, D])
    bq_t_d = din("bq_t", [128, 8])
    bkv_d = din("bkv", [2, D])
    bo_t_d = din("bo_t", [128, 8])
    wl1T_d = din("wl1T", [D, D])
    bl1_t_d = din("bl1_t", [128, 8])
    blk2_d = din("blk2", [2, 128], BF16)

    kd = dint("kd", [64, 128, 128], BF16)
    vd = dint("vd", [64, 128, 128], BF16)
    mr_d = dint("mr_d", [2, NT_O, TT])
    mx_d = dint("mx_d", [2, 128])
    x1_d = dint("x1_d", [8, 128, TOWN], BF16)
    x1p_d = dint("x1p_d", [8, 128, TOWN], BF16)
    kvin_d = dint("kvin", [64, 1040])
    kvout_d = dint("kvout", [64, 1040])
    y_d = dout("y", [8, 128, TOWN])

    base0 = (nc.sbuf_base + 31) & ~31
    pad = base0 - nc.sbuf_base
    TOTAL = 207 * KB + 256
    nc.alloc_sbuf_tensor("arena", [128, TOTAL + pad], mybir.dt.uint8)

    def place(name, shape, dt, off):
        assert off % 32 == 0, (name, off)
        sz = int(np.prod(shape[1:])) * mybir.dt.size(dt)
        assert off + sz <= TOTAL, (name, off, sz, TOTAL)
        return nc.alloc_sbuf_tensor_at(name, shape, dt, offset=base0 + off)

    sc = place("sc", [128, NCONST], F32, 0)
    ones = place("ones", [128, 576], BF16, 32)
    onesf_sb = place("onesf_sb", [1, 128], F32, 1184)
    wo_sb = place("wo_sb", [128, 8, 128], BF16, 1696)
    w1r = place("w1r", [1, 384], BF16, 3744)
    BIGO = 4512

    xb = place("xb", [128, 3, 8, TT], BF16, BIGO + 0)
    sqb = place("sqb", [128, 2, 8, TT], BF16, BIGO + 24576)
    l0w = place("l0w", [128, 8, 384], BF16, BIGO + 40960)
    l0wf = place("l0wf", [128, 8, 384], F32, BIGO + 47104)
    rbA = place("rbA", [128, 2, TT], BF16, BIGO + 59392)
    kevac = place("kevac", [128, 2, TT], BF16, BIGO + 61440)
    vevac = place("vevac", [128, 2, 4, 128], BF16, BIGO + 63488)
    wtfc = place("wtfc", [128, 8, 4096], FP16, BIGO + 0)
    wtpj = place("wtpj", [128, 32, 1024], FP16, BIGO + 65536)
    qp = place("qp", [128, 8, TOWN], BF16, BIGO + 0)
    wqT = place("wqT", [128, 8, D], BF16, BIGO + 65536)
    wkT = place("wkT", [128, 8, D], BF16, BIGO + 81920)
    wvT = place("wvT", [128, 8, D], BF16, BIGO + 98304)
    wl8 = place("wl8", [128, 8, D], FP8, BIGO + 114688)
    wstgD = place("wstgD", [128, 2, D], F32, BIGO + 122880)
    woT = place("woT", [128, 8, D], BF16, BIGO + 65536)

    SCRO = BIGO + 131072
    rowt = place("rowt", [1, 4, TT], F32, SCRO + 0)
    rowb = place("rowb", [1, 4, TT], BF16, SCRO + 8192)
    rcol = place("rcol", [128, 2, 4], F32, SCRO + 12288)
    DW0 = SCRO + 12320
    masks = place("masks", [128, 8, TT], BF16, DW0 + 0)
    q_sb = place("q_sb", [128, NT_O, TT], BF16, DW0 + 8192)
    onorm = place("onorm", [128, NT_O, TT], BF16, DW0 + 16384)
    WKO = DW0 + 24576

    pt = place("pt", [128, 4, TT], BF16, WKO + 0)
    ksb = place("ksb", [128, 4, 4, 128], BF16, WKO + 4096)
    vsb = place("vsb", [128, 4, 4, 128], BF16, WKO + 8192)
    rbb = place("rbb", [128, TT], F32, WKO + 12288)
    wstg = place("wstg", [128, 2, D], F32, WKO + 14336)
    lt_z = place("lt_z", [128, 2, TT], I32, WKO + 22528)
    lt_f = place("lt_f", [128, 2, TT], FP16, WKO + 26624)
    abm = place("abm", [16, 4096], BF16, WKO + 28672)
    abm2 = place("abm2", [16, D], BF16, WKO + 36864)
    mxc = place("mxc", [128, 2], F32, WKO + 38912)
    sclb = place("sclb", [128, 8], F32, WKO + 38944)
    hsb = place("hsb", [128, 32, TTC], FP16, WKO + 0)
    x1t = place("x1t", [128, 2, 8, TTC], BF16, WKO + 16384)
    mbc = place("mbc", [128, 2, TTC], FP16, WKO + 24576)
    rbc = place("rbc", [128, 2, TTC], FP16, WKO + 25600)
    bfc_t = place("bfc_t", [128, 32], F32, WKO + 26624)
    bpj_t = place("bpj_t", [128, 8], F32, WKO + 26752)
    bstg = place("bstg", [128, 64], F32, WKO + 26784)
    xrb_b = place("xrb_b", [128, 8, TTC], F32, WKO + 27040)
    xbt = place("xbt", [128, 2, 8, TTC], FP16, DW0 + 0)
    xrb_a = place("xrb_a", [128, 8, TTC], F32, DW0 + 8192)
    xrb = [xrb_a, xrb_b]
    x1rd = place("x1rd", [128, 2, 8, TT], BF16, DW0 + 0)
    xn1 = place("xn1", [128, 8, TT], BF16, DW0 + 16384)
    xq8 = place("xq8", [128, 2, 8, TT], FP8, DW0 + 24576)
    ph1 = place("ph1", [128, 2, TT], F32, DW0 + 32768)
    ph2 = place("ph2", [128, 2, TT], F32, DW0 + 36864)
    ph3 = place("ph3", [128, 2, TT], BF16, DW0 + 40960)
    mbd = place("mbd", [128, TT], BF16, DW0 + 43008)
    rbd = place("rbd", [128, TT], BF16, DW0 + 44032)
    kpb = place("kpb", [128, 2, D], BF16, DW0 + 45056)
    vpb = place("vpb", [128, 2, 16, 65], BF16, DW0 + 49152)
    kvsb = place("kvsb", [64, 16, 65], F32, DW0 + 53312)
    ffnt = place("ffnt", [128, 2, TT], BF16, DW0 + 57472)
    bq_t = place("bq_t", [128, 8], F32, DW0 + 59520)
    bl1_t = place("bl1_t", [128, 8], F32, DW0 + 59552)
    bkvb = place("bkvb", [1, 2, D], BF16, DW0 + 59584)
    nrm = place("nrm", [128, 2, 8, TT], BF16, DW0 + 0)
    x1prd = place("x1prd", [128, 2, 8, TT], BF16, DW0 + 16384)
    presb = place("presb", [128, 8, TT], BF16, DW0 + 32768)
    outb = place("outb", [128, 2, TT], F32, DW0 + 40960)
    rbe2 = place("rbe2", [2, 2, TT], F32, DW0 + 45056)
    rbf2 = place("rbf2", [2, 2, TT], F32, DW0 + 49152)
    nrb2 = place("nrb2", [2, 2, TT], BF16, DW0 + 53248)
    rbefm = place("rbefm", [128, 2, TT], BF16, DW0 + 55296)
    aev = place("aev", [128, 2, TT], BF16, DW0 + 57344)
    kvbd = place("kvbd", [128, 8, 128], BF16, DW0 + 59392)
    kvzd = place("kvzd", [128, 8, 2], BF16, DW0 + 61440)
    blk2 = place("blk2", [2, 128], BF16, DW0 + 61472)
    bo_tE = place("bo_tE", [128, 8], F32, DW0 + 61728)

    import contextlib
    stack = contextlib.ExitStack()
    PSt = lambda *a: stack.enter_context(nc.psum_tensor(*a))
    pb = [PSt(f"pb{i}", [128, TT], F32) for i in range(8)]

    CC = lambda n: sc[:, CONST_COLS[n][0]:CONST_COLS[n][0] + 1]
    CCr = lambda n: sc[0:1, CONST_COLS[n][0]:CONST_COLS[n][0] + 1]
    ONE_COL = ones[:, 0:1]
    ONE_ROW = ones[0:1, 0:128]
    ONE_1 = ones[0:1, 0:1]

    P.dma("init", sc[:], consts_d, writes=["consts"])
    P.dma("init", ones[:], onesbf_d, writes=["ones"])
    P.dma("init", onesf_sb[:], onesf_d, writes=["onesf"])
    P.dma("init", masks[:], masks_d, writes=["masks"])
    for dc in range(8):
        P.dma("init", l0wf[:, dc, :], l0qkv_d[dc * 128:(dc + 1) * 128, :], writes=[("l0wf", dc)])
    for dc in range(8):
        P.op("vector", lambda v, dc=dc: v.tensor_copy(l0w[:, dc, :], l0wf[:, dc, :]),
             reads=[("l0wf", dc)], writes=[("l0w", dc)])
    for dc in range(8):
        P.op("tensor", lambda te, dc=dc: te.matmul(pb[2][0:1, 0:384], ONE_COL, l0w[:, dc, :],
                                                   start=(dc == 0), stop=(dc == 7)),
             reads=[("l0w", dc), "ones"], accum=("pb2", dc == 0, dc == 7), tag="w1mm")
    P.op("scalar", lambda s: s.activation(w1r[0:1, :], pb[2][0:1, 0:384], AF.Copy),
         reads=["pb2"], writes=["w1r"])

    xsrc = xfull.rearrange("(c p) t -> p c t", p=128)

    def ldx(t):
        r3 = t % 3
        P.dma(f"x{r3}", xb[:, r3, :, :], xsrc[:, :, t * TT:(t + 1) * TT], queue="gpsimd",
              writes=[("xb", r3, dc) for dc in range(8)], tag="ldx")

    ldx(0)
    ldx(1)

    prevproj = [None]
    for t in range(NT_G + 1):
        rot = t % 2
        r3 = t % 3
        if t < NT_G:
            for dc in range(8):
                eng = "vector" if dc % 2 == 0 else "gpsimd"
                P.op(eng, lambda e, dc=dc, rot=rot, r3=r3: e.tensor_tensor(
                        sqb[:, rot, dc, :], xb[:, r3, dc, :], xb[:, r3, dc, :], ALU.mult),
                     reads=[("xb", r3, dc)], writes=[("sq", rot, dc)], tag="sq")
            for dc in range(8):
                P.op("tensor", lambda te, dc=dc, r3=r3: te.matmul(pb[0][0:1, :], ONE_COL, xb[:, r3, dc, :],
                                                                  start=(dc == 0), stop=(dc == 7)),
                     reads=[("xb", r3, dc), "ones"], accum=("pb0", dc == 0, dc == 7), tag="summ")
            for dc in range(8):
                P.op("tensor", lambda te, dc=dc, rot=rot: te.matmul(pb[1][0:1, :], ONE_COL, sqb[:, rot, dc, :],
                                                                    start=(dc == 0), stop=(dc == 7)),
                     reads=[("sq", rot, dc), "ones"], accum=("pb1", dc == 0, dc == 7), tag="sumsq")
        if t < NT_G:
            m_ap, var_ap, s_ap, r_ap = (rowt[0:1, i, :] for i in range(4))
            P.op("vector", lambda v: v.tensor_scalar(m_ap, pb[0][0:1, :], 1.0 / D, None, ALU.mult),
                 reads=["pb0"], writes=[("rowt", 0)])
            P.op("vector", lambda v: v.tensor_tensor(var_ap, m_ap, m_ap, ALU.mult),
                 reads=[("rowt", 0)], writes=[("rowt", 1)])
            P.op("vector", lambda v: v.scalar_tensor_tensor(var_ap, pb[1][0:1, :], 1.0 / D, var_ap, ALU.mult, ALU.subtract),
                 reads=["pb1", ("rowt", 1)], writes=[("rowt", 1)])
            P.op("scalar", lambda s: s.activation(s_ap, var_ap, AF.Ln, bias=CCr("eps_ln"), scale=1.0),
                 reads=[("rowt", 1), "consts"], writes=[("rowt", 2)])
            P.op("scalar", lambda s: s.activation(r_ap, s_ap, AF.Exp, scale=-0.5),
                 reads=[("rowt", 2)], writes=[("rowt", 3)])
            if t < NT_O:
                P.dma("mrstm", mr_d[0, t, :], m_ap, reads=[("rowt", 0)], writes=[("mrd", 0, t)])
                P.dma("mrstr", mr_d[1, t, :], r_ap, reads=[("rowt", 3)], writes=[("mrd", 1, t)])
            P.op("vector", lambda v, rot=rot: v.tensor_scalar(rowb[0:1, 2 * rot, :], m_ap, -1.0, None, ALU.mult),
                 reads=[("rowt", 0)], writes=[("rowb", rot, 0)])
            P.op("vector", lambda v, rot=rot: v.tensor_copy(rowb[0:1, 2 * rot + 1, :], r_ap),
                 reads=[("rowt", 3)], writes=[("rowb", rot, 1)])
        if prevproj[0] is not None:
            prevproj[0]()
            prevproj[0] = None
        if t < NT_G:
            P.op("tensor", lambda te, rot=rot: te.matmul(pb[3][:, :], ONE_ROW, rowb[0:1, 2 * rot + 1, :], start=True, stop=True),
                 reads=[("rowb", rot, 1), "ones"], writes=["pb3"])
            P.op("scalar", lambda s, rot=rot: s.activation(rbA[:, rot, :], pb[3][:, :], AF.Copy),
                 reads=["pb3"], writes=[("rbA", rot)])
            for s4 in range(4):
                P.op("tensor", lambda te, s4=s4, rot=rot: te.matmul(pb[2][:, s4:s4 + 1],
                                                                    rowb[0:1, 2 * rot + 1, s4 * 128:(s4 + 1) * 128],
                                                                    ONE_1, start=True, stop=True),
                     reads=[("rowb", rot, 1), "ones"], writes=["pb2"], tag="rcolmm")
            P.op("scalar", lambda s, rot=rot: s.activation(rcol[:, rot, :], pb[2][:, 0:4], AF.Copy),
                 reads=["pb2"], writes=[("rcol", rot)])

            def mk_proj(t=t, rot=rot, r3=r3):
                def proj():
                    for dc in range(8):
                        P.op("tensor", lambda te, dc=dc: te.matmul(pb[4][:, :], l0w[:, dc, 128:256], xb[:, r3, dc, :],
                                                                   start=(dc == 0), stop=False),
                             reads=[("xb", r3, dc), ("l0w", dc)], accum=("pb4", dc == 0, False), tag="kmm")
                    P.op("tensor", lambda te: te.matmul(pb[4][:, :], w1r[0:1, 128:256], rowb[0:1, 2 * rot, :],
                                                        start=False, stop=True),
                         reads=["w1r", ("rowb", rot, 0)], accum=("pb4", False, True), tag="kout")
                    P.op("scalar", lambda s: s.activation(kevac[:, rot, :], pb[4][:, :], AF.Copy),
                         reads=["pb4"], writes=[("kevac", rot)])
                    P.op("vector", lambda v: v.tensor_tensor(kevac[:, rot, :], kevac[:, rot, :], rbA[:, rot, :], ALU.mult),
                         reads=[("kevac", rot), ("rbA", rot)], writes=[("kevac", rot)])
                    P.dma(f"kst{rot}", kd[4 * t:4 * t + 4, :, :].rearrange("j p t -> p j t"), kevac[:, rot, :],
                          reads=[("kevac", rot)], writes=[("kdt", t)])
                    for s4 in range(4):
                        bank = 5 + s4 % 2
                        for dc in range(8):
                            P.op("tensor", lambda te, dc=dc, s4=s4, bank=bank: te.matmul(
                                    pb[bank][:, 0:128], xb[:, r3, dc, s4 * 128:(s4 + 1) * 128], l0w[:, dc, 256:384],
                                    start=(dc == 0), stop=False),
                                 reads=[("xb", r3, dc), ("l0w", dc)], accum=(f"pb{bank}", dc == 0, False), tag="vmm")
                        P.op("tensor", lambda te, s4=s4, bank=bank: te.matmul(
                                pb[bank][:, 0:128], rowb[0:1, 2 * rot, s4 * 128:(s4 + 1) * 128], w1r[0:1, 256:384],
                                start=False, stop=True),
                             reads=[("rowb", rot, 0), "w1r"], accum=(f"pb{bank}", False, True), tag="vout")
                        P.op("scalar", lambda s, s4=s4, bank=bank: s.activation(
                                vevac[:, rot, s4, :], pb[bank][:, 0:128], AF.Copy, scale=rcol[:, rot, s4:s4 + 1]),
                             reads=[f"pb{bank}", ("rcol", rot)], writes=[("vevac", rot, s4)])
                    P.dma(f"vst{rot}", vd[4 * t:4 * t + 4, :, :].rearrange("j p t -> p j t"), vevac[:, rot, :, :],
                          reads=[("vevac", rot, s4) for s4 in range(4)], writes=[("vdt", t)])
                    if t < NT_O:
                        for dc in range(8):
                            P.op("tensor", lambda te, dc=dc: te.matmul(pb[7][:, :], l0w[:, dc, 0:128], xb[:, r3, dc, :],
                                                                       start=(dc == 0), stop=False),
                                 reads=[("xb", r3, dc), ("l0w", dc)], accum=("pb7", dc == 0, False), tag="qmm")
                        P.op("tensor", lambda te: te.matmul(pb[7][:, :], w1r[0:1, 0:128], rowb[0:1, 2 * rot, :],
                                                            start=False, stop=True),
                             reads=["w1r", ("rowb", rot, 0)], accum=("pb7", False, True), tag="qout")
                        P.op("scalar", lambda s: s.activation(q_sb[:, t, :], pb[7][:, :], AF.Copy),
                             reads=["pb7"], writes=[("q", t)])
                        P.op("vector", lambda v: v.tensor_tensor(q_sb[:, t, :], q_sb[:, t, :], rbA[:, rot, :], ALU.mult),
                             reads=[("q", t), ("rbA", rot)], writes=[("q", t)])
                return proj
            prevproj[0] = mk_proj()
            if t + 2 < NT_G:
                ldx(t + 2)

    outs = []
    if dbg_stage == "A":
        q_out = dout("q_out", [128, TOWN], BF16)
        mr_out = dout("mr_out", [2, NT_O, TT])
        k_out = dout("k_out", [64, 128, 128], BF16)
        v_out = dout("v_out", [64, 128, 128], BF16)
        for t in range(NT_O):
            P.dma("dbg", q_out[:, t * TT:(t + 1) * TT], q_sb[:, t, :],
                  reads=[("q", t)], writes=[("out", "q", t)])
        P.dma("dbg", mr_out[:, :, :], mr_d[:, :, :],
              reads=[("mrd", i, t) for i in range(2) for t in range(NT_O)], writes=[("out", "mr")])
        P.dma("dbg", k_out[:, :, :], kd[:, :, :], reads=[("kdt", t) for t in range(NT_G)], writes=[("out", "k")])
        P.dma("dbg", v_out[:, :, :], vd[:, :, :], reads=[("vdt", t) for t in range(NT_G)], writes=[("out", "v")])
        outs = ["q_out", "mr_out", "k_out", "v_out"]

    if st >= 1:
        P.fence("A->B")
        wof = l0wf
        P.dma("winit", wof[:, 0:3, :].rearrange("p c t -> p (c t)")[:, 0:1024], l0wo_d[:, :],
              writes=[("wof",)])
        P.op("vector", lambda v: v.tensor_copy(wo_sb[:, :, :].rearrange("p c t -> p (c t)"),
                                               wof[:, 0:3, :].rearrange("p c t -> p (c t)")[:, 0:1024]),
             reads=[("wof",)], writes=["wo"])

        def ld16(dstslice, key, src_ap):
            P.dma("abld", wstg[0:16, 0, :], src_ap, writes=[("wstg", 0)])
            P.op("vector", lambda v: v.tensor_copy(dstslice, wstg[0:16, 0, :]),
                 reads=[("wstg", 0)], writes=[key])

        wq = []

        def wq_pass1(fam, src_d, nch, ncc, wslice):
            for ch in range(nch):
                for cc in range(ncc):
                    def th(ch=ch, cc=cc, fam=fam, src_d=src_d, wslice=wslice):
                        r = (ch + cc) % 2
                        P.dma(f"w{fam}{r}", wstg[:, r, :], src_d[ch * 128:(ch + 1) * 128, cc * D:(cc + 1) * D],
                              writes=[("wstg", r)], tag="wld")
                        P.op("vector", lambda v: v.tensor_reduce(mxc[:, r:r + 1], wstg[:, r, :], AX, ALU.max,
                                                                 apply_absolute_value=True),
                             reads=[("wstg", r)], writes=[("mxcol", fam, r)], tag="wmx")
                        P.op("vector", lambda v: v.tensor_tensor(sclb[:, 3:4], sclb[:, 3:4], mxc[:, r:r + 1], ALU.max),
                             reads=[("mxcol", fam, r), ("mxacc", fam)], writes=[("mxacc", fam)], tag="wmxa")
                        dst, dkey = wslice(ch, cc)
                        P.op("vector", lambda v: v.tensor_copy(dst, wstg[:, r, :]),
                             reads=[("wstg", r)], writes=[dkey], tag="wcast")
                    wq.append(th)

        def wq_scale(fam):
            def th(fam=fam):
                mslot = 0 if fam == "wfc" else 1
                P.dma("mxstfc" if fam == "wfc" else "mxstpj",
                      mx_d[mslot, :], sclb[:, 3:4], reads=[("mxacc", fam)], writes=[("mxd", fam)])
                P.dma("mxldfc" if fam == "wfc" else "mxldpj",
                      rowt[0:1, 0, 0:128], mx_d[mslot, :], reads=[("mxd", fam)], writes=[("rowt", 0)])
                P.op("vector", lambda v: v.tensor_reduce(rowt[0:1, 1, 0:1], rowt[0:1, 0, 0:128], AX, ALU.max,
                                                         apply_absolute_value=True),
                     reads=[("rowt", 0)], writes=[("rowt", 1)])
                P.op("vector", lambda v: v.tensor_scalar(rowt[0:1, 1, 1:2], rowt[0:1, 1, 0:1], 1.0 / 127.0, None, ALU.mult),
                     reads=[("rowt", 1)], writes=[("rowt", 1)])
                P.op("scalar", lambda sx: sx.activation(rowt[0:1, 1, 2:3], rowt[0:1, 1, 1:2], AF.Ln),
                     reads=[("rowt", 1)], writes=[("rowt", 1)])
                P.op("scalar", lambda sx: sx.activation(rowt[0:1, 1, 2:3], rowt[0:1, 1, 2:3], AF.Exp, scale=-1.0),
                     reads=[("rowt", 1)], writes=[("rowt", 1)])
                P.op("tensor", lambda te: te.matmul(pb[3][:, 0:2], onesf_sb[0:1, :], rowt[0:1, 1, 1:3],
                                                    start=True, stop=True),
                     reads=[("rowt", 1), "onesf"], writes=["pb3"])
                P.op("scalar", lambda s: s.activation(sclb[:, 0:2], pb[3][:, 0:2], AF.Copy),
                     reads=["pb3"], writes=[("sclb", fam)])
                if fam == "wfc":
                    P.op("gpsimd", lambda g: g.memset(sclb[:, 3:4], 0.0),
                         writes=[("mxacc", "wfc"), ("mxacc", "wpj")])
            wq.append(th)

        def wq_pass2(fam, nch, ncc512, wslice512, a_lhs, b_rhs):
            for ch in range(nch):
                for fc in range(ncc512):
                    def th(ch=ch, fc=fc, fam=fam, wslice512=wslice512, a_lhs=a_lhs, b_rhs=b_rhs):
                        r = (ch * ncc512 + fc) % 2
                        wsl, wkey = wslice512(ch, fc)
                        P.op("tensor", lambda te: te.matmul(pb[2][:, :], a_lhs(ch), b_rhs(fc), start=True, stop=True),
                             reads=[("abm",)], writes=["pb2"], tag="lmm")
                        P.op("vector", lambda v: v.tensor_scalar(lt_z[:, r, :], wsl, sclb[:, 1:2], None, ALU.mult),
                             reads=[wkey, ("sclb", fam)], writes=[("ltz", r)], tag="lz")
                        P.op("scalar", lambda s: s.activation(wsl, pb[2][:, :], AF.Copy, scale=2.0),
                             reads=["pb2"], writes=[wkey], tag="lme")
                        P.op("vector", lambda v: v.tensor_scalar(lt_f[:, r, :], lt_z[:, r, :], sclb[:, 0:1], None, ALU.mult),
                             reads=[("ltz", r), ("sclb", fam)], writes=[("ltf", r)], tag="lzs")
                        P.op("gpsimd", lambda g: g.tensor_tensor(wsl, lt_f[:, r, :], wsl, ALU.add),
                             reads=[("ltf", r), wkey], writes=[wkey], tag="lw")
                    wq.append(th)

        def fc_slice1024(ch, cc):
            return wtfc[:, ch, cc * D:(cc + 1) * D], ("wtfc", ch, cc)

        def fc_slice512(ch, fc):
            return wtfc[:, ch, fc * TT:(fc + 1) * TT], ("wtfc", ch, fc // 2)

        def pj_slice1024(ch, cc):
            return wtpj[:, ch, :], ("wtpj", ch)

        def pj_slice512(ch, fc):
            return wtpj[:, ch, fc * TT:(fc + 1) * TT], ("wtpj", ch)

        def afc_lhs(ch):
            return abm2[:, ch * 128:(ch + 1) * 128]

        def bfc_rhs(fc):
            return abm[:, fc * TT:(fc + 1) * TT]

        def apj_lhs(ch):
            return abm[:, ch * 128:(ch + 1) * 128]

        def bpj_rhs(fc):
            return abm2[:, fc * TT:(fc + 1) * TT]

        def fc_mats():
            P.op("gpsimd", lambda g: g.memset(sclb[:, 3:4], 0.0), writes=[("mxacc", "wfc")])
            ld16(abm2[:, :], ("abm",), afc_d[:, :])
            for q in range(4):
                ld16(abm[:, q * D:(q + 1) * D], ("abm",), bfcT_d[:, q * D:(q + 1) * D])

        def pj_mats():
            for q in range(4):
                ld16(abm[:, q * D:(q + 1) * D], ("abm",), apj_d[:, q * D:(q + 1) * D])
            ld16(abm2[:, :], ("abm",), bpjT_d[:, :])

        wq.append(fc_mats)
        wq_pass1("wfc", wfcT_d, 8, 4, fc_slice1024)
        wq_scale("wfc")
        wq_pass2("wfc", 8, 8, fc_slice512, afc_lhs, bfc_rhs)
        wq.append(pj_mats)
        wq_pass1("wpj", wpjT_d, 32, 1, pj_slice1024)
        wq_scale("wpj")
        wq_pass2("wpj", 32, 2, pj_slice512, apj_lhs, bpj_rhs)

        SBANKS = [0, 1, 4, 5]
        total_steps = sum(8 * i + 8 for i in range(NT_O))
        thunk_idx = 0
        step_no = 0
        pending_chain = [None]

        for i in range(NT_O):
            nk = 8 * i + 8
            ntile = 2 * i + 2

            def tile_load(ti, i=i):
                s = ti % 4
                tid = ti if ti <= i else 8 + (ti - i - 1)
                kj0 = 4 * tid
                P.dma(f"kld{s}", ksb[:, s, :, :], kd[kj0:kj0 + 4, :, :].rearrange("j p t -> p j t"),
                      reads=[("kdt", tid)], writes=[("ksb", s)], tag="kld")
                P.dma(f"vld{s}", vsb[:, s, :, :], vd[kj0:kj0 + 4, :, :].rearrange("j p t -> p j t"),
                      reads=[("vdt", tid)], writes=[("vsb", s)], tag="vld")

            def emit_ov(j, i=i, nk=nk, ntile=ntile):
                ti = j // 4
                s = ti % 4
                q = j % 4
                P.op("tensor", lambda te, s=s, q=q, j=j, nk=nk: te.matmul(
                        pb[6][:, :], vsb[:, s, q, :], pt[:, j % 4, :], start=(j == 0), stop=(j == nk - 1)),
                     reads=[("vsb", s), ("pt", j % 4)], accum=("pb6", j == 0, j == nk - 1), tag="omm")
                P.op("tensor", lambda te, j=j, nk=nk: te.matmul(pb[7][0:1, :], ONE_COL, pt[:, j % 4, :],
                                                                start=(j == 0), stop=(j == nk - 1)),
                     reads=[("pt", j % 4), "ones"], accum=("pb7", j == 0, j == nk - 1), tag="dmm")
                if q == 3 and ti + 4 < ntile:
                    tile_load(ti + 4)

            for ti in range(min(4, ntile)):
                tile_load(ti)
            for j in range(nk):
                ti = j // 4
                s = ti % 4
                q = j % 4
                sbank = SBANKS[j % 4]
                P.op("tensor", lambda te, s=s, q=q, sbank=sbank, i=i: te.matmul(
                        pb[sbank][:, :], ksb[:, s, q, :], q_sb[:, i, :], start=True, stop=True),
                     reads=[("ksb", s), ("q", i)], writes=[f"pb{sbank}"], tag="smm")
                P.op("scalar", lambda sx, j=j, sbank=sbank: sx.activation(
                        pt[:, j % 4, :], pb[sbank][:, :], AF.Exp, bias=CC("neg16"), scale=SM_SCALE),
                     reads=[f"pb{sbank}", "consts"], writes=[("pt", j % 4)], tag="exp")
                half = 0 if ti <= i else 1
                tau = ti if half == 0 else ti - i - 1
                if tau == i:
                    mi = q if half == 0 else 4 + q
                    P.op("vector", lambda v, j=j, mi=mi: v.tensor_tensor(
                            pt[:, j % 4, :], pt[:, j % 4, :], masks[:, mi, :], ALU.mult),
                         reads=[("pt", j % 4), "masks"], writes=[("pt", j % 4)], tag="mask")
                if j >= 3:
                    emit_ov(j - 3)
                step_no += 1
                if pending_chain[0] is not None and j == 1:
                    pending_chain[0]()
                    pending_chain[0] = None
                while thunk_idx < len(wq) and thunk_idx * total_steps <= step_no * len(wq):
                    wq[thunk_idx]()
                    thunk_idx += 1
            emit_ov(nk - 3)
            emit_ov(nk - 2)
            emit_ov(nk - 1)

            def mk_chain(i=i):
                def chain():
                    P.op("scalar", lambda sx: sx.activation(rowt[0:1, 0, :], pb[7][0:1, :], AF.Ln),
                         reads=["pb7"], writes=[("rowt", 0)])
                    P.op("scalar", lambda sx: sx.activation(rowt[0:1, 0, :], rowt[0:1, 0, :], AF.Exp, scale=-1.0),
                         reads=[("rowt", 0)], writes=[("rowt", 0)])
                    P.op("vector", lambda v: v.tensor_copy(rowb[0:1, 2, :], rowt[0:1, 0, :]),
                         reads=[("rowt", 0)], writes=[("rowb", 2)])
                    P.op("tensor", lambda te: te.matmul(pb[3][:, :], ONE_ROW, rowb[0:1, 2, :], start=True, stop=True),
                         reads=[("rowb", 2), "ones"], writes=["pb3"])
                    P.op("scalar", lambda s: s.activation(rbb[:, :], pb[3][:, :], AF.Copy),
                         reads=["pb3"], writes=["rbb"])
                    P.op("vector", lambda v: v.tensor_tensor(onorm[:, i, :], pb[6][:, :], rbb[:, :], ALU.mult),
                         reads=["pb6", "rbb"], writes=[("onorm", i)], tag="onorm")
                return chain
            pending_chain[0] = mk_chain()
        pending_chain[0]()
        pending_chain[0] = None
        while thunk_idx < len(wq):
            wq[thunk_idx]()
            thunk_idx += 1

        if dbg_stage == "B":
            on_out = dout("on_out", [128, TOWN], BF16)
            wt_out = dout("wt_out", [128, 8, 4096], FP16)
            wt2_out = dout("wt2_out", [128, 32, 1024], FP16)
            for i in range(NT_O):
                P.dma("dbg", on_out[:, i * TT:(i + 1) * TT], onorm[:, i, :],
                      reads=[("onorm", i)], writes=[("out", "on", i)])
            P.dma("dbg", wt_out[:, :, :], wtfc[:, :, :],
                  reads=[("wtfc", ch, cc) for ch in range(8) for cc in range(4)], writes=[("out", "wt")])
            P.dma("dbg", wt2_out[:, :, :], wtpj[:, :, :],
                  reads=[("wtpj", ch) for ch in range(32)], writes=[("out", "wt2")])
            outs = ["on_out", "wt_out", "wt2_out"]

    if st >= 2:
        P.fence("B->C")
        P.dma("cinit", bstg[:, 0:32], bfc_t_d, writes=[("bstg", 0)])
        P.op("vector", lambda v: v.tensor_copy(bfc_t[:, :], bstg[:, 0:32]), reads=[("bstg", 0)], writes=["bfct"])
        P.dma("cinit", bstg[:, 32:40], bpj_t_d, writes=[("bstg", 1)])
        P.op("vector", lambda v: v.tensor_copy(bpj_t[:, :], bstg[:, 32:40]), reads=[("bstg", 1)], writes=["bpjt"])
        src3 = xown.rearrange("(c p) t -> p c t", p=128)

        def c_prep(tc):
            rot = tc % 2
            t512 = tc // 2
            cofs = (tc % 2) * TTC
            P.dma(f"mrldm{rot}", rowt[0:1, 2 * rot, 0:TTC], mr_d[0, t512, cofs:cofs + TTC],
                  reads=[("mrd", 0, t512)], writes=[("rowt", 2 * rot)], tag="mrld")
            P.dma(f"mrldr{rot}", rowt[0:1, 2 * rot + 1, 0:TTC], mr_d[1, t512, cofs:cofs + TTC],
                  reads=[("mrd", 1, t512)], writes=[("rowt", 2 * rot + 1)], tag="mrld")
            P.op("vector", lambda v, rot=rot: v.tensor_copy(rowb[0:1, 2 * rot, 0:TTC], rowt[0:1, 2 * rot, 0:TTC]),
                 reads=[("rowt", 2 * rot)], writes=[("rowb", 2 * rot)])
            P.op("vector", lambda v, rot=rot: v.tensor_copy(rowb[0:1, 2 * rot + 1, 0:TTC], rowt[0:1, 2 * rot + 1, 0:TTC]),
                 reads=[("rowt", 2 * rot + 1)], writes=[("rowb", 2 * rot + 1)])
            P.op("tensor", lambda te, rot=rot: te.matmul(pb[2][:, 0:TTC], ONE_ROW, rowb[0:1, 2 * rot, 0:TTC],
                                                         start=True, stop=True),
                 reads=[("rowb", 2 * rot), "ones"], writes=["pb2"])
            P.op("scalar", lambda s, rot=rot: s.activation(mbc[:, rot, :], pb[2][:, 0:TTC], AF.Copy),
                 reads=["pb2"], writes=[("mbc", rot)])
            P.op("tensor", lambda te, rot=rot: te.matmul(pb[3][:, 0:TTC], ONE_ROW, rowb[0:1, 2 * rot + 1, 0:TTC],
                                                         start=True, stop=True),
                 reads=[("rowb", 2 * rot + 1), "ones"], writes=["pb3"])
            P.op("scalar", lambda s, rot=rot: s.activation(rbc[:, rot, :], pb[3][:, 0:TTC], AF.Copy),
                 reads=["pb3"], writes=[("rbc", rot)])
            P.dma(f"xca{rot}", xbt[:, rot, :, :], src3[:, :, tc * TTC:(tc + 1) * TTC],
                  queue="gpsimd", writes=[("xbt", rot, dc) for dc in range(8)], tag="ldxc")
            for dc in range(8):
                eng = "vector" if dc % 2 == 0 else "gpsimd"
                P.op(eng, lambda e, dc=dc, rot=rot: e.tensor_tensor(
                        xbt[:, rot, dc, :], xbt[:, rot, dc, :], mbc[:, rot, :], ALU.subtract),
                     reads=[("xbt", rot, dc), ("mbc", rot)], writes=[("xbt", rot, dc)])
                P.op(eng, lambda e, dc=dc, rot=rot: e.tensor_tensor(
                        xbt[:, rot, dc, :], xbt[:, rot, dc, :], rbc[:, rot, :], ALU.mult),
                     reads=[("xbt", rot, dc), ("rbc", rot)], writes=[("xbt", rot, dc)])
            P.dma(f"xcb{rot}", xrb[rot][:, :, :], src3[:, :, tc * TTC:(tc + 1) * TTC],
                  writes=[("xrb", rot)], tag="ldxr")

        c_prep(0)
        for tc in range(NT_C):
            rot = tc % 2
            t512 = tc // 2
            cofs = (tc % 2) * TTC
            for f in range(32):
                bank = f % 8
                for dc in range(8):
                    P.op("tensor", lambda te, f=f, dc=dc, bank=bank, rot=rot: te.matmul(
                            pb[bank][:, 0:TTC], wtfc[:, dc, f * 128:(f + 1) * 128], xbt[:, rot, dc, :],
                            start=(dc == 0), stop=(dc == 7)),
                         reads=[("xbt", rot, dc), ("wtfc", dc, f // 8)], accum=(f"pb{bank}", dc == 0, dc == 7),
                         tag="fcmm")
                P.op("scalar", lambda s, f=f, bank=bank: s.activation(
                        hsb[:, f, :], pb[bank][:, 0:TTC], AF.Gelu, bias=bfc_t[:, f:f + 1], scale=1.0),
                     reads=[f"pb{bank}", "bfct"], writes=[("h", f)], tag="gelu")
            if tc + 1 < NT_C:
                c_prep(tc + 1)
            for do in range(8):
                bank = do % 8
                for fc in range(32):
                    P.op("tensor", lambda te, do=do, fc=fc, bank=bank: te.matmul(
                            pb[bank][:, 0:TTC], wtpj[:, fc, do * 128:(do + 1) * 128], hsb[:, fc, :],
                            start=(fc == 0), stop=False),
                         reads=[("h", fc), ("wtpj", fc)], accum=(f"pb{bank}", fc == 0, False), tag="pjmm")
                P.op("tensor", lambda te, do=do, bank=bank, t512=t512, cofs=cofs: te.matmul(
                        pb[bank][:, 0:TTC], wo_sb[:, do, :], onorm[:, t512, cofs:cofs + TTC],
                        start=False, stop=True),
                     reads=["wo", ("onorm", t512)], accum=(f"pb{bank}", False, True), tag="womm")
                P.op("scalar", lambda s, do=do, bank=bank, rot=rot: s.activation(
                        x1t[:, rot, do, :], pb[bank][:, 0:TTC], AF.Identity, bias=bpj_t[:, do:do + 1], scale=1.0),
                     reads=[f"pb{bank}", "bpjt"], writes=[("x1t", rot, do)], tag="pjev")
                eng = "vector" if do % 2 == 0 else "gpsimd"
                P.op(eng, lambda e, do=do, rot=rot: e.tensor_tensor(
                        x1t[:, rot, do, :], x1t[:, rot, do, :], xrb[rot][:, do, :], ALU.add),
                     reads=[("x1t", rot, do), ("xrb", rot)], writes=[("x1t", rot, do)], tag="resadd")
            P.dma(f"x1st{rot}", x1_d[:, :, tc * TTC:(tc + 1) * TTC].rearrange("c p t -> p c t"), x1t[:, rot, :, :],
                  reads=[("x1t", rot, do) for do in range(8)], writes=[("x1d", tc)], tag="x1st")

        if dbg_stage == "C":
            x1_out = dout("x1_out", [8, 128, TOWN], BF16)
            P.dma("dbg", x1_out[:, :, :], x1_d[:, :, :],
                  reads=[("x1d", tc) for tc in range(NT_C)], writes=[("out", "x1")])
            outs = ["x1_out"]

    if st >= 3:
        P.fence("C->D")
        for wname, wdst, wsrc in (("wq", wqT, wqT_d), ("wk", wkT, wkT_d), ("wv", wvT, wvT_d)):
            P.dma("dinitg", wdst[:, :, :], wsrc.rearrange("(c p) t -> p c t", p=128),
                  queue="gpsimd", writes=[("w1", wname)])
        for dc in range(8):
            r = dc % 2
            P.dma(f"wl8s{r}", wstgD[:, r, :], wl1T_d[dc * 128:(dc + 1) * 128, :],
                  writes=[("wstgD", r)], tag="wl8ld")
            P.op("vector", lambda v, dc=dc, r=r: v.tensor_copy(wl8[:, dc, :], wstgD[:, r, :]),
                 reads=[("wstgD", r)], writes=[("wl8", dc)], tag="wl8c")
        P.dma("dinitg", bkvb[0:1, :, :], bkv_d.rearrange("a b -> (a b)").rearrange("(a b) -> a b", a=1).rearrange("a (c d) -> a c d", c=2), queue="gpsimd", writes=["bkvb"])
        P.dma("dinit", bq_t[:, :], bq_t_d, writes=["bqt"])
        P.dma("dinit", bl1_t[:, :], bl1_t_d, writes=["bl1t"])
        P.op("gpsimd", lambda g: g.memset(vpb[:, :, :, 64:65], 1.0), writes=["vpones"])
        P.op("vector", lambda v: v.memset(kvsb[:, :, :], 0.0), writes=["kvsb"])

        def phi_evac(psum_ap, out_ap, out_key, bias_col, pbkey, pr, bkey="bqt"):
            if bias_col is None:
                P.op("scalar", lambda sx: sx.activation(ph1[:, pr, :], psum_ap, AF.Relu),
                     reads=[pbkey], writes=[("ph1", pr)], tag="prelu")
                P.op("scalar", lambda sx: sx.activation(ph2[:, pr, :], psum_ap, AF.Exp),
                     reads=[pbkey], writes=[("ph2", pr)], tag="pexp")
            else:
                P.op("scalar", lambda sx: sx.activation(ph1[:, pr, :], psum_ap, AF.Relu,
                                                        bias=bias_col, scale=1.0),
                     reads=[pbkey, bkey], writes=[("ph1", pr)], tag="prelu")
                P.op("scalar", lambda sx: sx.activation(ph2[:, pr, :], psum_ap, AF.Exp,
                                                        bias=bias_col, scale=1.0),
                     reads=[pbkey, bkey], writes=[("ph2", pr)], tag="pexp")
            P.op("vector", lambda v: v.scalar_tensor_tensor(out_ap, ph2[:, pr, :], 1.0, ph1[:, pr, :],
                                                            ALU.min, ALU.add),
                 reads=[("ph1", pr), ("ph2", pr)], writes=[out_key], tag="phi")

        def x1ld(it):
            rot = it % 2
            c0 = it * TT
            P.dma(f"x1r{rot}", x1rd[:, rot, :, :], x1_d[:, :, c0:c0 + TT].rearrange("c p t -> p c t"),
                  reads=[("x1d", 2 * it), ("x1d", 2 * it + 1)],
                  writes=[("x1rd", rot)], tag="x1ld")

        x1ld(0)
        x1ld(1)
        for it in range(NT_O + 1):
            rot = it % 2
            rotp = (it - 1) % 2
            c0 = it * TT
            if it < NT_O:
                for dc in range(8):
                    P.op("tensor", lambda te, dc=dc, rot=rot: te.matmul(pb[0][0:1, :], ONE_COL, x1rd[:, rot, dc, :],
                                                                        start=(dc == 0), stop=(dc == 7)),
                         reads=[("x1rd", rot), "ones"], accum=("pb0", dc == 0, dc == 7), tag="dsum")
                for dc in range(8):
                    pr = dc % 2
                    eng = "vector" if dc % 2 == 0 else "gpsimd"
                    P.op(eng, lambda g, dc=dc, pr=pr, rot=rot: g.tensor_tensor(
                            ph3[:, pr, :], x1rd[:, rot, dc, :], x1rd[:, rot, dc, :], ALU.mult),
                         reads=[("x1rd", rot)], writes=[("ph3", pr)], tag="dsq")
                    P.op("tensor", lambda te, dc=dc, pr=pr: te.matmul(pb[1][0:1, :], ONE_COL, ph3[:, pr, :],
                                                                      start=(dc == 0), stop=(dc == 7)),
                         reads=[("ph3", pr), "ones"], accum=("pb1", dc == 0, dc == 7), tag="dsumsq")
            def f1_block(it=it, rotp=rotp):
                for oc in range(8):
                    bank = 2 + oc % 2
                    for dc in range(8):
                        P.op("tensor", lambda te, oc=oc, dc=dc, bank=bank, rotp=rotp: te.matmul(
                                pb[bank][:, :], wl8[:, dc, oc * 128:(oc + 1) * 128], xq8[:, rotp, dc, :],
                                start=(dc == 0), stop=(dc == 7)),
                             reads=[("xq8", rotp), ("wl8", dc)], accum=(f"pb{bank}", dc == 0, dc == 7), tag="f1mm")
                    P.op("scalar", lambda sx, oc=oc, bank=bank: sx.activation(
                            ffnt[:, oc % 2, :], pb[bank][:, :], AF.Identity, bias=bl1_t[:, oc:oc + 1], scale=1.0),
                         reads=[f"pb{bank}", "bl1t"], writes=[("ffnt", oc % 2)], tag="f1ev")
                    eng = "vector" if oc % 2 == 0 else "gpsimd"
                    P.op(eng, lambda e, oc=oc, rotp=rotp: e.tensor_tensor(
                            x1rd[:, rotp, oc, :], x1rd[:, rotp, oc, :], ffnt[:, oc % 2, :], ALU.add),
                         reads=[("x1rd", rotp), ("ffnt", oc % 2)], writes=[("x1rd", rotp)], tag="f1add")
                P.dma(f"x1ps{rotp}", x1p_d[:, :, (it - 1) * TT:it * TT].rearrange("c p t -> p c t"),
                      x1rd[:, rotp, :, :],
                      reads=[("x1rd", rotp)], writes=[("x1pd", it - 1)], tag="x1pst")
                if it + 1 < NT_O:
                    x1ld(it + 1)
            if it >= 1 and it >= NT_O:
                f1_block()
            if it < NT_O:
                m_ap, var_ap, s_ap, r_ap = (rowt[0:1, i, :] for i in range(4))
                m_bf, r_bf = rowb[0:1, 0, :], rowb[0:1, 1, :]
                P.op("vector", lambda v: v.tensor_scalar(m_ap, pb[0][0:1, :], 1.0 / D, None, ALU.mult),
                     reads=["pb0"], writes=[("rowt", 0)])
                P.op("vector", lambda v: v.tensor_tensor(var_ap, m_ap, m_ap, ALU.mult),
                     reads=[("rowt", 0)], writes=[("rowt", 1)])
                P.op("vector", lambda v: v.scalar_tensor_tensor(var_ap, pb[1][0:1, :], 1.0 / D, var_ap, ALU.mult, ALU.subtract),
                     reads=["pb1", ("rowt", 1)], writes=[("rowt", 1)])
                P.op("scalar", lambda sx: sx.activation(s_ap, var_ap, AF.Ln, bias=CCr("eps_ln"), scale=1.0),
                     reads=[("rowt", 1), "consts"], writes=[("rowt", 2)])
                P.op("scalar", lambda sx: sx.activation(r_ap, s_ap, AF.Exp, scale=-0.5),
                     reads=[("rowt", 2)], writes=[("rowt", 3)])
                P.op("vector", lambda v: v.tensor_copy(m_bf, m_ap), reads=[("rowt", 0)], writes=[("rowb", 0)])
                P.op("vector", lambda v: v.tensor_copy(r_bf, r_ap), reads=[("rowt", 3)], writes=[("rowb", 1)])
                P.op("tensor", lambda te: te.matmul(pb[0][:, :], ONE_ROW, m_bf, start=True, stop=True),
                     reads=[("rowb", 0), "ones"], writes=["pb0"])
                P.op("tensor", lambda te: te.matmul(pb[1][:, :], ONE_ROW, r_bf, start=True, stop=True),
                     reads=[("rowb", 1), "ones"], writes=["pb1"])
                P.op("scalar", lambda sx: sx.activation(mbd[:, :], pb[0][:, :], AF.Copy),
                     reads=["pb0"], writes=["mbd"])
                P.op("scalar", lambda sx: sx.activation(rbd[:, :], pb[1][:, :], AF.Copy),
                     reads=["pb1"], writes=["rbd"])
                for dc in range(8):
                    eng = "vector" if dc % 2 == 0 else "gpsimd"
                    P.op(eng, lambda e, dc=dc, rot=rot: e.tensor_tensor(xn1[:, dc, :], x1rd[:, rot, dc, :], mbd[:, :], ALU.subtract),
                         reads=[("x1rd", rot), "mbd"], writes=[("xn1", dc)])
                    P.op(eng, lambda e, dc=dc: e.tensor_tensor(xn1[:, dc, :], xn1[:, dc, :], rbd[:, :], ALU.mult),
                         reads=[("xn1", dc), "rbd"], writes=[("xn1", dc)])
                for dc in range(8):
                    eng = "vector" if dc % 2 == 0 else "gpsimd"
                    P.op(eng, lambda e, dc=dc, rot=rot: e.tensor_copy(xq8[:, rot, dc, :], xn1[:, dc, :]),
                         reads=[("xn1", dc)], writes=[("xq8", rot)], tag="xq8c")
                if it >= 1:
                    f1_block()
                for oc in range(8):
                    bank = oc % 2
                    for dc in range(8):
                        P.op("tensor", lambda te, oc=oc, dc=dc, bank=bank: te.matmul(
                                pb[bank][:, :], wqT[:, dc, oc * 128:(oc + 1) * 128], xn1[:, dc, :],
                                start=(dc == 0), stop=(dc == 7)),
                             reads=[("xn1", dc), ("w1", "wq")], accum=(f"pb{bank}", dc == 0, dc == 7), tag="qpmm")
                    phi_evac(pb[bank][:, :], qp[:, oc, c0:c0 + TT], ("qp", oc, it), bq_t[:, oc:oc + 1],
                             f"pb{bank}", oc % 2)
                for s4 in range(4):
                    sr = s4 % 2
                    for oh in range(2):
                        bank = 2 + oh
                        for dc in range(8):
                            P.op("tensor", lambda te, s4=s4, oh=oh, dc=dc, bank=bank: te.matmul(
                                    pb[bank][:, :], xn1[:, dc, s4 * 128:(s4 + 1) * 128], wkT[:, dc, oh * 512:(oh + 1) * 512],
                                    start=(dc == 0), stop=False),
                                 reads=[("xn1", dc), ("w1", "wk")], accum=(f"pb{bank}", dc == 0, False), tag="kpmm")
                        P.op("tensor", lambda te, oh=oh, bank=bank: te.matmul(
                                pb[bank][:, :], ONE_ROW, bkvb[0:1, 0, oh * 512:(oh + 1) * 512],
                                start=False, stop=True),
                             reads=["bkvb", "ones"], accum=(f"pb{bank}", False, True), tag="kbias")
                        phi_evac(pb[bank][:, :], kpb[:, sr, oh * 512:(oh + 1) * 512], ("kpb", sr),
                                 None, f"pb{bank}", oh)
                    for oh in range(2):
                        bank2 = 2 + oh
                        for dc in range(8):
                            P.op("tensor", lambda te, s4=s4, oh=oh, dc=dc, bank2=bank2: te.matmul(
                                    pb[bank2][:, :], xn1[:, dc, s4 * 128:(s4 + 1) * 128], wvT[:, dc, oh * 512:(oh + 1) * 512],
                                    start=(dc == 0), stop=False),
                                 reads=[("xn1", dc), ("w1", "wv")], accum=(f"pb{bank2}", dc == 0, False), tag="vpmm")
                        P.op("tensor", lambda te, oh=oh, bank2=bank2: te.matmul(
                                pb[bank2][:, :], ONE_ROW, bkvb[0:1, 1, oh * 512:(oh + 1) * 512],
                                start=False, stop=True),
                             reads=["bkvb", "ones"], accum=(f"pb{bank2}", False, True), tag="vbias")
                        P.op("scalar", lambda sx, sr=sr, oh=oh, bank2=bank2: sx.activation(
                                vpb[:, sr, oh * 8:(oh + 1) * 8, 0:64], pb[bank2][:, :], AF.Copy),
                             reads=[f"pb{bank2}"], writes=[("vpb", sr, oh)], tag="vev")
                    for h in range(16):
                        bank = 4 + (h % 4)
                        colof = (h // 4) * 128
                        P.op("tensor", lambda te, h=h, sr=sr, bank=bank, colof=colof, s4=s4: te.matmul(
                                pb[bank][0:64, colof:colof + 65], kpb[:, sr, h * 64:(h + 1) * 64], vpb[:, sr, h, :],
                                start=(s4 == 0 and h // 4 == 0), stop=(s4 == 3 and h // 4 == 3),
                                skip_group_check=True),
                             reads=[("kpb", sr), ("vpb", sr, h // 8), "vpones"],
                             accum=(f"pb{bank}", s4 == 0 and h // 4 == 0, s4 == 3 and h // 4 == 3), tag="kvmm")
                for h in range(16):
                    bank = 4 + (h % 4)
                    colof = (h // 4) * 128
                    P.op("vector", lambda v, h=h, bank=bank, colof=colof: v.scalar_tensor_tensor(
                            kvsb[:, h, :], pb[bank][0:64, colof:colof + 65], 0.0, kvsb[:, h, :], ALU.max, ALU.add),
                         reads=[f"pb{bank}", "kvsb"], writes=["kvsb"], tag="kvacc")
        P.dma("kvcin", kvin_d[:, :], kvsb[:, :, :].rearrange("p h e -> p (h e)"),
              reads=["kvsb"], writes=["kvin"], tag="kvin")
        P.op("gpsimd", lambda g: g.collective_compute(
                "AllReduce", ALU.add, replica_groups=[[0, 1], [2, 3], [4, 5], [6, 7]],
                ins=[dram["kvin"]], outs=[dram["kvout"]]),
             reads=["kvin"], writes=["kvout"], axis="coll", tag="allreduce")

    if st >= 4:
        P.fence("D->E")
        kvo3 = dram["kvout"].rearrange("p (h e) -> p h e", e=65)
        P.op("vector", lambda v: v.memset(kvbd[:, :, :], 0.0),
             writes=[("kvbd", h) for h in range(16)], tag="kvbd0")
        P.op("vector", lambda v: v.memset(kvzd[:, :, :], 0.0),
             writes=[("kvzd", h) for h in range(16)], tag="kvzd0")
        for h in range(16):
            P.dma("kvcout", kvbd[(h % 2) * 64:(h % 2) * 64 + 64, h // 2, :][:, 0:64],
                  kvo3[:, h, 0:64], queue="gpsimd", reads=["kvout"], writes=[("kvbd", h)], tag="kvbd")
            P.dma("kvcout", kvzd[(h % 2) * 64:(h % 2) * 64 + 64, h // 2, (h % 2):(h % 2) + 1],
                  kvo3[:, h, 64:65], queue="gpsimd", reads=["kvout"], writes=[("kvzd", h)],
                  tag="kvzd", slow_ok=True)
        P.dma("einit", bo_tE[:, :], bo_t_d, writes=["boE"])
        P.dma("einit", blk2[:, :], blk2_d, writes=["blk2"])
        P.dma("einitg", woT[:, :, :], woT_d.rearrange("(c p) t -> p c t", p=128),
              queue="gpsimd", writes=[("w1", "wo")])

        def x1pld(it):
            rot = it % 2
            c0 = it * TT
            P.dma(f"x1p{rot}", x1prd[:, rot, :, :], x1p_d[:, :, c0:c0 + TT].rearrange("c p t -> p c t"),
                  reads=[("x1pd", it)], writes=[("x1prd", rot)], tag="x1pld")

        x1pld(0)
        x1pld(1)

        def e_pre(it):
            rot = it % 2
            c0 = it * TT
            for m in range(8):
                bank = m % 2
                mr = m % 2
                P.op("tensor", lambda te, m=m, bank=bank, c0=c0, it=it: te.matmul(
                        pb[bank][:, :], kvbd[:, m, :], qp[:, m, c0:c0 + TT], start=True, stop=True),
                     reads=[("kvbd", 2 * m), ("kvbd", 2 * m + 1), ("qp", m, it)], writes=[f"pb{bank}"], tag="apre")
                P.op("scalar", lambda sx, m=m, bank=bank: sx.activation(presb[:, m, :], pb[bank][:, :], AF.Copy),
                     reads=[f"pb{bank}"], writes=[("presb", m)], tag="apev")
                zbank = 2 + m % 2
                P.op("tensor", lambda te, m=m, c0=c0, it=it, zbank=zbank: te.matmul(
                        pb[zbank][0:2, :], kvzd[:, m, :], qp[:, m, c0:c0 + TT], start=True, stop=True),
                     reads=[("kvzd", 2 * m), ("kvzd", 2 * m + 1), ("qp", m, it)], writes=[f"pb{zbank}"], tag="anrm")
                P.op("vector", lambda v, mr=mr, zbank=zbank: v.tensor_scalar(
                        rbe2[0:2, mr, :], pb[zbank][0:2, :], 1e-6, None, ALU.add),
                     reads=[f"pb{zbank}"], writes=[("rbe2", mr)], tag="zeps")
                P.op("scalar", lambda sx, mr=mr: sx.activation(rbf2[0:2, mr, :], rbe2[0:2, mr, :], AF.Ln),
                     reads=[("rbe2", mr)], writes=[("rbf2", mr)], tag="zln")
                P.op("scalar", lambda sx, mr=mr: sx.activation(rbf2[0:2, mr, :], rbf2[0:2, mr, :], AF.Exp, scale=-1.0),
                     reads=[("rbf2", mr)], writes=[("rbf2", mr)], tag="zexp")
                P.op("vector", lambda v, mr=mr: v.tensor_copy(nrb2[:, mr, :], rbf2[0:2, mr, :]),
                     reads=[("rbf2", mr)], writes=[("nrb2", mr)], tag="zbf")
                if m >= 1:
                    emit_ebc(m - 1, rot)
            emit_ebc(7, rot)

        def emit_ebc(m, rot):
            zbank = 6 + m % 2
            mr = m % 2
            P.op("tensor", lambda te, mr=mr, zbank=zbank: te.matmul(
                    pb[zbank][:, :], blk2[:, :], nrb2[:, mr, :], start=True, stop=True),
                 reads=[("nrb2", mr), "blk2"], writes=[f"pb{zbank}"], tag="ebc")
            P.op("vector", lambda v, mr=mr, zbank=zbank: v.tensor_copy(rbefm[:, mr, :], pb[zbank][:, :]),
                 reads=[f"pb{zbank}"], writes=[("rbefm", mr)], tag="ebce")
            eng = "vector" if m % 2 == 0 else "gpsimd"
            P.op(eng, lambda e, m=m, mr=mr, rot=rot: e.tensor_tensor(
                    nrm[:, rot, m, :], presb[:, m, :], rbefm[:, mr, :], ALU.mult),
                 reads=[("presb", m), ("rbefm", mr)], writes=[("nrm", rot, m)], tag="enorm")

        e_pre(0)
        for it in range(1, NT_O + 1):
            rotp = (it - 1) % 2
            c0p = (it - 1) * TT
            if it < NT_O:
                e_pre(it)
            for do in range(8):
                bank = 4 + do % 2
                for oc in range(8):
                    P.op("tensor", lambda te, do=do, oc=oc, bank=bank, rotp=rotp: te.matmul(
                            pb[bank][:, :], woT[:, oc, do * 128:(do + 1) * 128], nrm[:, rotp, oc, :],
                            start=(oc == 0), stop=(oc == 7)),
                         reads=[("nrm", rotp, oc), ("w1", "wo")], accum=(f"pb{bank}", oc == 0, oc == 7), tag="womm2")
                P.op("scalar", lambda sx, do=do, bank=bank: sx.activation(
                        aev[:, do % 2, :], pb[bank][:, :], AF.Identity, bias=bo_tE[:, do:do + 1], scale=1.0),
                     reads=[f"pb{bank}", "boE"], writes=[("aev", do % 2)], tag="woev")
                P.op("vector", lambda v, do=do, rotp=rotp: v.tensor_tensor(
                        outb[:, do % 2, :], x1prd[:, rotp, do, :], aev[:, do % 2, :], ALU.add),
                     reads=[("x1prd", rotp), ("aev", do % 2)], writes=[("outbf", do % 2)], tag="finadd")
                P.dma(f"yst{do % 2}", y_d[do, :, c0p:c0p + TT], outb[:, do % 2, :],
                      reads=[("outbf", do % 2)], writes=[("out", "y", it - 1, do)], tag="yst")
            if it + 1 < NT_O:
                x1pld(it + 1)

    P.barrier("sync", [k for k in P.last_writer if isinstance(k, tuple) and k[0] == "out"])
    em = P.finalize_and_emit(nc)
    em.close()
    stack.close()
    return nc, P, outs


def own_token_idx(p):
    return np.concatenate([np.arange((2 * i + p) * TT, (2 * i + p + 1) * TT) for i in range(8)])


def host_prep_core(I, core):
    b, p = core // 2, core % 2
    x = I["L0_attn_x"]
    perm = np.concatenate([own_token_idx(p), own_token_idx(1 - p)])
    xT = np.ascontiguousarray(x[b].T)
    xfull = np.ascontiguousarray(xT[:, perm])
    xown = np.ascontiguousarray(xT[:, own_token_idx(p)])
    l0qkv = np.ascontiguousarray(np.concatenate(
        [I["L0_attn_Wq"], I["L0_attn_Wk"], I["L0_attn_Wv"]], axis=1))
    consts = np.zeros((128, NCONST), dtype=np.float32)
    for name, (col, val) in CONST_COLS.items():
        consts[:, col] = val
    onesbf = np.ones((128, 576), dtype=BF)
    onesf = np.ones((1, 128), dtype=np.float32)
    kk = np.arange(128)[:, None]
    tq = np.arange(TT)[None, :]
    diag = [(128 * m + kk <= tq).astype(np.float32) for m in range(4)]
    if p == 0:
        mlist = diag + [np.zeros((128, TT), np.float32)] * 4
    else:
        mlist = diag + [np.ones((128, TT), np.float32)] * 4
    masks = np.stack(mlist, axis=1).astype(BF)
    d = dict(xfull=xfull, xown=xown, l0qkv=l0qkv, consts=consts,
             onesbf=onesbf, onesf=onesf, masks=masks,
             l0wo=np.ascontiguousarray(I["L0_attn_Wo"]),
             wfcT=np.ascontiguousarray(I["L0_ffn_W_fc"].T),
             wpjT=np.ascontiguousarray(I["L0_ffn_W_proj"].T),
             afc=np.ascontiguousarray(I["L0_ffn_A_fc"]),
             bfcT=np.ascontiguousarray(I["L0_ffn_B_fc"].T),
             apj=np.ascontiguousarray(I["L0_ffn_A_proj"]),
             bpjT=np.ascontiguousarray(I["L0_ffn_B_proj"].T),
             bfc_t=np.ascontiguousarray(I["L0_ffn_b_fc"].reshape(32, 128).T),
             bpj_t=np.ascontiguousarray(I["L0_ffn_b_proj"].reshape(8, 128).T),
             wqT=np.ascontiguousarray(I["L1_attn_Wq"].T),
             wkT=np.ascontiguousarray(I["L1_attn_Wk"].T),
             wvT=np.ascontiguousarray(I["L1_attn_Wv"].T),
             woT=np.ascontiguousarray(I["L1_attn_Wo"].T),
             bq_t=np.ascontiguousarray(I["L1_attn_bq"].reshape(8, 128).T),
             bkv=np.ascontiguousarray(np.stack([I["L1_attn_bk"], I["L1_attn_bv"]])),
             bo_t=np.ascontiguousarray(I["L1_attn_bo"].reshape(8, 128).T),
             wl1T=np.ascontiguousarray(I["L1_ffn_weight"].T),
             bl1_t=np.ascontiguousarray(I["L1_ffn_bias"].reshape(8, 128).T),
             blk2=np.concatenate([np.repeat([[1, 0]], 64, 1).reshape(1, 128),
                                  np.repeat([[0, 1]], 64, 1).reshape(1, 128)]).astype(BF))
    return d


_BUILD_CACHE = {}


def kernel(**inputs):
    from concourse.bass_utils import run_bass_kernel_spmd
    if "nc" not in _BUILD_CACHE:
        nc, P, _ = build(stage="E")
        _BUILD_CACHE["nc"] = nc
    nc = _BUILD_CACHE["nc"]
    in_maps = [host_prep_core(inputs, c) for c in range(8)]
    res = run_bass_kernel_spmd(nc, in_maps, core_ids=list(range(8)))
    out = np.zeros((4, 8192, 1024), dtype=np.float32)
    for c in range(8):
        b, p = c // 2, c % 2
        yc = np.asarray(res.results[c]["y"]).reshape(1024, 4096)
        out[b, own_token_idx(p), :] = yc.T
    return out
